# version 22 (final)
# baseline (speedup 1.0000x reference)
import os
import numpy as np
import ml_dtypes
import concourse.bass as bass
import concourse.mybir as mybir
from concourse.bass_utils import run_bass_kernel_spmd

F32 = mybir.dt.float32
BF16 = mybir.dt.bfloat16
AF = mybir.ActivationFunctionType
ALU = mybir.AluOpType
AXX = mybir.AxisListType.X

D = 1024
T = 2048
CH = 8
NT = 16
NQ = 512
TQ = 4

DEBUG_TAPS = False


class Ctr:
    __slots__ = ("h", "n")

    def __init__(self, h):
        self.h = h
        self.n = 0


class KB:
    ENGS = ("tensor", "vector", "scalar", "gpsimd", "sync")

    def __init__(self, simsync=False):
        self.prog = {e: [] for e in self.ENGS}
        self.simsync = simsync

    def emit(self, eng, fn):
        self.prog[eng].append(fn)

    def wait(self, eng, ctr, val):
        assert 0 <= val <= ctr.n, (eng, val, ctr.n)
        if val > 0:
            self.emit(eng, lambda e, h=ctr.h, v=val: e.wait_ge(h, v))

    def step(self, eng, fn, inc=None, k=1, drain=False):
        if inc is None:
            self.emit(eng, fn)
            return None
        inc.n += k
        if drain or (self.simsync and eng in ("tensor", "vector", "scalar")):
            def run(e, f=fn, h=inc.h, kk=k):
                f(e)
                e.drain().then_inc(h, kk)
            self.emit(eng, run)
        else:
            self.emit(eng, lambda e, f=fn, h=inc.h, kk=k: f(e).then_inc(h, kk))
        return inc.n


def build():
    nc = bass.Bass("TRN2", target_bir_lowering=False, dynamic_dma_scratch_size=12288)

    dr = {}
    def din(name, shape):
        dr[name] = nc.dram_tensor(name, list(shape), F32, kind="ExternalInput").ap()
    din("x", (D, T))
    for w in ("wq", "wk", "wv", "wg", "wo", "cw1"):
        din(w, (D, D))
    din("cw2", (D, 512)); din("cw3", (512, 256)); din("cw4", (256, 1))
    din("gw1", (D, 512)); din("gw2", (512, 256))
    din("gwpack", (128, 8))
    din("biaspack", (128, 64))
    din("rowpack", (64, D))
    dr["cpack"] = nc.dram_tensor("cpack", [128, 1824], BF16, kind="ExternalInput").ap()
    y = nc.dram_tensor("y", [CH, 128, T], F32, kind="ExternalOutput").ap()
    x1d = nc.dram_tensor("x1d", [128, CH * T], F32, kind="Internal").ap()
    cc_in = nc.dram_tensor("cc_in", [128, 1032], F32, kind="Internal").ap()
    cc_out = nc.dram_tensor("cc_out", [128, 1032], F32, kind="Internal").ap()
    gc_in = nc.dram_tensor("gc_in", [128, 8], F32, kind="Internal").ap()
    gc_out = nc.dram_tensor("gc_out", [128, 8], F32, kind="Internal").ap()
    taps = {}
    if DEBUG_TAPS:
        for nm, shape, dt_ in (("t_kvacc", [128, 1032], F32), ("t_phiq", [128, CH * T], BF16),
                               ("t_r", [128, CH * T], F32), ("t_x1", [128, CH * T], F32),
                               ("t_z", [16, T], F32), ("t_gs", [128, CH * T], BF16),
                               ("t_c1", [128, CH * T], F32), ("t_cp", [4, 512], F32),
                               ("t_th", [4, 512], F32), ("t_tiny", [4, 8], F32),
                               ("t_ksf", [128, 8], F32), ("t_kvbd", [128, 1024], BF16),
                               ("t_cb", [128, 2048], F32),
                               ("t_h1", [128, CH * T], BF16)):
            taps[nm] = nc.dram_tensor(nm, shape, dt_, kind="ExternalOutput").ap()

    from contextlib import ExitStack
    es = ExitStack()
    sb = lambda name, shape, dt_: es.enter_context(nc.sbuf_tensor(name, list(shape), dt_))

    stream = sb("stream", (128, CH * T), F32)
    R1 = sb("R1", (128, CH * T), BF16)
    R2 = sb("R2", (128, CH * T), BF16)
    R3 = sb("R3", (128, CH * 1024), BF16)
    R4 = sb("R4", (128, CH * 1024), BF16)
    CW2 = sb("CW2", (128, 4096), BF16)
    CW3 = sb("CW3", (128, 1024), BF16)
    SCR = sb("SCR", (128, 3072), F32)
    ZHI = sb("ZHI", (16, 2048), BF16)
    ZLO = sb("ZLO", (16, 2048), BF16)
    S4 = sb("S4", (4, 512), F32)
    Q4 = sb("Q4", (4, 512), F32)
    TH4 = sb("TH4", (4, 512), F32)
    CPHI = sb("CPHI", (4, 512), BF16)
    CPLO = sb("CPLO", (4, 512), BF16)
    HI4 = sb("HI4", (64, D), BF16)
    LO4 = sb("LO4", (64, D), BF16)
    BP = sb("BP", (128, 64), F32)
    CPK = sb("CPK", (128, 1824), BF16)
    KVBD = sb("KVBD", (128, 1024), BF16)
    TINY = sb("TINY", (128, 48), F32)
    TINYB = sb("TINYB", (128, 64), BF16)
    PS = es.enter_context(nc.psum_tensor("PS", [128, 4096], F32))

    xbf = R1
    phiq = R2
    kvacc = R2[:, 0:2064].bitcast(F32)
    ktile = [SCR[:, 0:512].bitcast(BF16), SCR[:, 512:1024].bitcast(BF16)]
    vtile = [SCR[:, 1024:1536].bitcast(BF16), SCR[:, 1536:2048].bitcast(BF16)]
    etile = [SCR[:, 2048:2560], SCR[:, 2560:3072]]
    etq = [SCR[:, 0:512], SCR[:, 512:1024]]
    tgt = [SCR[:, 1024:1280].bitcast(BF16), SCR[:, 1280:1536].bitcast(BF16)]
    gw1sb = SCR[:, 0:2048].bitcast(BF16)
    gw2sb = SCR[:, 2048:2560].bitcast(BF16)
    x1t = [SCR[:, 0:1024], SCR[:, 1024:2048]]
    twslot = [SCR[:, i * 256:(i + 1) * 256].bitcast(BF16) for i in range(8)]
    rowstg = SCR[0:64, 0:1024]
    outbf = R1
    h3 = R1
    twA = R1[:, 0:4096].bitcast(F32)
    twB = R1[:, 4096:8192].bitcast(F32)
    r2t = [R1[:, 8192 + i * 2048: 8192 + (i + 1) * 2048] for i in range(4)]
    a4hi = CPHI[0:4, :]
    a4lo = CPLO[0:4, :]
    b4hi = TH4[0:4, 0:256].bitcast(BF16)
    b4lo = TH4[0:4, 256:512].bitcast(BF16)
    ntmp = SCR[:, 2560:3072]
    tqmask = CPK[:, 0:16]
    selz = CPK[0:16, 16:1040]
    jsel = CPK[0:4, 1040:1552]
    ones1 = CPK[0:1, 1552:1680]
    ones1b = CPK[32:33, 1552:1680]
    onescol = CPK[:, 1680:1681]
    zmask = CPK[:, 1696:1824]
    bq = BP[:, 0:8]; bg = BP[:, 8:16]; bo = BP[:, 16:24]
    cb1 = BP[:, 24:32]; cb2 = BP[:, 32:36]; cb3 = BP[:, 36:38]
    gb1 = BP[:, 38:42]; gb2 = BP[:, 42:44]
    cb4c = BP[0:4, 44:45]; gb3c = BP[0:4, 45:46]; arc = BP[0:4, 46:47]
    epsc = BP[:, 47:48]
    nbq = TINY[:, 0:8]; bq1 = TINY[:, 8:16]; bgh = TINY[:, 16:24]
    epz = TINY[:, 44:45]
    ksf = TINY[:, 24:32]; gcp = TINY[:, 32:40]; cw4f = TINY[:, 40:42]
    b4h4 = TINY[0:4, 42:43]; gb3h4 = TINY[0:4, 43:44]
    g4sb = TINY[0:4, 46:47]; g7col = TINY[0:4, 47:48]
    ZLF = SCR[:, 1024:1152]
    ZLF2 = SCR[:, 1152:1280]
    zlhi = SCR[:, 1280:1344].bitcast(BF16)
    zllo = SCR[:, 1344:1408].bitcast(BF16)
    ZBr = SCR[0:16, 2560:3072]
    gcbf = TINYB[:, 0:8]; gh1 = TINYB[:, 8:12]; gh2 = TINYB[:, 12:14]
    cw4m0 = TINYB[:, 16:32]; cw4m1 = TINYB[:, 32:48]
    GW3 = TINYB[:, 48:56]

    SIMSYNC = os.environ.get("BASS_SIM_FP64") == "1"
    sems = {}
    for s in ("dT", "Ts", "Vs", "Ss", "Gs", "CC1", "CC2"):
        sems[s] = Ctr(es.enter_context(nc.semaphore(s)))
    dTap, CC1, CC2, Tc, Vc, Sc, Gc = (
        sems[s] for s in ("dT", "CC1", "CC2", "Ts", "Vs", "Ss", "Gs"))
    _semn = [0]
    def mkdma():
        _semn[0] += 1
        return Ctr(es.enter_context(nc.semaphore(f"dma{_semn[0]}")))

    kb = KB(simsync=SIMSYNC)
    W = kb.wait
    St = kb.step

    s_bp = mkdma(); s_rp = mkdma(); s_cpk = mkdma(); s_x = mkdma(); s_cw4f = mkdma()
    St("sync", lambda e: e.dma_start(out=BP[:, :], in_=dr["biaspack"]), inc=s_bp, k=16)
    St("sync", lambda e: e.dma_start(out=rowstg, in_=dr["rowpack"]), inc=s_rp, k=16)
    St("sync", lambda e: e.dma_start(out=CPK[:, :], in_=dr["cpack"]), inc=s_cpk, k=16)
    St("sync", lambda e: e.dma_start(
        out=stream[:, :].rearrange("p (c t) -> p c t", c=CH),
        in_=dr["x"].rearrange("(c p) t -> p c t", p=128)), inc=s_x, k=16)
    St("sync", lambda e: e.dma_start(
        out=cw4f, in_=dr["cw4"].rearrange("(kc p) o -> p (kc o)", p=128),
        allow_slow_non_contiguous=True), inc=s_cw4f, k=16)

    def wload(dst, name, kc):
        return lambda e: e.dma_start(
            out=dst.rearrange("p (kc m) -> p kc m", kc=kc),
            in_=dr[name].rearrange("(kc p) m -> p kc m", p=128))

    s_xbf = mkdma(); s_wk = mkdma(); s_wv = mkdma()
    s_cw2 = mkdma(); s_cw3 = mkdma(); s_gw3 = mkdma()
    St("gpsimd", lambda e: e.dma_start(
        out=xbf[:, :].rearrange("p (c t) -> p c t", c=CH),
        in_=dr["x"].rearrange("(c p) t -> p c t", p=128)), inc=s_xbf, k=16)
    St("gpsimd", wload(R3, "wk", 8), inc=s_wk, k=16)
    St("gpsimd", wload(R4, "wv", 8), inc=s_wv, k=16)
    St("gpsimd", wload(CW2, "cw2", 8), inc=s_cw2, k=16)
    St("gpsimd", wload(CW3, "cw3", 4), inc=s_cw3, k=16)
    St("gpsimd", lambda e: e.dma_start(out=GW3, in_=dr["gwpack"]), inc=s_gw3, k=16)

    for s_ in (s_bp, s_rp, s_cpk, s_cw4f):
        W("vector", s_, 16)
    St("vector", lambda e: e.tensor_scalar(out=nbq, in0=bq, scalar1=-1.0, scalar2=0.0, op0=ALU.mult, op1=ALU.add))
    St("vector", lambda e: e.tensor_scalar(out=bq1, in0=bq, scalar1=1.0, scalar2=0.0, op0=ALU.add, op1=ALU.add))
    St("vector", lambda e: e.tensor_scalar(out=bgh, in0=bg, scalar1=0.5, scalar2=0.0, op0=ALU.mult, op1=ALU.add))
    W("scalar", s_bp, 16)
    St("scalar", lambda e: e.mul(out=b4h4, in_=cb4c, mul=0.5))
    St("scalar", lambda e: e.mul(out=gb3h4, in_=gb3c, mul=0.5), inc=Sc)
    St("vector", lambda e: e.memset(epz, 1e-6))
    St("vector", lambda e: e.tensor_copy(HI4[:, :], rowstg))
    St("vector", lambda e: e.tensor_tensor(out=LO4[:, :], in0=rowstg, in1=HI4[:, :], op=ALU.subtract))
    St("vector", lambda e: e.tensor_scalar(out=cw4m0, in0=tqmask, scalar1=cw4f[:, 0:1], scalar2=0.0, op0=ALU.mult, op1=ALU.add))
    v_consts = St("vector", lambda e: e.tensor_scalar(out=cw4m1, in0=tqmask, scalar1=cw4f[:, 1:2],
                                                      scalar2=0.0, op0=ALU.mult, op1=ALU.add), inc=Vc)

    kdone = [0] * NT; vdone = [0] * NT; vevac = [0] * NT
    kstt = [0] * NT; kvdone = [0] * NT; kvadd = [0] * NT

    def proj_tm(wsb, base, hrow, t):
        onesr = ones1 if hrow == 0 else ones1b
        def f(e):
            for kc in range(CH):
                lhs = xbf[:, kc * T + t * 128: kc * T + (t + 1) * 128]
                for h in range(2):
                    e.matmul(PS[:, base + h * NQ: base + (h + 1) * NQ], lhs,
                             wsb[:, kc * 1024 + h * NQ: kc * 1024 + (h + 1) * NQ],
                             start=(kc == 0), stop=False, skip_group_check=True)
            for h in range(2):
                dst = PS[:, base + h * NQ: base + (h + 1) * NQ]
                e.matmul(dst, onesr, HI4[hrow:hrow + 1, h * NQ:(h + 1) * NQ],
                         start=False, stop=False, skip_group_check=True)
                mm = e.matmul(dst, onesr, LO4[hrow:hrow + 1, h * NQ:(h + 1) * NQ],
                              start=False, stop=True, skip_group_check=True)
            return mm
        return f

    def kv_mm(j):
        sj = j % 2
        kbj = sj * 2048; vbj = kbj + 1024
        def f(e):
            for c in range(CH):
                off = kbj + (c // 4) * NQ + (c % 4) * 128
                e.matmul(PS[:, off:off + 128], ktile[sj][:, c * 128:(c + 1) * 128],
                         vtile[sj][:, c * 128:(c + 1) * 128],
                         start=(c % 4 == 0), stop=(c % 4 == 3), skip_group_check=True)
                mm = e.matmul(PS[:, vbj + c: vbj + c + 1],
                              ktile[sj][:, c * 128:(c + 1) * 128], onescol,
                              start=(c == 0), stop=(c == CH - 1), skip_group_check=True)
            return mm
        return f

    def emit_kvadd(j):
        sj = j % 2; kbj = sj * 2048; vbj = kbj + 1024
        W("vector", Tc, kvdone[j])
        if j == 0:
            St("vector", lambda e, kbj=kbj: e.tensor_copy(kvacc[:, 0:512], PS[:, kbj:kbj + 512]))
            St("vector", lambda e, kbj=kbj: e.tensor_copy(kvacc[:, 512:1024], PS[:, kbj + 512:kbj + 1024]))
            kvadd[j] = St("vector", lambda e, vbj=vbj: e.tensor_copy(
                kvacc[:, 1024:1032], PS[:, vbj:vbj + 8]), inc=Vc)
        else:
            St("vector", lambda e, kbj=kbj: e.tensor_tensor(
                out=kvacc[:, 0:512], in0=kvacc[:, 0:512], in1=PS[:, kbj:kbj + 512], op=ALU.add))
            St("vector", lambda e, kbj=kbj: e.tensor_tensor(
                out=kvacc[:, 512:1024], in0=kvacc[:, 512:1024], in1=PS[:, kbj + 512:kbj + 1024], op=ALU.add))
            kvadd[j] = St("vector", lambda e, vbj=vbj: e.tensor_tensor(
                out=kvacc[:, 1024:1032], in0=kvacc[:, 1024:1032], in1=PS[:, vbj:vbj + 8],
                op=ALU.add), inc=Vc)

    for t in range(NT):
        s = t % 2
        kbase = s * 2048
        vbase = kbase + 1024
        if t >= 2:
            W("tensor", Vc, kvadd[t - 2])
        if t == 0:
            W("tensor", s_xbf, 16)
            W("tensor", s_wk, 16)
            W("tensor", s_cpk, 16)
            W("tensor", Vc, v_consts)
        kdone[t] = St("tensor", proj_tm(R3, kbase, 0, t), inc=Tc)
        if t == 0:
            W("tensor", s_wv, 16)
        vdone[t] = St("tensor", proj_tm(R4, vbase, 32, t), inc=Tc)
        if t >= 1:
            jj = t - 1
            W("tensor", Vc, kstt[jj])
            W("tensor", Sc, vevac[jj])
            kvdone[jj] = St("tensor", kv_mm(jj), inc=Tc, drain=True)
        W("scalar", Tc, kdone[t])
        if t >= 2:
            W("scalar", Vc, kstt[t - 2])
        khalf = [0, 0]
        for h in range(2):
            if h == 1:
                W("scalar", Vc, khalf[0])
            St("scalar", lambda e, s=s, kbase=kbase, h=h: e.activation(
                out=etile[s], in_=PS[:, kbase + h * NQ: kbase + (h + 1) * NQ],
                func=AF.Relu, scale=-1.0))
            kexp_h = St("scalar", lambda e, s=s: e.activation(
                out=etile[s], in_=etile[s], func=AF.Exp, scale=-1.0), inc=Sc)
            W("vector", Sc, kexp_h)
            W("vector", Tc, kdone[t])
            if t >= 2:
                W("vector", Tc, kvdone[t - 2])
            khalf[h] = St("vector", lambda e, s=s, kbase=kbase, h=h: e.scalar_tensor_tensor(
                out=ktile[s][:, h * NQ:(h + 1) * NQ],
                in0=PS[:, kbase + h * NQ: kbase + (h + 1) * NQ],
                scalar=1.0, in1=etile[s], op0=ALU.add, op1=ALU.max), inc=Vc)
        kstt[t] = khalf[1]
        W("scalar", Tc, vdone[t])
        if t >= 2:
            W("scalar", Tc, kvdone[t - 2])
        vevac[t] = St("scalar", lambda e, s=s, vbase=vbase: e.activation(
            out=vtile[s][:, :], in_=PS[:, vbase:vbase + 1024], func=AF.Copy), inc=Sc)
        if t >= 1:
            emit_kvadd(t - 1)
    jj = NT - 1
    W("tensor", Vc, kstt[jj])
    W("tensor", Sc, vevac[jj])
    kvdone[jj] = St("tensor", kv_mm(jj), inc=Tc, drain=True)
    emit_kvadd(jj)

    s_wq = mkdma(); s_wg = mkdma(); s_ccin = mkdma(); s_kvbd = mkdma(); s_ksf = mkdma()
    W("gpsimd", Tc, kdone[NT - 1])
    St("gpsimd", wload(R3, "wq", 8), inc=s_wq, k=16)
    W("gpsimd", Tc, vdone[NT - 1])
    St("gpsimd", wload(R4, "wg", 8), inc=s_wg, k=16)
    W("gpsimd", Vc, kvadd[NT - 1])
    if DEBUG_TAPS:
        St("gpsimd", lambda e: e.dma_start(out=taps["t_kvacc"], in_=kvacc[:, :]), inc=dTap, k=16)
    St("gpsimd", lambda e: e.dma_start(out=cc_in, in_=kvacc[:, :]), inc=s_ccin, k=16)
    W("gpsimd", s_ccin, 16)
    St("gpsimd", lambda e: e.collective_compute(
        "AllReduce", ALU.add, replica_groups=[[0, 1, 2, 3], [4, 5, 6, 7]],
        ins=[cc_in.opt()], outs=[cc_out.opt()]), inc=CC1)
    W("gpsimd", CC1, 1)
    St("gpsimd", lambda e: e.dma_start(out=KVBD[:, :], in_=cc_out[:, 0:1024]), inc=s_kvbd, k=16)
    St("gpsimd", lambda e: e.dma_start(out=ksf, in_=cc_out[:, 1024:1032]), inc=s_ksf, k=16)

    phiq_stt = [[0] * TQ for _ in range(CH)]
    phiq_mm = [0] * CH
    etq_guard = [0, 0]
    for c in range(CH):
        if c == 0:
            W("tensor", s_wq, 16)
            W("tensor", Vc, kvadd[NT - 1])
            W("tensor", Vc, kvadd[NT - 2])
        if c >= 2:
            W("tensor", Vc, phiq_stt[c - 2][TQ - 1])
        qb = (c % 2) * 4
        def qmm(e, c=c, qb=qb):
            for kc in range(CH):
                lhs = R3[:, kc * 1024 + c * 128: kc * 1024 + (c + 1) * 128]
                for tq in range(TQ):
                    mm = e.matmul(PS[:, (qb + tq) * NQ:(qb + tq + 1) * NQ], lhs,
                                  xbf[:, kc * T + tq * NQ: kc * T + (tq + 1) * NQ],
                                  start=(kc == 0), stop=(kc == CH - 1), skip_group_check=True)
            return mm
        phiq_mm[c] = St("tensor", qmm, inc=Tc)
        for tq in range(TQ):
            sl = tq % 2
            W("scalar", Tc, phiq_mm[c])
            if etq_guard[sl]:
                W("scalar", Vc, etq_guard[sl])
            St("scalar", lambda e, c=c, qb=qb, tq=tq, sl=sl: e.activation(
                out=etq[sl], in_=PS[:, (qb + tq) * NQ:(qb + tq + 1) * NQ],
                func=AF.Relu, bias=nbq[:, c:c + 1], scale=-1.0))
            sx = St("scalar", lambda e, sl=sl: e.activation(
                out=etq[sl], in_=etq[sl], func=AF.Exp, scale=-1.0), inc=Sc)
            W("vector", Sc, sx)
            if c == 0 and tq == 0:
                W("vector", s_ccin, 16)
            phiq_stt[c][tq] = St("vector", lambda e, c=c, qb=qb, tq=tq, sl=sl: e.scalar_tensor_tensor(
                out=phiq[:, c * T + tq * NQ: c * T + (tq + 1) * NQ],
                in0=PS[:, (qb + tq) * NQ:(qb + tq + 1) * NQ], scalar=bq1[:, c:c + 1],
                in1=etq[sl], op0=ALU.add, op1=ALU.max), inc=Vc)
            etq_guard[sl] = phiq_stt[c][tq]
    if DEBUG_TAPS:
        W("gpsimd", Vc, phiq_stt[CH - 1][TQ - 1])
        phiq_tap = St("gpsimd", lambda e: e.dma_start(out=taps["t_phiq"], in_=phiq[:, :]),
                      inc=dTap, k=16)
        St("gpsimd", lambda e: e.dma_start(out=taps["t_ksf"], in_=ksf), inc=dTap, k=16)

    W("gpsimd", s_kvbd, 16)
    g_kv = 0
    for c in range(CH):
        St("gpsimd", lambda e, c=c: e.memset(KVBD[0:64, c * 128 + 64:(c + 1) * 128], 0.0))
        g_kv = St("gpsimd", lambda e, c=c: e.memset(KVBD[64:128, c * 128:c * 128 + 64], 0.0),
                  inc=Gc, drain=(c == CH - 1))
    W("vector", s_cpk, 16)
    W("vector", s_ksf, 16)
    St("vector", lambda e: e.tensor_copy(ZLF2[:, :], zmask))
    for c in range(CH):
        St("vector", lambda e, c=c: e.tensor_scalar(
            out=ZLF[:, c * 16:(c + 1) * 16], in0=ZLF2[:, c * 16:(c + 1) * 16],
            scalar1=ksf[:, c:c + 1], scalar2=0.0, op0=ALU.mult, op1=ALU.add))
    St("vector", lambda e: e.tensor_copy(zlhi[:, :], ZLF[:, :]))
    St("vector", lambda e: e.tensor_copy(ZLF2[:, :], zlhi[:, :]))
    St("vector", lambda e: e.tensor_tensor(out=ZLF[:, :], in0=ZLF[:, :], in1=ZLF2[:, :],
                                           op=ALU.subtract))
    v_zl = St("vector", lambda e: e.tensor_copy(zllo[:, :], ZLF[:, :]), inc=Vc)

    if DEBUG_TAPS:
        W("gpsimd", Gc, g_kv)
        W("gpsimd", Vc, v_zl)
        St("gpsimd", lambda e: e.dma_start(out=taps["t_kvbd"], in_=KVBD[:, :]), inc=dTap, k=16)

    W("tensor", Gc, g_kv)
    W("tensor", Vc, v_zl)
    W("tensor", Vc, phiq_stt[CH - 1][TQ - 1])
    W("tensor", Vc, kvadd[NT - 1])
    def zden_mm(e):
        for c in range(CH):
            for tq in range(TQ):
                dst = PS[0:16, 1024 + tq * NQ: 1024 + (tq + 1) * NQ]
                e.matmul(dst, zlhi[:, c * 16:(c + 1) * 16],
                         phiq[:, c * T + tq * NQ: c * T + (tq + 1) * NQ],
                         start=(c == 0), stop=False, skip_group_check=True)
                mm = e.matmul(dst, zllo[:, c * 16:(c + 1) * 16],
                              phiq[:, c * T + tq * NQ: c * T + (tq + 1) * NQ],
                              start=False, stop=(c == CH - 1), skip_group_check=True)
        return mm
    zden_done = St("tensor", zden_mm, inc=Tc, drain=True)

    zrow = [0] * TQ
    for tq in range(TQ):
        W("scalar", Tc, zden_done)
        if tq > 0:
            W("scalar", Vc, zrow[tq - 1])
        St("scalar", lambda e, tq=tq: e.activation(
            out=ZBr, in_=PS[0:16, 1024 + tq * NQ: 1024 + (tq + 1) * NQ],
            func=AF.Ln, bias=epz[0:16, :]))
        zs = St("scalar", lambda e: e.activation(out=ZBr, in_=ZBr, func=AF.Exp,
                                                 scale=-1.0), inc=Sc)
        W("vector", Sc, zs)
        slot = tq
        St("vector", lambda e, slot=slot: e.tensor_scalar(
            out=ZHI[:, slot * NQ:(slot + 1) * NQ], in0=ZBr, scalar1=0.5,
            scalar2=0.0, op0=ALU.mult, op1=ALU.add))
        zrow[tq] = St("vector", lambda e, slot=slot: e.scalar_tensor_tensor(
            out=ZLO[:, slot * NQ:(slot + 1) * NQ], in0=ZBr, scalar=0.5,
            in1=ZHI[:, slot * NQ:(slot + 1) * NQ], op0=ALU.mult, op1=ALU.subtract), inc=Vc)
        if DEBUG_TAPS:
            W("gpsimd", Vc, zrow[tq])
            zt = St("gpsimd", lambda g, tq=tq: g.dma_start(
                out=taps["t_z"][:, tq * NQ:(tq + 1) * NQ], in_=ZBr), inc=dTap, k=16)
            W("vector", dTap, zt)

    s_wo = mkdma()
    W("gpsimd", Tc, phiq_mm[CH - 1])
    St("gpsimd", wload(R3, "wo", 8), inc=s_wo, k=16)

    gs_v = [[0] * CH for _ in range(TQ)]
    oevac = [[0] * TQ for _ in range(CH)]
    gate_last_mm = 0
    op_mm_last = 0
    for tq in range(TQ):
        slot = tq
        for c in range(CH):
            idx = tq * CH + c
            rot = idx % 2
            if idx == 0:
                W("tensor", s_kvbd, 16)
                W("tensor", s_wg, 16)
                W("tensor", Vc, zrow[1])
            if idx >= 2:
                W("tensor", Vc, gs_v[(idx - 2) // CH][(idx - 2) % CH])
            W("tensor", Vc, zrow[tq])
            def c4mm(e, c=c, tq=tq, rot=rot, slot=slot):
                zb_dst = PS[:, 3072 + rot * NQ: 3072 + (rot + 1) * NQ]
                e.matmul(zb_dst, selz[:, c * 128:(c + 1) * 128],
                         ZHI[:, slot * NQ:(slot + 1) * NQ], start=True, stop=False,
                         skip_group_check=True)
                e.matmul(zb_dst, selz[:, c * 128:(c + 1) * 128],
                         ZLO[:, slot * NQ:(slot + 1) * NQ], start=False, stop=True,
                         skip_group_check=True)
                sc_dst = PS[:, rot * NQ:(rot + 1) * NQ]
                e.matmul(sc_dst, KVBD[:, c * 128:(c + 1) * 128],
                         phiq[:, c * T + tq * NQ: c * T + (tq + 1) * NQ],
                         start=True, stop=True, skip_group_check=True)
                g_dst = PS[:, 1024 + rot * NQ: 1024 + (rot + 1) * NQ]
                for kc in range(CH):
                    mm = e.matmul(g_dst, R4[:, kc * 1024 + c * 128: kc * 1024 + (c + 1) * 128],
                                  xbf[:, kc * T + tq * NQ: kc * T + (tq + 1) * NQ],
                                  start=(kc == 0), stop=(kc == CH - 1), skip_group_check=True)
                return mm
            mm_n = St("tensor", c4mm, inc=Tc)
            gate_last_mm = mm_n
            W("scalar", Tc, mm_n)
            if idx >= 2:
                W("scalar", Vc, gs_v[(idx - 2) // CH][(idx - 2) % CH])
            tg_n = St("scalar", lambda e, c=c, rot=rot: e.activation(
                out=tgt[rot], in_=PS[:, 1024 + rot * NQ: 1024 + (rot + 1) * NQ],
                func=AF.Tanh, bias=bgh[:, c:c + 1], scale=0.5), inc=Sc)
            W("vector", Tc, mm_n)
            W("vector", Sc, tg_n)
            if DEBUG_TAPS and idx == 0:
                W("vector", dTap, phiq_tap)
            St("vector", lambda e, rot=rot: e.scalar_tensor_tensor(
                out=etq[rot], in0=tgt[rot], scalar=1.0,
                in1=PS[:, 3072 + rot * NQ: 3072 + (rot + 1) * NQ],
                op0=ALU.add, op1=ALU.mult))
            gs_v[tq][c] = St("vector", lambda e, c=c, tq=tq, rot=rot: e.tensor_tensor(
                out=phiq[:, c * T + tq * NQ: c * T + (tq + 1) * NQ],
                in0=etq[rot], in1=PS[:, rot * NQ:(rot + 1) * NQ], op=ALU.mult), inc=Vc)

    for mc in range(CH):
        if mc == 0:
            W("tensor", Vc, gs_v[TQ - 1][CH - 1])
            W("tensor", s_wo, 16)
        if mc >= 2:
            W("tensor", Vc, oevac[mc - 2][TQ - 1])
        ob_ = (mc % 2) * 4
        def opmm(e, mc=mc, ob_=ob_):
            for kc in range(CH):
                lhs = R3[:, kc * 1024 + mc * 128: kc * 1024 + (mc + 1) * 128]
                for tq in range(TQ):
                    mm = e.matmul(PS[:, (ob_ + tq) * NQ:(ob_ + tq + 1) * NQ], lhs,
                                  phiq[:, kc * T + tq * NQ: kc * T + (tq + 1) * NQ],
                                  start=(kc == 0), stop=(kc == CH - 1), skip_group_check=True)
            return mm
        op_mm_last = St("tensor", opmm, inc=Tc)
        for tq in range(TQ):
            W("vector", Tc, op_mm_last)
            if mc == 0 and tq == 0:
                W("vector", s_x, 16)
            oevac[mc][tq] = St("vector", lambda e, mc=mc, tq=tq, ob_=ob_: e.scalar_tensor_tensor(
                out=stream[:, mc * T + tq * NQ: mc * T + (tq + 1) * NQ],
                in0=PS[:, (ob_ + tq) * NQ:(ob_ + tq + 1) * NQ],
                scalar=bo[:, mc:mc + 1],
                in1=stream[:, mc * T + tq * NQ: mc * T + (tq + 1) * NQ],
                op0=ALU.add, op1=ALU.add), inc=Vc)

    s_cw1 = mkdma(); s_gw1 = mkdma(); s_gw2 = mkdma()
    W("gpsimd", Tc, gate_last_mm)
    St("gpsimd", wload(R4, "cw1", 8), inc=s_cw1, k=16)
    W("gpsimd", Vc, gs_v[TQ - 1][CH - 1])
    St("gpsimd", lambda e: e.dma_start(
        out=gw1sb.rearrange("p (kc m) -> p kc m", kc=8),
        in_=dr["gw1"].rearrange("(kc p) m -> p kc m", p=128)), inc=s_gw1, k=16)
    St("gpsimd", lambda e: e.dma_start(
        out=gw2sb.rearrange("p (kc m) -> p kc m", kc=4),
        in_=dr["gw2"].rearrange("(kc p) m -> p kc m", p=128)), inc=s_gw2, k=16)
    if DEBUG_TAPS:
        W("gpsimd", Vc, oevac[CH - 1][TQ - 1])
        r_tap = St("gpsimd", lambda e: e.dma_start(out=taps["t_r"], in_=stream[:, :]),
                   inc=dTap, k=16)
        gs_tap = St("gpsimd", lambda e: e.dma_start(out=taps["t_gs"], in_=phiq[:, :]),
                    inc=dTap, k=16)

    cb_mm = [0, 0, 0]
    r2add = [0] * CH

    def layer_norm(tag, norm_list):
        ln2 = tag == "ln2"
        ord_ = [0, 4, 1, 5, 2, 6, 3, 7] if ln2 else list(range(CH))
        sq = [0] * CH
        rbfv = [0] * CH
        sums_c = [0] * CH
        for pos, c in enumerate(ord_):
            if not ln2:
                W("scalar", Tc, op_mm_last)
                W("vector", Tc, op_mm_last)
                W("scalar", Vc, oevac[c][TQ - 1])
                if DEBUG_TAPS:
                    W("scalar", dTap, gs_tap)
                    W("vector", dTap, gs_tap)
                rbfv[c] = St("vector", lambda e, c=c: e.tensor_copy(
                    R2[:, c * T:(c + 1) * T], stream[:, c * T:(c + 1) * T]), inc=Vc)
            else:
                W("scalar", Vc, r2add[c])
                W("gpsimd", Vc, r2add[c])
                rbfv[c] = St("gpsimd", lambda e, c=c: e.tensor_copy(
                    R2[:, c * T:(c + 1) * T], stream[:, c * T:(c + 1) * T]), inc=Gc,
                    drain=True)
            slot = pos % 4
            if pos >= 4:
                W("scalar", Tc, sums_c[pos - 4])
            sq[c] = St("scalar", lambda e, c=c, slot=slot: e.activation(
                out=r2t[slot], in_=stream[:, c * T:(c + 1) * T], func=AF.Square), inc=Sc)
            W("tensor", Sc, sq[c])
            W("tensor", Gc if ln2 else Vc, rbfv[c])
            def summ(e, c=c, slot=slot, pos=pos):
                for tq in range(TQ):
                    e.matmul(PS[0:4, 0:NQ], tqmask[:, tq * 4:(tq + 1) * 4],
                             R2[:, c * T + tq * NQ: c * T + (tq + 1) * NQ],
                             start=(pos == 0 and tq == 0), stop=(pos == CH - 1 and tq == TQ - 1),
                             skip_group_check=True)
                    mm = e.matmul(PS[0:4, NQ:2 * NQ], tqmask[:, tq * 4:(tq + 1) * 4],
                                  r2t[slot][:, tq * NQ:(tq + 1) * NQ],
                                  start=(pos == 0 and tq == 0), stop=(pos == CH - 1 and tq == TQ - 1),
                                  skip_group_check=True)
                return mm
            sums_c[pos] = St("tensor", summ, inc=Tc, drain=True)
        W("vector", Tc, sums_c[CH - 1])
        St("vector", lambda e: e.tensor_scalar(out=S4[:, :], in0=PS[0:4, 0:NQ],
                                               scalar1=1.0 / D, scalar2=0.0, op0=ALU.mult, op1=ALU.add))
        St("vector", lambda e: e.tensor_scalar(out=Q4[:, :], in0=PS[0:4, NQ:2 * NQ],
                                               scalar1=1.0 / D, scalar2=0.0, op0=ALU.mult, op1=ALU.add))
        St("vector", lambda e: e.tensor_tensor(out=TH4[:, :], in0=S4[:, :], in1=S4[:, :], op=ALU.mult))
        v_var = St("vector", lambda e: e.tensor_tensor(out=Q4[:, :], in0=Q4[:, :], in1=TH4[:, :],
                                                       op=ALU.subtract), inc=Vc)
        W("scalar", Vc, v_var)
        St("scalar", lambda e: e.activation(out=Q4[:, :], in_=Q4[:, :], func=AF.Ln, bias=epsc[0:4, :]))
        s_a = St("scalar", lambda e: e.activation(out=Q4[:, :], in_=Q4[:, :], func=AF.Exp,
                                                  scale=-0.5), inc=Sc)
        W("vector", Sc, s_a)
        St("vector", lambda e: e.scalar_tensor_tensor(out=S4[:, :], in0=S4[:, :], scalar=-1.0,
                                                      in1=Q4[:, :], op0=ALU.mult, op1=ALU.mult))
        St("vector", lambda e: e.tensor_copy(a4hi, Q4[:, :]))
        St("vector", lambda e: e.tensor_tensor(out=a4lo, in0=Q4[:, :], in1=a4hi, op=ALU.subtract))
        St("vector", lambda e: e.tensor_copy(b4hi, S4[:, :]))
        v_ab = St("vector", lambda e: e.tensor_tensor(out=b4lo, in0=S4[:, :], in1=b4hi,
                                                      op=ALU.subtract), inc=Vc)
        last_norm = 0
        bc_mm = [0] * TQ
        aps2 = [PS[:, tq * NQ:(tq + 1) * NQ] for tq in range(TQ)]
        bps2 = [PS[:, 2048 + tq * NQ: 2048 + (tq + 1) * NQ] for tq in range(TQ)]
        for tq in range(TQ):
            if ln2:
                a_ps = aps2[tq]
                b_ps = bps2[tq]
            else:
                rot = tq % 2
                a_ps = PS[:, 2048 + rot * NQ: 2048 + (rot + 1) * NQ]
                b_ps = PS[:, 3072 + rot * NQ: 3072 + (rot + 1) * NQ]
            W("tensor", Vc, v_ab)
            if not ln2 and tq >= 2:
                W("tensor", Vc, norm_list[(tq - 2) * CH + CH - 1])
            def bcmm(e, a_ps=a_ps, b_ps=b_ps, tq=tq):
                e.matmul(a_ps, jsel[:, tq * 128:(tq + 1) * 128], a4hi, start=True, stop=False,
                         skip_group_check=True)
                e.matmul(a_ps, jsel[:, tq * 128:(tq + 1) * 128], a4lo, start=False, stop=True,
                         skip_group_check=True)
                e.matmul(b_ps, jsel[:, tq * 128:(tq + 1) * 128], b4hi, start=True, stop=False,
                         skip_group_check=True)
                mm = e.matmul(b_ps, jsel[:, tq * 128:(tq + 1) * 128], b4lo, start=False, stop=True,
                              skip_group_check=True)
                return mm
            bc_mm[tq] = St("tensor", bcmm, inc=Tc, drain=True)
            if ln2:
                continue
            for c in range(CH):
                W("vector", Tc, bc_mm[tq])
                if not ln2 and DEBUG_TAPS and tq == 0 and c == 0:
                    W("vector", dTap, r_tap)
                St("vector", lambda e, c=c, tq=tq, a_ps=a_ps: e.tensor_tensor(
                    out=ntmp, in0=stream[:, c * T + tq * NQ: c * T + (tq + 1) * NQ],
                    in1=a_ps, op=ALU.mult))
                last_norm = St("vector", lambda e, c=c, tq=tq, b_ps=b_ps: e.tensor_tensor(
                    out=stream[:, c * T + tq * NQ: c * T + (tq + 1) * NQ],
                    in0=ntmp, in1=b_ps,
                    op=ALU.add), inc=Vc)
                norm_list[tq * CH + c] = last_norm
        if ln2:
            W("vector", Tc, bc_mm[TQ - 1])
            for c in range(CH):
                for tq in range(TQ):
                    St("vector", lambda e, c=c, tq=tq: e.tensor_tensor(
                        out=ntmp, in0=stream[:, c * T + tq * NQ: c * T + (tq + 1) * NQ],
                        in1=aps2[tq], op=ALU.mult))
                    last_norm = St("vector", lambda e, c=c, tq=tq: e.tensor_tensor(
                        out=stream[:, c * T + tq * NQ: c * T + (tq + 1) * NQ],
                        in0=ntmp, in1=bps2[tq],
                        op=ALU.add), inc=Vc)
                    norm_list[tq * CH + c] = last_norm
        return last_norm

    norm1 = [0] * (TQ * CH)
    ln1_last = layer_norm("ln1", norm1)
    if DEBUG_TAPS:
        W("gpsimd", Vc, ln1_last)
        x1_tap = St("gpsimd", lambda e: e.dma_start(out=taps["t_x1"], in_=stream[:, :]),
                    inc=dTap, k=16)

    prev_outbf_tq0 = [[0] * CH for _ in range(TQ)]
    prev_outbf_sem0 = [Sc, Gc, Gc, Gc]
    for tq in range(TQ):
        eng = "scalar" if tq == 0 else "gpsimd"
        W(eng, Vc, norm1[tq * CH + CH - 1])
        for c in range(CH):
            if tq == 0:
                prev_outbf_tq0[tq][c] = St("scalar", lambda e, c=c, tq=tq: e.activation(
                    out=outbf[:, c * T + tq * NQ: c * T + (tq + 1) * NQ],
                    in_=stream[:, c * T + tq * NQ: c * T + (tq + 1) * NQ],
                    func=AF.Copy), inc=Sc)
            else:
                prev_outbf_tq0[tq][c] = St("gpsimd", lambda e, c=c, tq=tq: e.tensor_copy(
                    outbf[:, c * T + tq * NQ: c * T + (tq + 1) * NQ],
                    stream[:, c * T + tq * NQ: c * T + (tq + 1) * NQ]), inc=Gc,
                    drain=(c == CH - 1))
    outbf_c = [prev_outbf_tq0[3][c] for c in range(CH)]

    for c in range(CH):
        W("vector", Vc, norm1[3 * CH + c])
        St("vector", lambda e, c=c: e.reduce_sum(out=gcp[:, c:c + 1],
                                                 in_=stream[:, c * T:(c + 1) * T], axis=AXX))
    v_gcp = St("vector", lambda e: e.engine_nop(), inc=Vc)

    s_park = mkdma()
    W("sync", Vc, ln1_last)
    St("sync", lambda e: e.dma_start(out=x1d, in_=stream[:, :]), inc=s_park, k=16)

    s_gcin = mkdma(); s_gcret = mkdma()
    W("gpsimd", Vc, v_gcp)
    St("gpsimd", lambda e: e.dma_start(out=gc_in, in_=gcp), inc=s_gcin, k=16)
    W("gpsimd", s_gcin, 16)
    St("gpsimd", lambda e: e.collective_compute(
        "AllReduce", ALU.add, replica_groups=[[0, 1, 2, 3], [4, 5, 6, 7]],
        ins=[gc_in.opt()], outs=[gc_out.opt()]), inc=CC2)
    W("sync", CC2, 1)
    St("sync", lambda e: e.dma_start(out=gcp, in_=gc_out), inc=s_gcret, k=16)

    twist_v = [[0] * 4 for _ in range(3)]
    th_done = [0] * 3
    cp_done = [0] * 3
    prev_outbf = outbf_c[:]
    c0_done = 0

    for i in range(3):
        cn1_mm_last = 0
        gelu1w = [[0] * CH for _ in range(TQ)]
        for tq in range(TQ):
            for mc in range(CH):
                idx = tq * CH + mc
                rot4 = idx % 4
                if idx == 0:
                    W("tensor", s_cw1, 16)
                    if i == 0:
                        W("tensor", s_cw2, 16)
                        W("tensor", s_cw3, 16)
                if mc == 0 and tq > 0:
                    if i > 0:
                        W("tensor", prev_outbf_sem[tq], prev_outbf_tq[tq][CH - 1])
                    else:
                        W("tensor", prev_outbf_sem0[tq], prev_outbf_tq0[tq][CH - 1])
                if idx >= 4:
                    W("tensor", Sc, gelu1w[(idx - 4) // CH][(idx - 4) % CH])
                if idx == 0:
                    pv = prev_outbf_tq if i > 0 else prev_outbf_tq0
                    for kc in range(CH):
                        W("tensor", Sc, pv[0][kc])
                        def cn1mm0(e, kc=kc):
                            return e.matmul(PS[:, 0:NQ],
                                            R4[:, kc * 1024: kc * 1024 + 128],
                                            outbf[:, kc * T: kc * T + NQ],
                                            start=(kc == 0), stop=(kc == CH - 1),
                                            skip_group_check=True)
                        cn1_mm_last = St("tensor", cn1mm0,
                                         inc=(Tc if kc == CH - 1 else None))
                    W("scalar", Tc, cn1_mm_last)
                    gelu1w[tq][mc] = St("scalar", lambda e: e.activation(
                        out=R2[:, 0:NQ], in_=PS[:, 0:NQ], func=AF.Gelu,
                        bias=cb1[:, 0:1]), inc=Sc)
                    continue
                def cn1mm(e, mc=mc, tq=tq, rot4=rot4):
                    dst = PS[:, rot4 * NQ:(rot4 + 1) * NQ]
                    for kc in range(CH):
                        mm = e.matmul(dst, R4[:, kc * 1024 + mc * 128: kc * 1024 + (mc + 1) * 128],
                                      outbf[:, kc * T + tq * NQ: kc * T + (tq + 1) * NQ],
                                      start=(kc == 0), stop=(kc == CH - 1), skip_group_check=True)
                    return mm
                cn1_mm_last = St("tensor", cn1mm, inc=Tc)
                W("scalar", Tc, cn1_mm_last)
                if i > 0 and idx == 0 and DEBUG_TAPS:
                    W("scalar", dTap, h1_tap)
                gelu1w[tq][mc] = St("scalar", lambda e, mc=mc, tq=tq, rot4=rot4: e.activation(
                    out=R2[:, mc * T + tq * NQ: mc * T + (tq + 1) * NQ],
                    in_=PS[:, rot4 * NQ:(rot4 + 1) * NQ], func=AF.Gelu,
                    bias=cb1[:, mc:mc + 1]), inc=Sc)
        gelu1 = [None, gelu1w[TQ - 1]]
        if i == 2:
            cn1_fence = St("tensor", lambda e: e.nop(), inc=Tc, drain=True)
        if i == 0 and DEBUG_TAPS:
            W("gpsimd", Sc, gelu1[TQ - 1][CH - 1])
            h1_tap = St("gpsimd", lambda e: e.dma_start(out=taps["t_h1"], in_=R2[:, :]),
                        inc=dTap, k=16)
        gelu2 = [0] * 4
        for mc in range(4):
            W("tensor", Sc, gelu1[1][CH - 1])
            if mc >= 2:
                W("tensor", Sc, gelu2[mc - 2])
            def cn2mm(e, mc=mc):
                base = (mc % 2) * 4
                for kc in range(CH):
                    lhs = CW2[:, kc * 512 + mc * 128: kc * 512 + (mc + 1) * 128]
                    for tq in range(TQ):
                        mm = e.matmul(PS[:, (base + tq) * NQ:(base + tq + 1) * NQ], lhs,
                                      R2[:, kc * T + tq * NQ: kc * T + (tq + 1) * NQ],
                                      start=(kc == 0), stop=(kc == CH - 1), skip_group_check=True)
                return mm
            mmn = St("tensor", cn2mm, inc=Tc)
            cn2_mm_last = mmn
            base = (mc % 2) * 4
            last = 0
            for tq in range(TQ):
                W("scalar", Tc, mmn)
                last = St("scalar", lambda e, mc=mc, tq=tq, base=base: e.activation(
                    out=R3[:, mc * T + tq * NQ: mc * T + (tq + 1) * NQ],
                    in_=PS[:, (base + tq) * NQ:(base + tq + 1) * NQ], func=AF.Gelu,
                    bias=cb2[:, mc:mc + 1]), inc=Sc)
            gelu2[mc] = last
        if i == 2:
            cn2_fence = St("tensor", lambda e: e.nop(), inc=Tc, drain=True)
        if i == 0:
            W("vector", s_gcret, 16)
            v_gcbf = St("vector", lambda e: e.tensor_scalar(
                out=gcbf, in0=gcp, scalar1=1.0 / 8192.0, scalar2=0.0, op0=ALU.mult,
                op1=ALU.add), inc=Vc)
            W("tensor", s_gw1, 16)
            W("tensor", s_gw2, 16)
            W("tensor", s_gw3, 16)
            W("tensor", Vc, v_gcbf)
            W("tensor", Sc, gelu2[3])
            def gf1(e):
                for mc in range(4):
                    for kc in range(CH):
                        mm = e.matmul(PS[:, 2048 + mc: 2048 + mc + 1],
                                      gw1sb[:, kc * 512 + mc * 128: kc * 512 + (mc + 1) * 128],
                                      gcbf[:, kc:kc + 1],
                                      start=(mc == 0 and kc == 0), stop=(mc == 3 and kc == CH - 1),
                                      skip_group_check=True)
                return mm
            t_gf1 = St("tensor", gf1, inc=Tc, drain=True)
            W("scalar", Tc, t_gf1)
            s_gh1 = 0
            for mc in range(4):
                s_gh1 = St("scalar", lambda e, mc=mc: e.activation(
                    out=gh1[:, mc:mc + 1], in_=PS[:, 2048 + mc:2048 + mc + 1],
                    func=AF.Gelu, bias=gb1[:, mc:mc + 1]), inc=Sc)
            W("tensor", Sc, s_gh1)
            def gf2(e):
                for mc in range(2):
                    for kc in range(4):
                        mm = e.matmul(PS[:, 2056 + mc: 2056 + mc + 1],
                                      gw2sb[:, kc * 256 + mc * 128: kc * 256 + (mc + 1) * 128],
                                      gh1[:, kc:kc + 1],
                                      start=(mc == 0 and kc == 0), stop=(mc == 1 and kc == 3),
                                      skip_group_check=True)
                return mm
            t_gf2 = St("tensor", gf2, inc=Tc, drain=True)
            W("scalar", Tc, t_gf2)
            s_gh2 = 0
            for mc in range(2):
                s_gh2 = St("scalar", lambda e, mc=mc: e.activation(
                    out=gh2[:, mc:mc + 1], in_=PS[:, 2056 + mc:2056 + mc + 1],
                    func=AF.Gelu, bias=gb2[:, mc:mc + 1]), inc=Sc)
            W("tensor", Sc, s_gh2)
            def gf3(e):
                for kc in range(2):
                    mm = e.matmul(PS[0:4, 2064:2065], GW3[:, kc * 4:(kc + 1) * 4],
                                  gh2[:, kc:kc + 1], start=(kc == 0), stop=(kc == 1),
                                  skip_group_check=True)
                return mm
            t_gf3 = St("tensor", gf3, inc=Tc, drain=True)
            W("scalar", Tc, t_gf3)
            St("scalar", lambda e: e.activation(
                out=g4sb, in_=PS[0:4, 2064:2065], func=AF.Tanh,
                bias=gb3h4, scale=0.5))
            c0_done = St("scalar", lambda e: e.mul(out=g7col, in_=g4sb, mul=0.7), inc=Sc)
        gelu3 = [0] * 2
        for mc in range(2):
            W("tensor", Sc, gelu2[3])
            if i == 0:
                W("tensor", Sc, c0_done)
            def cn3mm(e, mc=mc):
                base = (mc % 2) * 4
                for kc in range(4):
                    lhs = CW3[:, kc * 256 + mc * 128: kc * 256 + (mc + 1) * 128]
                    for tq in range(TQ):
                        mm = e.matmul(PS[:, (base + tq) * NQ:(base + tq + 1) * NQ], lhs,
                                      R3[:, kc * T + tq * NQ: kc * T + (tq + 1) * NQ],
                                      start=(kc == 0), stop=(kc == 3), skip_group_check=True)
                return mm
            mmn = St("tensor", cn3mm, inc=Tc)
            cn3_mm_last = mmn
            base = (mc % 2) * 4
            last = 0
            for tq in range(TQ):
                W("scalar", Tc, mmn)
                if tq == 0:
                    W("scalar", Tc, cn1_mm_last)
                last = St("scalar", lambda e, mc=mc, tq=tq, base=base: e.activation(
                    out=h3[:, mc * T + tq * NQ: mc * T + (tq + 1) * NQ],
                    in_=PS[:, (base + tq) * NQ:(base + tq + 1) * NQ], func=AF.Gelu,
                    bias=cb3[:, mc:mc + 1]), inc=Sc)
            gelu3[mc] = last
        if i == 2:
            cn3_fence = St("tensor", lambda e: e.nop(), inc=Tc, drain=True)
        W("tensor", Sc, gelu3[1])
        def cn4mm(e):
            for tq in range(TQ):
                for kc in range(2):
                    lhs = (cw4m0 if kc == 0 else cw4m1)[:, tq * 4:(tq + 1) * 4]
                    mm = e.matmul(PS[0:4, 2560:2560 + NQ], lhs,
                                  h3[:, kc * T + tq * NQ: kc * T + (tq + 1) * NQ],
                                  start=(tq == 0 and kc == 0), stop=(tq == TQ - 1 and kc == 1),
                                  skip_group_check=True)
            return mm
        t_cn4 = St("tensor", cn4mm, inc=Tc, drain=True)
        W("scalar", Tc, t_cn4)
        if i > 0 and DEBUG_TAPS:
            W("scalar", dTap, cp_tap)
        th_done[i] = St("scalar", lambda e: e.activation(
            out=TH4[:, :], in_=PS[0:4, 2560:2560 + NQ], func=AF.Tanh,
            bias=b4h4, scale=0.5), inc=Sc)
        if i == 0 and DEBUG_TAPS:
            W("gpsimd", Sc, th_done[0])
            th_tap = St("gpsimd", lambda g: g.dma_start(out=taps["t_th"], in_=TH4[:, :]),
                        inc=dTap, k=16)
            W("gpsimd", Sc, c0_done)
            St("gpsimd", lambda g: g.dma_start(out=taps["t_tiny"], in_=TINY[0:4, 40:48]),
               inc=dTap, k=16)
        W("vector", Sc, th_done[i])
        if i == 0:
            W("vector", Sc, c0_done)
            if DEBUG_TAPS:
                W("vector", dTap, th_tap)
        St("vector", lambda e: e.tensor_scalar(out=TH4[:, :], in0=TH4[:, :],
                                               scalar1=0.3, scalar2=0.0,
                                               op0=ALU.mult, op1=ALU.add))
        St("vector", lambda e: e.tensor_scalar(out=TH4[:, :], in0=TH4[:, :],
                                               scalar1=g7col, scalar2=0.0,
                                               op0=ALU.add, op1=ALU.add))
        St("vector", lambda e: e.tensor_scalar(out=TH4[:, :], in0=TH4[:, :],
                                               scalar1=arc, scalar2=0.1,
                                               op0=ALU.mult, op1=ALU.add))
        St("vector", lambda e: e.tensor_copy(CPHI[:, :], TH4[:, :]))
        cp_done[i] = St("vector", lambda e: e.tensor_tensor(
            out=CPLO[:, :], in0=TH4[:, :], in1=CPHI[:, :], op=ALU.subtract), inc=Vc)
        if i == 0 and DEBUG_TAPS:
            W("gpsimd", Vc, cp_done[0])
            cp_tap = St("gpsimd", lambda g: g.dma_start(out=taps["t_cp"], in_=TH4[:, :]),
                        inc=dTap, k=16)
        W("tensor", Vc, cp_done[i])
        def cbmm(e):
            for tq in range(TQ):
                dst = PS[:, 2048 + tq * NQ: 2048 + (tq + 1) * NQ]
                e.matmul(dst, jsel[:, tq * 128:(tq + 1) * 128], CPHI[:, :], start=True, stop=False,
                         skip_group_check=True)
                mm = e.matmul(dst, jsel[:, tq * 128:(tq + 1) * 128], CPLO[:, :], start=False,
                              stop=True, skip_group_check=True)
            return mm
        cb_mm[i] = St("tensor", cbmm, inc=Tc, drain=True)
        W("vector", Tc, cb_mm[i])
        if i == 0:
            W("vector", s_park, 16)
            if DEBUG_TAPS:
                W("vector", dTap, x1_tap)
        if i < 2:
            twq = [0] * TQ
            for tq in range(TQ):
                cbq = PS[:, 2048 + tq * NQ: 2048 + (tq + 1) * NQ]
                for p, (ca, sA, sB) in enumerate(((0, 1.0, -1.0), (1, 1.0, -1.0),
                                                  (2, -1.0, 1.0), (3, -1.0, 1.0))):
                    St("vector", lambda e, ca=ca, sA=sA, tq=tq, p=p, cbq=cbq: e.scalar_tensor_tensor(
                        out=twslot[2 * p],
                        in0=stream[:, (ca + 4) * T + tq * NQ: (ca + 4) * T + (tq + 1) * NQ],
                        scalar=sA, in1=cbq, op0=ALU.mult, op1=ALU.mult))
                    St("vector", lambda e, ca=ca, sB=sB, tq=tq, p=p, cbq=cbq: e.scalar_tensor_tensor(
                        out=twslot[2 * p + 1],
                        in0=stream[:, ca * T + tq * NQ: ca * T + (tq + 1) * NQ],
                        scalar=sB, in1=cbq, op0=ALU.mult, op1=ALU.mult))
                for p, (ca, _, _) in enumerate(((0, 0, 0), (1, 0, 0), (2, 0, 0), (3, 0, 0))):
                    St("vector", lambda e, ca=ca, tq=tq, p=p: e.tensor_tensor(
                        out=stream[:, ca * T + tq * NQ: ca * T + (tq + 1) * NQ],
                        in0=stream[:, ca * T + tq * NQ: ca * T + (tq + 1) * NQ],
                        in1=twslot[2 * p], op=ALU.add))
                    last = St("vector", lambda e, ca=ca, tq=tq, p=p: e.tensor_tensor(
                        out=stream[:, (ca + 4) * T + tq * NQ: (ca + 4) * T + (tq + 1) * NQ],
                        in0=stream[:, (ca + 4) * T + tq * NQ: (ca + 4) * T + (tq + 1) * NQ],
                        in1=twslot[2 * p + 1], op=ALU.add), inc=(Vc if p == 3 else None))
                twq[tq] = last
            twist_v[i] = [twq[3]] * 4
            twist_tq = twq
        else:
            x1R1 = R1[:, 0:8192].bitcast(F32)
            x1R3 = R3[:, 0:8192].bitcast(F32)
            x1R4 = R4[:, 0:8192].bitcast(F32)
            x1C6 = CW2[:, 0:4096].bitcast(F32)
            s_xr4 = mkdma(); s_xc6 = mkdma(); s_xr3 = mkdma(); s_x1pf = mkdma()
            W("sync", Tc, cn1_fence)
            St("sync", lambda e: e.dma_start(out=x1R4, in_=x1d[:, 4 * T:6 * T]), inc=s_xr4, k=16)
            W("sync", Tc, cn2_fence)
            St("sync", lambda e: e.dma_start(out=x1C6, in_=x1d[:, 6 * T:7 * T]), inc=s_xc6, k=16)
            W("sync", Tc, cn3_fence)
            St("sync", lambda e: e.dma_start(out=x1R3, in_=x1d[:, 2 * T:4 * T]), inc=s_xr3, k=16)
            W("sync", Tc, t_cn4)
            St("sync", lambda e: e.dma_start(out=x1R1, in_=x1d[:, 0:2 * T]), inc=s_x1pf, k=16)
            x1c = {0: x1R1[:, 0:T], 1: x1R1[:, T:2 * T],
                   2: x1R3[:, 0:T], 3: x1R3[:, T:2 * T],
                   4: x1R4[:, 0:T], 5: x1R4[:, T:2 * T], 6: x1C6[:, 0:T]}
            csem = {0: s_x1pf, 1: s_x1pf, 2: s_xr3, 3: s_xr3,
                    4: s_xr4, 5: s_xr4, 6: s_xc6}
            twp = [0] * 4
            for p, (sA, sB) in enumerate(((1.0, -1.0), (1.0, -1.0),
                                          (-1.0, 1.0), (-1.0, 1.0))):
                for tq in range(TQ):
                    cbq = PS[:, 2048 + tq * NQ: 2048 + (tq + 1) * NQ]
                    St("vector", lambda e, p=p, tq=tq, sA=sA, cbq=cbq: e.scalar_tensor_tensor(
                        out=twslot[2 * tq],
                        in0=stream[:, (p + 4) * T + tq * NQ: (p + 4) * T + (tq + 1) * NQ],
                        scalar=sA, in1=cbq, op0=ALU.mult, op1=ALU.mult))
                    St("vector", lambda e, p=p, tq=tq, sB=sB, cbq=cbq: e.scalar_tensor_tensor(
                        out=twslot[2 * tq + 1],
                        in0=stream[:, p * T + tq * NQ: p * T + (tq + 1) * NQ],
                        scalar=sB, in1=cbq, op0=ALU.mult, op1=ALU.mult))
                for tq in range(TQ):
                    St("vector", lambda e, p=p, tq=tq: e.tensor_tensor(
                        out=stream[:, p * T + tq * NQ: p * T + (tq + 1) * NQ],
                        in0=stream[:, p * T + tq * NQ: p * T + (tq + 1) * NQ],
                        in1=twslot[2 * tq], op=ALU.add))
                    twp[p] = St("vector", lambda e, p=p, tq=tq: e.tensor_tensor(
                        out=stream[:, (p + 4) * T + tq * NQ: (p + 4) * T + (tq + 1) * NQ],
                        in0=stream[:, (p + 4) * T + tq * NQ: (p + 4) * T + (tq + 1) * NQ],
                        in1=twslot[2 * tq + 1], op=ALU.add), inc=Vc)
                for c in (p, p + 4):
                    if c == 7:
                        continue
                    W("vector", csem[c], 16)
                    r2add[c] = St("vector", lambda e, c=c: e.tensor_tensor(
                        out=stream[:, c * T:(c + 1) * T], in0=stream[:, c * T:(c + 1) * T],
                        in1=x1c[c], op=ALU.add), inc=Vc)
            rs7 = [mkdma(), mkdma()]
            W("sync", Vc, twp[3])
            St("sync", lambda e: e.dma_start(out=x1t[0], in_=x1d[:, 7 * T: 7 * T + 1024]),
               inc=rs7[0], k=16)
            St("sync", lambda e: e.dma_start(out=x1t[1], in_=x1d[:, 7 * T + 1024: 8 * T]),
               inc=rs7[1], k=16)
            for h in range(2):
                W("vector", rs7[h], 16)
                r2add[7] = St("vector", lambda e, h=h: e.tensor_tensor(
                    out=stream[:, 7 * T + h * 1024: 7 * T + (h + 1) * 1024],
                    in0=stream[:, 7 * T + h * 1024: 7 * T + (h + 1) * 1024],
                    in1=x1t[h], op=ALU.add), inc=Vc)
        if i == 0 and DEBUG_TAPS:
            W("gpsimd", Vc, twist_tq[3])
            c1_tap = St("gpsimd", lambda g: g.dma_start(out=taps["t_c1"], in_=stream[:, :]),
                        inc=dTap, k=16)
        if i < 2:
            nbq_ = [[0] * CH for _ in range(TQ)]
            nbq_sem = [Sc, Gc, Gc, Gc]
            for tq in range(TQ):
                eng = "scalar" if tq == 0 else "gpsimd"
                W(eng, Vc, twist_tq[tq])
                if i == 0 and DEBUG_TAPS and tq == 0:
                    W(eng, dTap, c1_tap)
                for c in range(CH):
                    if tq == 0:
                        nbq_[tq][c] = St("scalar", lambda e, c=c, tq=tq: e.activation(
                            out=outbf[:, c * T + tq * NQ: c * T + (tq + 1) * NQ],
                            in_=stream[:, c * T + tq * NQ: c * T + (tq + 1) * NQ],
                            func=AF.Copy), inc=Sc)
                    else:
                        nbq_[tq][c] = St("gpsimd", lambda e, c=c, tq=tq: e.tensor_copy(
                            outbf[:, c * T + tq * NQ: c * T + (tq + 1) * NQ],
                            stream[:, c * T + tq * NQ: c * T + (tq + 1) * NQ]), inc=Gc,
                            drain=(c == CH - 1))
            prev_outbf_tq = nbq_
            prev_outbf_sem = nbq_sem
            prev_outbf = [nbq_[3][c] for c in range(CH)]

    norm2 = [0] * (TQ * CH)
    layer_norm("ln2", norm2)

    ysem = [mkdma() for _ in range(CH)]
    for c in range(CH):
        W("sync", Vc, norm2[3 * CH + c])
        St("sync", lambda e, c=c: e.dma_start(out=y[c], in_=stream[:, c * T:(c + 1) * T]),
           inc=ysem[c], k=16)
    for c in range(CH):
        W("gpsimd", ysem[c], 16)
    St("gpsimd", lambda e: e.nop())

    class _SimSync:
        _SKIP = {"wait_ge", "dma_start", "dma_start_transpose", "then_inc", "drain", "nop"}

        def __init__(self, eng, sem):
            self._e = eng
            self._s = sem
            self._n = 0

        def __getattr__(self, name):
            a = getattr(self._e, name)
            if name.startswith("_") or name in self._SKIP or not callable(a):
                return a
            def call(*args, **kw):
                if self._n:
                    self._e.wait_ge(self._s, self._n)
                r = a(*args, **kw)
                try:
                    r.then_inc(self._s, 1)
                    self._n += 1
                except Exception:
                    pass
                return r
            return call

    if SIMSYNC:
        ss = {e: es.enter_context(nc.semaphore(f"ss_{e}")) for e in ("t", "v", "s")}

    with nc.Block() as block:
        @block.tensor
        def _(t):
            if SIMSYNC:
                t = _SimSync(t, ss["t"])
            for f in kb.prog["tensor"]:
                f(t)

        @block.vector
        def _(v):
            if SIMSYNC:
                v = _SimSync(v, ss["v"])
            for f in kb.prog["vector"]:
                f(v)

        @block.scalar
        def _(s):
            if SIMSYNC:
                s = _SimSync(s, ss["s"])
            for f in kb.prog["scalar"]:
                f(s)

        @block.gpsimd
        def _(g):
            for f in kb.prog["gpsimd"]:
                f(g)

        @block.sync
        def _(s):
            for f in kb.prog["sync"]:
                f(s)

    es.close()
    return nc


_NC_CACHE = {}


def _get_nc():
    key = ("main", DEBUG_TAPS)
    if key not in _NC_CACHE:
        _NC_CACHE[key] = build()
    return _NC_CACHE[key]


def _host_packs(inp):
    bp = np.zeros((128, 64), np.float32)
    fm = lambda b: np.ascontiguousarray(np.asarray(b).reshape(-1, 128).T)
    bp[:, 0:8] = fm(inp["L0_attn_wq_b"])
    bp[:, 8:16] = fm(inp["L0_attn_wg_b"])
    bp[:, 16:24] = fm(inp["L0_attn_out_b"])
    bp[:, 24:32] = fm(inp["L0_ffn_cn_b1"])
    bp[:, 32:36] = fm(inp["L0_ffn_cn_b2"])
    bp[:, 36:38] = fm(inp["L0_ffn_cn_b3"])
    bp[:, 38:42] = fm(inp["L0_ffn_gc_b1"])
    bp[:, 42:44] = fm(inp["L0_ffn_gc_b2"])
    bp[0:4, 44] = np.asarray(inp["L0_ffn_cn_b4"])[0]
    bp[0:4, 45] = np.asarray(inp["L0_ffn_gc_b3"])[0]
    bp[0:4, 46] = np.asarray(inp["L0_ffn_adaptive_range"])[0]
    bp[:, 47] = 1e-5
    rp = np.zeros((64, 1024), np.float32)
    rp[0, :] = np.asarray(inp["L0_attn_wk_b"])
    rp[32, :] = np.asarray(inp["L0_attn_wv_b"])
    cp = np.zeros((128, 1824), np.float32)
    for tq in range(4):
        cp[:, tq * 4 + tq] = 0.0
    for tq in range(4):
        for j in range(4):
            if j == tq:
                cp[:, tq * 4 + j] = 1.0
    for c in range(8):
        cp[2 * c, 16 + c * 128: 16 + c * 128 + 64] = 1.0
        cp[2 * c + 1, 16 + c * 128 + 64: 16 + (c + 1) * 128] = 1.0
    for tq in range(4):
        cp[tq, 1040 + tq * 128: 1040 + (tq + 1) * 128] = 1.0
    cp[0:2, 1552:1680] = 1.0
    cp[32, 1552:1680] = 1.0
    cp[:, 1680] = 1.0
    for c in range(8):
        cp[0:64, 1696 + c * 16 + 2 * c] = 1.0
        cp[64:128, 1696 + c * 16 + 2 * c + 1] = 1.0
    gwp = np.zeros((128, 8), np.float32)
    g3 = np.asarray(inp["L0_ffn_gc_w3"]).reshape(2, 128)
    for kc in range(2):
        for j in range(4):
            gwp[:, kc * 4 + j] = g3[kc]
    return bp, rp, cp.astype(ml_dtypes.bfloat16), gwp


def kernel(**inp):
    x = np.asarray(inp["L0_attn_x"])
    bp, rp, cp, gwp = _host_packs(inp)
    base = dict(
        wq=np.asarray(inp["L0_attn_wq_w"]), wk=np.asarray(inp["L0_attn_wk_w"]),
        wv=np.asarray(inp["L0_attn_wv_w"]), wg=np.asarray(inp["L0_attn_wg_w"]),
        wo=np.asarray(inp["L0_attn_out_w"]), cw1=np.asarray(inp["L0_ffn_cn_w1"]),
        cw2=np.asarray(inp["L0_ffn_cn_w2"]), cw3=np.asarray(inp["L0_ffn_cn_w3"]),
        cw4=np.asarray(inp["L0_ffn_cn_w4"]), gw1=np.asarray(inp["L0_ffn_gc_w1"]),
        gw2=np.asarray(inp["L0_ffn_gc_w2"]), gwpack=gwp,
        biaspack=bp, rowpack=rp, cpack=cp,
    )
    in_maps = []
    for core in range(8):
        b, s = core // 4, core % 4
        m = dict(base)
        m["x"] = np.ascontiguousarray(x[b, s * T:(s + 1) * T, :].T)
        in_maps.append(m)
    nc = _get_nc()
    res = run_bass_kernel_spmd(nc, in_maps, core_ids=list(range(8)))
    out = np.empty((2, 8192, D), np.float32)
    for core in range(8):
        b, s = core // 4, core % 4
        yc = np.asarray(res.results[core]["y"])
        out[b, s * T:(s + 1) * T, :] = yc.reshape(D, T).T
    return out


# version 23
# speedup vs baseline: 1.0007x; 1.0007x over previous
import os
import numpy as np
import ml_dtypes
import concourse.bass as bass
import concourse.mybir as mybir
from concourse.bass_utils import run_bass_kernel_spmd

F32 = mybir.dt.float32
BF16 = mybir.dt.bfloat16
AF = mybir.ActivationFunctionType
ALU = mybir.AluOpType
AXX = mybir.AxisListType.X

D = 1024
T = 2048
CH = 8
NT = 16
NQ = 512
TQ = 4

DEBUG_TAPS = False


class Ctr:
    __slots__ = ("h", "n")

    def __init__(self, h):
        self.h = h
        self.n = 0


class KB:
    ENGS = ("tensor", "vector", "scalar", "gpsimd", "sync")

    def __init__(self, simsync=False):
        self.prog = {e: [] for e in self.ENGS}
        self.simsync = simsync

    def emit(self, eng, fn):
        self.prog[eng].append(fn)

    def wait(self, eng, ctr, val):
        assert 0 <= val <= ctr.n, (eng, val, ctr.n)
        if val > 0:
            self.emit(eng, lambda e, h=ctr.h, v=val: e.wait_ge(h, v))

    def step(self, eng, fn, inc=None, k=1, drain=False):
        if inc is None:
            self.emit(eng, fn)
            return None
        inc.n += k
        if drain or (self.simsync and eng in ("tensor", "vector", "scalar")):
            def run(e, f=fn, h=inc.h, kk=k):
                f(e)
                e.drain().then_inc(h, kk)
            self.emit(eng, run)
        else:
            self.emit(eng, lambda e, f=fn, h=inc.h, kk=k: f(e).then_inc(h, kk))
        return inc.n


def build():
    nc = bass.Bass("TRN2", target_bir_lowering=False, dynamic_dma_scratch_size=12288)

    dr = {}
    def din(name, shape):
        dr[name] = nc.dram_tensor(name, list(shape), F32, kind="ExternalInput").ap()
    din("x", (D, T))
    for w in ("wq", "wk", "wv", "wg", "wo", "cw1"):
        din(w, (D, D))
    din("cw2", (D, 512)); din("cw3", (512, 256)); din("cw4", (256, 1))
    din("gw1", (D, 512)); din("gw2", (512, 256))
    din("gwpack", (128, 8))
    din("biaspack", (128, 64))
    din("rowpack", (64, D))
    dr["cpack"] = nc.dram_tensor("cpack", [128, 1824], BF16, kind="ExternalInput").ap()
    y = nc.dram_tensor("y", [CH, 128, T], F32, kind="ExternalOutput").ap()
    x1d = nc.dram_tensor("x1d", [128, CH * T], F32, kind="Internal").ap()
    cc_in = nc.dram_tensor("cc_in", [128, 1032], F32, kind="Internal").ap()
    cc_out = nc.dram_tensor("cc_out", [128, 1032], F32, kind="Internal").ap()
    gc_in = nc.dram_tensor("gc_in", [128, 8], F32, kind="Internal").ap()
    gc_out = nc.dram_tensor("gc_out", [128, 8], F32, kind="Internal").ap()
    taps = {}
    if DEBUG_TAPS:
        for nm, shape, dt_ in (("t_kvacc", [128, 1032], F32), ("t_phiq", [128, CH * T], BF16),
                               ("t_r", [128, CH * T], F32), ("t_x1", [128, CH * T], F32),
                               ("t_z", [16, T], F32), ("t_gs", [128, CH * T], BF16),
                               ("t_c1", [128, CH * T], F32), ("t_cp", [4, 512], F32),
                               ("t_th", [4, 512], F32), ("t_tiny", [4, 8], F32),
                               ("t_ksf", [128, 8], F32), ("t_kvbd", [128, 1024], BF16),
                               ("t_cb", [128, 2048], F32),
                               ("t_h1", [128, CH * T], BF16)):
            taps[nm] = nc.dram_tensor(nm, shape, dt_, kind="ExternalOutput").ap()

    from contextlib import ExitStack
    es = ExitStack()
    sb = lambda name, shape, dt_: es.enter_context(nc.sbuf_tensor(name, list(shape), dt_))

    stream = sb("stream", (128, CH * T), F32)
    R1 = sb("R1", (128, CH * T), BF16)
    R2 = sb("R2", (128, CH * T), BF16)
    R3 = sb("R3", (128, CH * 1024), BF16)
    R4 = sb("R4", (128, CH * 1024), BF16)
    CW2 = sb("CW2", (128, 4096), BF16)
    CW3 = sb("CW3", (128, 1024), BF16)
    SCR = sb("SCR", (128, 3072), F32)
    ZHI = sb("ZHI", (16, 2048), BF16)
    ZLO = sb("ZLO", (16, 2048), BF16)
    S4 = sb("S4", (4, 512), F32)
    Q4 = sb("Q4", (4, 512), F32)
    TH4 = sb("TH4", (4, 512), F32)
    CPHI = sb("CPHI", (4, 512), BF16)
    CPLO = sb("CPLO", (4, 512), BF16)
    HI4 = sb("HI4", (64, D), BF16)
    LO4 = sb("LO4", (64, D), BF16)
    BP = sb("BP", (128, 64), F32)
    CPK = sb("CPK", (128, 1824), BF16)
    KVBD = sb("KVBD", (128, 1024), BF16)
    TINY = sb("TINY", (128, 48), F32)
    TINYB = sb("TINYB", (128, 64), BF16)
    PS = es.enter_context(nc.psum_tensor("PS", [128, 4096], F32))

    xbf = R1
    phiq = R2
    kvacc = R2[:, 0:2064].bitcast(F32)
    ktile = [SCR[:, 0:512].bitcast(BF16), SCR[:, 512:1024].bitcast(BF16)]
    vtile = [SCR[:, 1024:1536].bitcast(BF16), SCR[:, 1536:2048].bitcast(BF16)]
    etile = [SCR[:, 2048:2560], SCR[:, 2560:3072]]
    etq = [SCR[:, 0:512], SCR[:, 512:1024]]
    tgt = [SCR[:, 1024:1280].bitcast(BF16), SCR[:, 1280:1536].bitcast(BF16)]
    gw1sb = SCR[:, 0:2048].bitcast(BF16)
    gw2sb = SCR[:, 2048:2560].bitcast(BF16)
    x1t = [SCR[:, 0:1024], SCR[:, 1024:2048]]
    twslot = [SCR[:, i * 256:(i + 1) * 256].bitcast(BF16) for i in range(8)]
    rowstg = SCR[0:64, 0:1024]
    outbf = R1
    h3 = R1
    twA = R1[:, 0:4096].bitcast(F32)
    twB = R1[:, 4096:8192].bitcast(F32)
    r2t = [R1[:, 8192 + i * 2048: 8192 + (i + 1) * 2048] for i in range(4)]
    a4hi = CPHI[0:4, :]
    a4lo = CPLO[0:4, :]
    b4hi = TH4[0:4, 0:256].bitcast(BF16)
    b4lo = TH4[0:4, 256:512].bitcast(BF16)
    ntmp = SCR[:, 2560:3072]
    tqmask = CPK[:, 0:16]
    selz = CPK[0:16, 16:1040]
    jsel = CPK[0:4, 1040:1552]
    ones1 = CPK[0:1, 1552:1680]
    ones1b = CPK[32:33, 1552:1680]
    onescol = CPK[:, 1680:1681]
    zmask = CPK[:, 1696:1824]
    bq = BP[:, 0:8]; bg = BP[:, 8:16]; bo = BP[:, 16:24]
    cb1 = BP[:, 24:32]; cb2 = BP[:, 32:36]; cb3 = BP[:, 36:38]
    gb1 = BP[:, 38:42]; gb2 = BP[:, 42:44]
    cb4c = BP[0:4, 44:45]; gb3c = BP[0:4, 45:46]; arc = BP[0:4, 46:47]
    epsc = BP[:, 47:48]
    nbq = TINY[:, 0:8]; bq1 = TINY[:, 8:16]; bgh = TINY[:, 16:24]
    epz = TINY[:, 44:45]
    ksf = TINY[:, 24:32]; gcp = TINY[:, 32:40]; cw4f = TINY[:, 40:42]
    b4h4 = TINY[0:4, 42:43]; gb3h4 = TINY[0:4, 43:44]
    g4sb = TINY[0:4, 46:47]; g7col = TINY[0:4, 47:48]
    ZLF = SCR[:, 1024:1152]
    ZLF2 = SCR[:, 1152:1280]
    zlhi = SCR[:, 1280:1344].bitcast(BF16)
    zllo = SCR[:, 1344:1408].bitcast(BF16)
    ZBr = SCR[0:16, 2560:3072]
    gcbf = TINYB[:, 0:8]; gh1 = TINYB[:, 8:12]; gh2 = TINYB[:, 12:14]
    cw4m0 = TINYB[:, 16:32]; cw4m1 = TINYB[:, 32:48]
    GW3 = TINYB[:, 48:56]

    SIMSYNC = os.environ.get("BASS_SIM_FP64") == "1"
    sems = {}
    for s in ("dT", "Ts", "Vs", "Ss", "Gs", "CC1", "CC2"):
        sems[s] = Ctr(es.enter_context(nc.semaphore(s)))
    dTap, CC1, CC2, Tc, Vc, Sc, Gc = (
        sems[s] for s in ("dT", "CC1", "CC2", "Ts", "Vs", "Ss", "Gs"))
    _semn = [0]
    def mkdma():
        _semn[0] += 1
        return Ctr(es.enter_context(nc.semaphore(f"dma{_semn[0]}")))

    kb = KB(simsync=SIMSYNC)
    W = kb.wait
    St = kb.step

    s_bp = mkdma(); s_rp = mkdma(); s_cpk = mkdma(); s_x = mkdma(); s_cw4f = mkdma()
    St("sync", lambda e: e.dma_start(out=BP[:, :], in_=dr["biaspack"]), inc=s_bp, k=16)
    St("sync", lambda e: e.dma_start(out=rowstg, in_=dr["rowpack"]), inc=s_rp, k=16)
    St("sync", lambda e: e.dma_start(out=CPK[:, :], in_=dr["cpack"]), inc=s_cpk, k=16)
    St("sync", lambda e: e.dma_start(
        out=stream[:, :].rearrange("p (c t) -> p c t", c=CH),
        in_=dr["x"].rearrange("(c p) t -> p c t", p=128)), inc=s_x, k=16)
    St("sync", lambda e: e.dma_start(
        out=cw4f, in_=dr["cw4"].rearrange("(kc p) o -> p (kc o)", p=128),
        allow_slow_non_contiguous=True), inc=s_cw4f, k=16)

    def wload(dst, name, kc):
        return lambda e: e.dma_start(
            out=dst.rearrange("p (kc m) -> p kc m", kc=kc),
            in_=dr[name].rearrange("(kc p) m -> p kc m", p=128))

    s_xbf = mkdma(); s_wk = mkdma(); s_wv = mkdma()
    s_cw2 = mkdma(); s_cw3 = mkdma(); s_gw3 = mkdma()
    St("gpsimd", lambda e: e.dma_start(
        out=xbf[:, :].rearrange("p (c t) -> p c t", c=CH),
        in_=dr["x"].rearrange("(c p) t -> p c t", p=128)), inc=s_xbf, k=16)
    St("gpsimd", wload(R3, "wk", 8), inc=s_wk, k=16)
    St("gpsimd", wload(R4, "wv", 8), inc=s_wv, k=16)
    St("gpsimd", wload(CW2, "cw2", 8), inc=s_cw2, k=16)
    St("gpsimd", wload(CW3, "cw3", 4), inc=s_cw3, k=16)
    St("gpsimd", lambda e: e.dma_start(out=GW3, in_=dr["gwpack"]), inc=s_gw3, k=16)

    for s_ in (s_bp, s_rp, s_cpk, s_cw4f):
        W("vector", s_, 16)
    St("vector", lambda e: e.tensor_scalar(out=nbq, in0=bq, scalar1=-1.0, scalar2=0.0, op0=ALU.mult, op1=ALU.add))
    St("vector", lambda e: e.tensor_scalar(out=bq1, in0=bq, scalar1=1.0, scalar2=0.0, op0=ALU.add, op1=ALU.add))
    St("vector", lambda e: e.tensor_scalar(out=bgh, in0=bg, scalar1=0.5, scalar2=0.0, op0=ALU.mult, op1=ALU.add))
    W("scalar", s_bp, 16)
    St("scalar", lambda e: e.mul(out=b4h4, in_=cb4c, mul=0.5))
    St("scalar", lambda e: e.mul(out=gb3h4, in_=gb3c, mul=0.5), inc=Sc)
    St("vector", lambda e: e.memset(epz, 1e-6))
    St("vector", lambda e: e.tensor_copy(HI4[:, :], rowstg))
    St("vector", lambda e: e.tensor_tensor(out=LO4[:, :], in0=rowstg, in1=HI4[:, :], op=ALU.subtract))
    St("vector", lambda e: e.tensor_scalar(out=cw4m0, in0=tqmask, scalar1=cw4f[:, 0:1], scalar2=0.0, op0=ALU.mult, op1=ALU.add))
    v_consts = St("vector", lambda e: e.tensor_scalar(out=cw4m1, in0=tqmask, scalar1=cw4f[:, 1:2],
                                                      scalar2=0.0, op0=ALU.mult, op1=ALU.add), inc=Vc)

    kdone = [0] * NT; vdone = [0] * NT; vevac = [0] * NT
    kstt = [0] * NT; kvdone = [0] * NT; kvadd = [0] * NT

    def proj_tm(wsb, base, hrow, t):
        onesr = ones1 if hrow == 0 else ones1b
        def f(e):
            for kc in range(CH):
                lhs = xbf[:, kc * T + t * 128: kc * T + (t + 1) * 128]
                for h in range(2):
                    e.matmul(PS[:, base + h * NQ: base + (h + 1) * NQ], lhs,
                             wsb[:, kc * 1024 + h * NQ: kc * 1024 + (h + 1) * NQ],
                             start=(kc == 0), stop=False, skip_group_check=True)
            for h in range(2):
                dst = PS[:, base + h * NQ: base + (h + 1) * NQ]
                e.matmul(dst, onesr, HI4[hrow:hrow + 1, h * NQ:(h + 1) * NQ],
                         start=False, stop=False, skip_group_check=True)
                mm = e.matmul(dst, onesr, LO4[hrow:hrow + 1, h * NQ:(h + 1) * NQ],
                              start=False, stop=True, skip_group_check=True)
            return mm
        return f

    def kv_mm(j):
        sj = j % 2
        kbj = sj * 2048; vbj = kbj + 1024
        def f(e):
            for c in range(CH):
                off = kbj + (c // 4) * NQ + (c % 4) * 128
                e.matmul(PS[:, off:off + 128], ktile[sj][:, c * 128:(c + 1) * 128],
                         vtile[sj][:, c * 128:(c + 1) * 128],
                         start=(c % 4 == 0), stop=(c % 4 == 3), skip_group_check=True)
                mm = e.matmul(PS[:, vbj + c: vbj + c + 1],
                              ktile[sj][:, c * 128:(c + 1) * 128], onescol,
                              start=(c == 0), stop=(c == CH - 1), skip_group_check=True)
            return mm
        return f

    def emit_kvadd(j):
        sj = j % 2; kbj = sj * 2048; vbj = kbj + 1024
        W("vector", Tc, kvdone[j])
        if j == 0:
            St("vector", lambda e, kbj=kbj: e.tensor_copy(kvacc[:, 0:512], PS[:, kbj:kbj + 512]))
            St("vector", lambda e, kbj=kbj: e.tensor_copy(kvacc[:, 512:1024], PS[:, kbj + 512:kbj + 1024]))
            kvadd[j] = St("vector", lambda e, vbj=vbj: e.tensor_copy(
                kvacc[:, 1024:1032], PS[:, vbj:vbj + 8]), inc=Vc)
        else:
            St("vector", lambda e, kbj=kbj: e.tensor_tensor(
                out=kvacc[:, 0:512], in0=kvacc[:, 0:512], in1=PS[:, kbj:kbj + 512], op=ALU.add))
            St("vector", lambda e, kbj=kbj: e.tensor_tensor(
                out=kvacc[:, 512:1024], in0=kvacc[:, 512:1024], in1=PS[:, kbj + 512:kbj + 1024], op=ALU.add))
            kvadd[j] = St("vector", lambda e, vbj=vbj: e.tensor_tensor(
                out=kvacc[:, 1024:1032], in0=kvacc[:, 1024:1032], in1=PS[:, vbj:vbj + 8],
                op=ALU.add), inc=Vc)

    for t in range(NT):
        s = t % 2
        kbase = s * 2048
        vbase = kbase + 1024
        if t >= 2:
            W("tensor", Vc, kvadd[t - 2])
        if t == 0:
            W("tensor", s_xbf, 16)
            W("tensor", s_wk, 16)
            W("tensor", s_cpk, 16)
            W("tensor", Vc, v_consts)
        kdone[t] = St("tensor", proj_tm(R3, kbase, 0, t), inc=Tc)
        if t == 0:
            W("tensor", s_wv, 16)
        vdone[t] = St("tensor", proj_tm(R4, vbase, 32, t), inc=Tc)
        if t >= 1:
            jj = t - 1
            W("tensor", Vc, kstt[jj])
            W("tensor", Sc, vevac[jj])
            kvdone[jj] = St("tensor", kv_mm(jj), inc=Tc, drain=True)
        W("scalar", Tc, kdone[t])
        if t >= 2:
            W("scalar", Vc, kstt[t - 2])
        khalf = [0, 0]
        for h in range(2):
            if h == 1:
                W("scalar", Vc, khalf[0])
            St("scalar", lambda e, s=s, kbase=kbase, h=h: e.activation(
                out=etile[s], in_=PS[:, kbase + h * NQ: kbase + (h + 1) * NQ],
                func=AF.Relu, scale=-1.0))
            kexp_h = St("scalar", lambda e, s=s: e.activation(
                out=etile[s], in_=etile[s], func=AF.Exp, scale=-1.0), inc=Sc)
            W("vector", Sc, kexp_h)
            W("vector", Tc, kdone[t])
            if t >= 2:
                W("vector", Tc, kvdone[t - 2])
            khalf[h] = St("vector", lambda e, s=s, kbase=kbase, h=h: e.scalar_tensor_tensor(
                out=ktile[s][:, h * NQ:(h + 1) * NQ],
                in0=PS[:, kbase + h * NQ: kbase + (h + 1) * NQ],
                scalar=1.0, in1=etile[s], op0=ALU.add, op1=ALU.max), inc=Vc)
        kstt[t] = khalf[1]
        W("scalar", Tc, vdone[t])
        if t >= 2:
            W("scalar", Tc, kvdone[t - 2])
        vevac[t] = St("scalar", lambda e, s=s, vbase=vbase: e.activation(
            out=vtile[s][:, :], in_=PS[:, vbase:vbase + 1024], func=AF.Copy), inc=Sc)
        if t >= 1:
            emit_kvadd(t - 1)
    jj = NT - 1
    W("tensor", Vc, kstt[jj])
    W("tensor", Sc, vevac[jj])
    kvdone[jj] = St("tensor", kv_mm(jj), inc=Tc, drain=True)
    emit_kvadd(jj)

    s_wq = mkdma(); s_wg = mkdma(); s_ccin = mkdma(); s_kvbd = mkdma(); s_ksf = mkdma()
    W("gpsimd", Tc, kdone[NT - 1])
    St("gpsimd", wload(R3, "wq", 8), inc=s_wq, k=16)
    W("gpsimd", Tc, vdone[NT - 1])
    St("gpsimd", wload(R4, "wg", 8), inc=s_wg, k=16)
    W("gpsimd", Vc, kvadd[NT - 1])
    if DEBUG_TAPS:
        St("gpsimd", lambda e: e.dma_start(out=taps["t_kvacc"], in_=kvacc[:, :]), inc=dTap, k=16)
    St("gpsimd", lambda e: e.dma_start(out=cc_in, in_=kvacc[:, :]), inc=s_ccin, k=16)
    W("gpsimd", s_ccin, 16)
    St("gpsimd", lambda e: e.collective_compute(
        "AllReduce", ALU.add, replica_groups=[[0, 1, 2, 3], [4, 5, 6, 7]],
        ins=[cc_in.opt()], outs=[cc_out.opt()]), inc=CC1)
    W("gpsimd", CC1, 1)
    St("gpsimd", lambda e: e.dma_start(out=KVBD[:, :], in_=cc_out[:, 0:1024]), inc=s_kvbd, k=16)
    St("gpsimd", lambda e: e.dma_start(out=ksf, in_=cc_out[:, 1024:1032]), inc=s_ksf, k=16)

    phiq_stt = [[0] * TQ for _ in range(CH)]
    phiq_mm = [0] * CH
    etq_guard = [0, 0]
    for c in range(CH):
        if c == 0:
            W("tensor", s_wq, 16)
            W("tensor", Vc, kvadd[NT - 1])
            W("tensor", Vc, kvadd[NT - 2])
        if c >= 2:
            W("tensor", Vc, phiq_stt[c - 2][TQ - 1])
        qb = (c % 2) * 4
        def qmm(e, c=c, qb=qb):
            for kc in range(CH):
                lhs = R3[:, kc * 1024 + c * 128: kc * 1024 + (c + 1) * 128]
                for tq in range(TQ):
                    mm = e.matmul(PS[:, (qb + tq) * NQ:(qb + tq + 1) * NQ], lhs,
                                  xbf[:, kc * T + tq * NQ: kc * T + (tq + 1) * NQ],
                                  start=(kc == 0), stop=(kc == CH - 1), skip_group_check=True)
            return mm
        phiq_mm[c] = St("tensor", qmm, inc=Tc)
        for tq in range(TQ):
            sl = tq % 2
            W("scalar", Tc, phiq_mm[c])
            if etq_guard[sl]:
                W("scalar", Vc, etq_guard[sl])
            St("scalar", lambda e, c=c, qb=qb, tq=tq, sl=sl: e.activation(
                out=etq[sl], in_=PS[:, (qb + tq) * NQ:(qb + tq + 1) * NQ],
                func=AF.Relu, bias=nbq[:, c:c + 1], scale=-1.0))
            sx = St("scalar", lambda e, sl=sl: e.activation(
                out=etq[sl], in_=etq[sl], func=AF.Exp, scale=-1.0), inc=Sc)
            W("vector", Sc, sx)
            if c == 0 and tq == 0:
                W("vector", s_ccin, 16)
            phiq_stt[c][tq] = St("vector", lambda e, c=c, qb=qb, tq=tq, sl=sl: e.scalar_tensor_tensor(
                out=phiq[:, c * T + tq * NQ: c * T + (tq + 1) * NQ],
                in0=PS[:, (qb + tq) * NQ:(qb + tq + 1) * NQ], scalar=bq1[:, c:c + 1],
                in1=etq[sl], op0=ALU.add, op1=ALU.max), inc=Vc)
            etq_guard[sl] = phiq_stt[c][tq]
    if DEBUG_TAPS:
        W("gpsimd", Vc, phiq_stt[CH - 1][TQ - 1])
        phiq_tap = St("gpsimd", lambda e: e.dma_start(out=taps["t_phiq"], in_=phiq[:, :]),
                      inc=dTap, k=16)
        St("gpsimd", lambda e: e.dma_start(out=taps["t_ksf"], in_=ksf), inc=dTap, k=16)

    W("gpsimd", s_kvbd, 16)
    g_kv = 0
    for c in range(CH):
        St("gpsimd", lambda e, c=c: e.memset(KVBD[0:64, c * 128 + 64:(c + 1) * 128], 0.0))
        g_kv = St("gpsimd", lambda e, c=c: e.memset(KVBD[64:128, c * 128:c * 128 + 64], 0.0),
                  inc=Gc, drain=(c == CH - 1))
    W("vector", s_cpk, 16)
    W("vector", s_ksf, 16)
    St("vector", lambda e: e.tensor_copy(ZLF2[:, :], zmask))
    for c in range(CH):
        St("vector", lambda e, c=c: e.tensor_scalar(
            out=ZLF[:, c * 16:(c + 1) * 16], in0=ZLF2[:, c * 16:(c + 1) * 16],
            scalar1=ksf[:, c:c + 1], scalar2=0.0, op0=ALU.mult, op1=ALU.add))
    St("vector", lambda e: e.tensor_copy(zlhi[:, :], ZLF[:, :]))
    St("vector", lambda e: e.tensor_copy(ZLF2[:, :], zlhi[:, :]))
    St("vector", lambda e: e.tensor_tensor(out=ZLF[:, :], in0=ZLF[:, :], in1=ZLF2[:, :],
                                           op=ALU.subtract))
    v_zl = St("vector", lambda e: e.tensor_copy(zllo[:, :], ZLF[:, :]), inc=Vc)

    if DEBUG_TAPS:
        W("gpsimd", Gc, g_kv)
        W("gpsimd", Vc, v_zl)
        St("gpsimd", lambda e: e.dma_start(out=taps["t_kvbd"], in_=KVBD[:, :]), inc=dTap, k=16)

    W("tensor", Gc, g_kv)
    W("tensor", Vc, v_zl)
    W("tensor", Vc, phiq_stt[CH - 1][TQ - 1])
    W("tensor", Vc, kvadd[NT - 1])
    def zden_mm(e):
        for c in range(CH):
            for tq in range(TQ):
                dst = PS[0:16, 1024 + tq * NQ: 1024 + (tq + 1) * NQ]
                e.matmul(dst, zlhi[:, c * 16:(c + 1) * 16],
                         phiq[:, c * T + tq * NQ: c * T + (tq + 1) * NQ],
                         start=(c == 0), stop=False, skip_group_check=True)
                mm = e.matmul(dst, zllo[:, c * 16:(c + 1) * 16],
                              phiq[:, c * T + tq * NQ: c * T + (tq + 1) * NQ],
                              start=False, stop=(c == CH - 1), skip_group_check=True)
        return mm
    zden_done = St("tensor", zden_mm, inc=Tc, drain=True)

    zrow = [0] * TQ
    for tq in range(TQ):
        W("scalar", Tc, zden_done)
        if tq > 0:
            W("scalar", Vc, zrow[tq - 1])
        St("scalar", lambda e, tq=tq: e.activation(
            out=ZBr, in_=PS[0:16, 1024 + tq * NQ: 1024 + (tq + 1) * NQ],
            func=AF.Ln, bias=epz[0:16, :]))
        zs = St("scalar", lambda e: e.activation(out=ZBr, in_=ZBr, func=AF.Exp,
                                                 scale=-1.0), inc=Sc)
        W("vector", Sc, zs)
        slot = tq
        St("vector", lambda e, slot=slot: e.tensor_scalar(
            out=ZHI[:, slot * NQ:(slot + 1) * NQ], in0=ZBr, scalar1=0.5,
            scalar2=0.0, op0=ALU.mult, op1=ALU.add))
        zrow[tq] = St("vector", lambda e, slot=slot: e.scalar_tensor_tensor(
            out=ZLO[:, slot * NQ:(slot + 1) * NQ], in0=ZBr, scalar=0.5,
            in1=ZHI[:, slot * NQ:(slot + 1) * NQ], op0=ALU.mult, op1=ALU.subtract), inc=Vc)
        if DEBUG_TAPS:
            W("gpsimd", Vc, zrow[tq])
            zt = St("gpsimd", lambda g, tq=tq: g.dma_start(
                out=taps["t_z"][:, tq * NQ:(tq + 1) * NQ], in_=ZBr), inc=dTap, k=16)
            W("vector", dTap, zt)

    s_wo = mkdma()
    W("gpsimd", Tc, phiq_mm[CH - 1])
    St("gpsimd", wload(R3, "wo", 8), inc=s_wo, k=16)

    gs_v = [[0] * CH for _ in range(TQ)]
    oevac = [[0] * TQ for _ in range(CH)]
    gate_last_mm = 0
    op_mm_last = 0
    for tq in range(TQ):
        slot = tq
        for c in range(CH):
            idx = tq * CH + c
            rot = idx % 2
            if idx == 0:
                W("tensor", s_kvbd, 16)
                W("tensor", s_wg, 16)
                W("tensor", Vc, zrow[1])
            if idx >= 2:
                W("tensor", Vc, gs_v[(idx - 2) // CH][(idx - 2) % CH])
            W("tensor", Vc, zrow[tq])
            def c4mm(e, c=c, tq=tq, rot=rot, slot=slot):
                zb_dst = PS[:, 3072 + rot * NQ: 3072 + (rot + 1) * NQ]
                e.matmul(zb_dst, selz[:, c * 128:(c + 1) * 128],
                         ZHI[:, slot * NQ:(slot + 1) * NQ], start=True, stop=False,
                         skip_group_check=True)
                e.matmul(zb_dst, selz[:, c * 128:(c + 1) * 128],
                         ZLO[:, slot * NQ:(slot + 1) * NQ], start=False, stop=True,
                         skip_group_check=True)
                sc_dst = PS[:, rot * NQ:(rot + 1) * NQ]
                e.matmul(sc_dst, KVBD[:, c * 128:(c + 1) * 128],
                         phiq[:, c * T + tq * NQ: c * T + (tq + 1) * NQ],
                         start=True, stop=True, skip_group_check=True)
                g_dst = PS[:, 1024 + rot * NQ: 1024 + (rot + 1) * NQ]
                for kc in range(CH):
                    mm = e.matmul(g_dst, R4[:, kc * 1024 + c * 128: kc * 1024 + (c + 1) * 128],
                                  xbf[:, kc * T + tq * NQ: kc * T + (tq + 1) * NQ],
                                  start=(kc == 0), stop=(kc == CH - 1), skip_group_check=True)
                return mm
            mm_n = St("tensor", c4mm, inc=Tc)
            gate_last_mm = mm_n
            W("scalar", Tc, mm_n)
            if idx >= 2:
                W("scalar", Vc, gs_v[(idx - 2) // CH][(idx - 2) % CH])
            tg_n = St("scalar", lambda e, c=c, rot=rot: e.activation(
                out=tgt[rot], in_=PS[:, 1024 + rot * NQ: 1024 + (rot + 1) * NQ],
                func=AF.Tanh, bias=bgh[:, c:c + 1], scale=0.5), inc=Sc)
            W("vector", Tc, mm_n)
            W("vector", Sc, tg_n)
            if DEBUG_TAPS and idx == 0:
                W("vector", dTap, phiq_tap)
            St("vector", lambda e, rot=rot: e.scalar_tensor_tensor(
                out=etq[rot], in0=tgt[rot], scalar=1.0,
                in1=PS[:, 3072 + rot * NQ: 3072 + (rot + 1) * NQ],
                op0=ALU.add, op1=ALU.mult))
            gs_v[tq][c] = St("vector", lambda e, c=c, tq=tq, rot=rot: e.tensor_tensor(
                out=phiq[:, c * T + tq * NQ: c * T + (tq + 1) * NQ],
                in0=etq[rot], in1=PS[:, rot * NQ:(rot + 1) * NQ], op=ALU.mult), inc=Vc)

    for mc in range(CH):
        if mc == 0:
            W("tensor", Vc, gs_v[TQ - 1][CH - 1])
            W("tensor", s_wo, 16)
        if mc >= 2:
            W("tensor", Vc, oevac[mc - 2][TQ - 1])
        ob_ = (mc % 2) * 4
        def opmm(e, mc=mc, ob_=ob_):
            for kc in range(CH):
                lhs = R3[:, kc * 1024 + mc * 128: kc * 1024 + (mc + 1) * 128]
                for tq in range(TQ):
                    mm = e.matmul(PS[:, (ob_ + tq) * NQ:(ob_ + tq + 1) * NQ], lhs,
                                  phiq[:, kc * T + tq * NQ: kc * T + (tq + 1) * NQ],
                                  start=(kc == 0), stop=(kc == CH - 1), skip_group_check=True)
            return mm
        op_mm_last = St("tensor", opmm, inc=Tc)
        for tq in range(TQ):
            W("vector", Tc, op_mm_last)
            if mc == 0 and tq == 0:
                W("vector", s_x, 16)
            oevac[mc][tq] = St("vector", lambda e, mc=mc, tq=tq, ob_=ob_: e.scalar_tensor_tensor(
                out=stream[:, mc * T + tq * NQ: mc * T + (tq + 1) * NQ],
                in0=PS[:, (ob_ + tq) * NQ:(ob_ + tq + 1) * NQ],
                scalar=bo[:, mc:mc + 1],
                in1=stream[:, mc * T + tq * NQ: mc * T + (tq + 1) * NQ],
                op0=ALU.add, op1=ALU.add), inc=Vc)

    s_cw1 = mkdma(); s_gw1 = mkdma(); s_gw2 = mkdma()
    W("gpsimd", Tc, gate_last_mm)
    St("gpsimd", wload(R4, "cw1", 8), inc=s_cw1, k=16)
    W("gpsimd", Vc, gs_v[TQ - 1][CH - 1])
    St("gpsimd", lambda e: e.dma_start(
        out=gw1sb.rearrange("p (kc m) -> p kc m", kc=8),
        in_=dr["gw1"].rearrange("(kc p) m -> p kc m", p=128)), inc=s_gw1, k=16)
    St("gpsimd", lambda e: e.dma_start(
        out=gw2sb.rearrange("p (kc m) -> p kc m", kc=4),
        in_=dr["gw2"].rearrange("(kc p) m -> p kc m", p=128)), inc=s_gw2, k=16)
    if DEBUG_TAPS:
        W("gpsimd", Vc, oevac[CH - 1][TQ - 1])
        r_tap = St("gpsimd", lambda e: e.dma_start(out=taps["t_r"], in_=stream[:, :]),
                   inc=dTap, k=16)
        gs_tap = St("gpsimd", lambda e: e.dma_start(out=taps["t_gs"], in_=phiq[:, :]),
                    inc=dTap, k=16)

    cb_mm = [0, 0, 0]
    r2add = [0] * CH

    def layer_norm(tag, norm_list):
        ln2 = tag == "ln2"
        ord_ = [0, 4, 1, 5, 2, 6, 3, 7] if ln2 else list(range(CH))
        sq = [0] * CH
        rbfv = [0] * CH
        sums_c = [0] * CH
        for pos, c in enumerate(ord_):
            if not ln2:
                W("scalar", Tc, op_mm_last)
                W("vector", Tc, op_mm_last)
                W("scalar", Vc, oevac[c][TQ - 1])
                if DEBUG_TAPS:
                    W("scalar", dTap, gs_tap)
                    W("vector", dTap, gs_tap)
                rbfv[c] = St("vector", lambda e, c=c: e.tensor_copy(
                    R2[:, c * T:(c + 1) * T], stream[:, c * T:(c + 1) * T]), inc=Vc)
            else:
                W("scalar", ln2_r2sem[c], r2add[c])
                W("gpsimd", ln2_r2sem[c], r2add[c])
                rbfv[c] = St("gpsimd", lambda e, c=c: e.tensor_copy(
                    R2[:, c * T:(c + 1) * T], stream[:, c * T:(c + 1) * T]), inc=Gc,
                    drain=True)
            slot = pos % 4
            if pos >= 4:
                W("scalar", Tc, sums_c[pos - 4])
            sq[c] = St("scalar", lambda e, c=c, slot=slot: e.activation(
                out=r2t[slot], in_=stream[:, c * T:(c + 1) * T], func=AF.Square), inc=Sc)
            W("tensor", Sc, sq[c])
            W("tensor", Gc if ln2 else Vc, rbfv[c])
            def summ(e, c=c, slot=slot, pos=pos):
                for tq in range(TQ):
                    e.matmul(PS[0:4, 0:NQ], tqmask[:, tq * 4:(tq + 1) * 4],
                             R2[:, c * T + tq * NQ: c * T + (tq + 1) * NQ],
                             start=(pos == 0 and tq == 0), stop=(pos == CH - 1 and tq == TQ - 1),
                             skip_group_check=True)
                    mm = e.matmul(PS[0:4, NQ:2 * NQ], tqmask[:, tq * 4:(tq + 1) * 4],
                                  r2t[slot][:, tq * NQ:(tq + 1) * NQ],
                                  start=(pos == 0 and tq == 0), stop=(pos == CH - 1 and tq == TQ - 1),
                                  skip_group_check=True)
                return mm
            sums_c[pos] = St("tensor", summ, inc=Tc, drain=True)
        W("vector", Tc, sums_c[CH - 1])
        St("vector", lambda e: e.tensor_scalar(out=S4[:, :], in0=PS[0:4, 0:NQ],
                                               scalar1=1.0 / D, scalar2=0.0, op0=ALU.mult, op1=ALU.add))
        St("vector", lambda e: e.tensor_scalar(out=Q4[:, :], in0=PS[0:4, NQ:2 * NQ],
                                               scalar1=1.0 / D, scalar2=0.0, op0=ALU.mult, op1=ALU.add))
        St("vector", lambda e: e.tensor_tensor(out=TH4[:, :], in0=S4[:, :], in1=S4[:, :], op=ALU.mult))
        v_var = St("vector", lambda e: e.tensor_tensor(out=Q4[:, :], in0=Q4[:, :], in1=TH4[:, :],
                                                       op=ALU.subtract), inc=Vc)
        W("scalar", Vc, v_var)
        St("scalar", lambda e: e.activation(out=Q4[:, :], in_=Q4[:, :], func=AF.Ln, bias=epsc[0:4, :]))
        s_a = St("scalar", lambda e: e.activation(out=Q4[:, :], in_=Q4[:, :], func=AF.Exp,
                                                  scale=-0.5), inc=Sc)
        W("vector", Sc, s_a)
        St("vector", lambda e: e.scalar_tensor_tensor(out=S4[:, :], in0=S4[:, :], scalar=-1.0,
                                                      in1=Q4[:, :], op0=ALU.mult, op1=ALU.mult))
        St("vector", lambda e: e.tensor_copy(a4hi, Q4[:, :]))
        St("vector", lambda e: e.tensor_tensor(out=a4lo, in0=Q4[:, :], in1=a4hi, op=ALU.subtract))
        St("vector", lambda e: e.tensor_copy(b4hi, S4[:, :]))
        v_ab = St("vector", lambda e: e.tensor_tensor(out=b4lo, in0=S4[:, :], in1=b4hi,
                                                      op=ALU.subtract), inc=Vc)
        last_norm = 0
        bc_mm = [0] * TQ
        aps2 = [PS[:, tq * NQ:(tq + 1) * NQ] for tq in range(TQ)]
        bps2 = [PS[:, 2048 + tq * NQ: 2048 + (tq + 1) * NQ] for tq in range(TQ)]
        for tq in range(TQ):
            if ln2:
                a_ps = aps2[tq]
                b_ps = bps2[tq]
            else:
                rot = tq % 2
                a_ps = PS[:, 2048 + rot * NQ: 2048 + (rot + 1) * NQ]
                b_ps = PS[:, 3072 + rot * NQ: 3072 + (rot + 1) * NQ]
            W("tensor", Vc, v_ab)
            if not ln2 and tq >= 2:
                W("tensor", Vc, norm_list[(tq - 2) * CH + CH - 1])
            def bcmm(e, a_ps=a_ps, b_ps=b_ps, tq=tq):
                e.matmul(a_ps, jsel[:, tq * 128:(tq + 1) * 128], a4hi, start=True, stop=False,
                         skip_group_check=True)
                e.matmul(a_ps, jsel[:, tq * 128:(tq + 1) * 128], a4lo, start=False, stop=True,
                         skip_group_check=True)
                e.matmul(b_ps, jsel[:, tq * 128:(tq + 1) * 128], b4hi, start=True, stop=False,
                         skip_group_check=True)
                mm = e.matmul(b_ps, jsel[:, tq * 128:(tq + 1) * 128], b4lo, start=False, stop=True,
                              skip_group_check=True)
                return mm
            bc_mm[tq] = St("tensor", bcmm, inc=Tc, drain=True)
            if ln2:
                continue
            for c in range(CH):
                W("vector", Tc, bc_mm[tq])
                if not ln2 and DEBUG_TAPS and tq == 0 and c == 0:
                    W("vector", dTap, r_tap)
                St("vector", lambda e, c=c, tq=tq, a_ps=a_ps: e.tensor_tensor(
                    out=ntmp, in0=stream[:, c * T + tq * NQ: c * T + (tq + 1) * NQ],
                    in1=a_ps, op=ALU.mult))
                last_norm = St("vector", lambda e, c=c, tq=tq, b_ps=b_ps: e.tensor_tensor(
                    out=stream[:, c * T + tq * NQ: c * T + (tq + 1) * NQ],
                    in0=ntmp, in1=b_ps,
                    op=ALU.add), inc=Vc)
                norm_list[tq * CH + c] = last_norm
        if ln2:
            W("vector", Tc, bc_mm[TQ - 1])
            for c in range(CH):
                for tq in range(TQ):
                    St("vector", lambda e, c=c, tq=tq: e.tensor_tensor(
                        out=ntmp, in0=stream[:, c * T + tq * NQ: c * T + (tq + 1) * NQ],
                        in1=aps2[tq], op=ALU.mult))
                    last_norm = St("vector", lambda e, c=c, tq=tq: e.tensor_tensor(
                        out=stream[:, c * T + tq * NQ: c * T + (tq + 1) * NQ],
                        in0=ntmp, in1=bps2[tq],
                        op=ALU.add), inc=Vc)
                    norm_list[tq * CH + c] = last_norm
        return last_norm

    norm1 = [0] * (TQ * CH)
    ln1_last = layer_norm("ln1", norm1)
    if DEBUG_TAPS:
        W("gpsimd", Vc, ln1_last)
        x1_tap = St("gpsimd", lambda e: e.dma_start(out=taps["t_x1"], in_=stream[:, :]),
                    inc=dTap, k=16)

    prev_outbf_tq0 = [[0] * CH for _ in range(TQ)]
    prev_outbf_sem0 = [Sc, Gc, Gc, Gc]
    for tq in range(TQ):
        eng = "scalar" if tq == 0 else "gpsimd"
        W(eng, Vc, norm1[tq * CH + CH - 1])
        for c in range(CH):
            if tq == 0:
                prev_outbf_tq0[tq][c] = St("scalar", lambda e, c=c, tq=tq: e.activation(
                    out=outbf[:, c * T + tq * NQ: c * T + (tq + 1) * NQ],
                    in_=stream[:, c * T + tq * NQ: c * T + (tq + 1) * NQ],
                    func=AF.Copy), inc=Sc)
            else:
                prev_outbf_tq0[tq][c] = St("gpsimd", lambda e, c=c, tq=tq: e.tensor_copy(
                    outbf[:, c * T + tq * NQ: c * T + (tq + 1) * NQ],
                    stream[:, c * T + tq * NQ: c * T + (tq + 1) * NQ]), inc=Gc,
                    drain=(c == CH - 1))
    outbf_c = [prev_outbf_tq0[3][c] for c in range(CH)]

    for c in range(CH):
        W("vector", Vc, norm1[3 * CH + c])
        St("vector", lambda e, c=c: e.reduce_sum(out=gcp[:, c:c + 1],
                                                 in_=stream[:, c * T:(c + 1) * T], axis=AXX))
    v_gcp = St("vector", lambda e: e.engine_nop(), inc=Vc)

    s_park = mkdma()
    W("sync", Vc, ln1_last)
    St("sync", lambda e: e.dma_start(out=x1d, in_=stream[:, :]), inc=s_park, k=16)

    s_gcin = mkdma(); s_gcret = mkdma()
    W("gpsimd", Vc, v_gcp)
    St("gpsimd", lambda e: e.dma_start(out=gc_in, in_=gcp), inc=s_gcin, k=16)
    W("gpsimd", s_gcin, 16)
    St("gpsimd", lambda e: e.collective_compute(
        "AllReduce", ALU.add, replica_groups=[[0, 1, 2, 3], [4, 5, 6, 7]],
        ins=[gc_in.opt()], outs=[gc_out.opt()]), inc=CC2)
    W("sync", CC2, 1)
    St("sync", lambda e: e.dma_start(out=gcp, in_=gc_out), inc=s_gcret, k=16)

    twist_v = [[0] * 4 for _ in range(3)]
    th_done = [0] * 3
    cp_done = [0] * 3
    prev_outbf = outbf_c[:]
    c0_done = 0

    for i in range(3):
        cn1_mm_last = 0
        gelu1w = [[0] * CH for _ in range(TQ)]
        for tq in range(TQ):
            for mc in range(CH):
                idx = tq * CH + mc
                rot4 = idx % 4
                if idx == 0:
                    W("tensor", s_cw1, 16)
                    if i == 0:
                        W("tensor", s_cw2, 16)
                        W("tensor", s_cw3, 16)
                if mc == 0 and tq > 0:
                    if i > 0:
                        W("tensor", prev_outbf_sem[tq], prev_outbf_tq[tq][CH - 1])
                    else:
                        W("tensor", prev_outbf_sem0[tq], prev_outbf_tq0[tq][CH - 1])
                if idx >= 4:
                    W("tensor", Sc, gelu1w[(idx - 4) // CH][(idx - 4) % CH])
                if idx == 0:
                    pv = prev_outbf_tq if i > 0 else prev_outbf_tq0
                    for kc in range(CH):
                        W("tensor", Sc, pv[0][kc])
                        def cn1mm0(e, kc=kc):
                            return e.matmul(PS[:, 0:NQ],
                                            R4[:, kc * 1024: kc * 1024 + 128],
                                            outbf[:, kc * T: kc * T + NQ],
                                            start=(kc == 0), stop=(kc == CH - 1),
                                            skip_group_check=True)
                        cn1_mm_last = St("tensor", cn1mm0,
                                         inc=(Tc if kc == CH - 1 else None))
                    W("scalar", Tc, cn1_mm_last)
                    gelu1w[tq][mc] = St("scalar", lambda e: e.activation(
                        out=R2[:, 0:NQ], in_=PS[:, 0:NQ], func=AF.Gelu,
                        bias=cb1[:, 0:1]), inc=Sc)
                    continue
                def cn1mm(e, mc=mc, tq=tq, rot4=rot4):
                    dst = PS[:, rot4 * NQ:(rot4 + 1) * NQ]
                    for kc in range(CH):
                        mm = e.matmul(dst, R4[:, kc * 1024 + mc * 128: kc * 1024 + (mc + 1) * 128],
                                      outbf[:, kc * T + tq * NQ: kc * T + (tq + 1) * NQ],
                                      start=(kc == 0), stop=(kc == CH - 1), skip_group_check=True)
                    return mm
                cn1_mm_last = St("tensor", cn1mm, inc=Tc)
                W("scalar", Tc, cn1_mm_last)
                if i > 0 and idx == 0 and DEBUG_TAPS:
                    W("scalar", dTap, h1_tap)
                gelu1w[tq][mc] = St("scalar", lambda e, mc=mc, tq=tq, rot4=rot4: e.activation(
                    out=R2[:, mc * T + tq * NQ: mc * T + (tq + 1) * NQ],
                    in_=PS[:, rot4 * NQ:(rot4 + 1) * NQ], func=AF.Gelu,
                    bias=cb1[:, mc:mc + 1]), inc=Sc)
        gelu1 = [None, gelu1w[TQ - 1]]
        if i == 2:
            cn1_fence = St("tensor", lambda e: e.nop(), inc=Tc, drain=True)
        if i == 0 and DEBUG_TAPS:
            W("gpsimd", Sc, gelu1[TQ - 1][CH - 1])
            h1_tap = St("gpsimd", lambda e: e.dma_start(out=taps["t_h1"], in_=R2[:, :]),
                        inc=dTap, k=16)
        gelu2 = [0] * 4
        for mc in range(4):
            W("tensor", Sc, gelu1[1][CH - 1])
            if mc >= 2:
                W("tensor", Sc, gelu2[mc - 2])
            def cn2mm(e, mc=mc):
                base = (mc % 2) * 4
                for kc in range(CH):
                    lhs = CW2[:, kc * 512 + mc * 128: kc * 512 + (mc + 1) * 128]
                    for tq in range(TQ):
                        mm = e.matmul(PS[:, (base + tq) * NQ:(base + tq + 1) * NQ], lhs,
                                      R2[:, kc * T + tq * NQ: kc * T + (tq + 1) * NQ],
                                      start=(kc == 0), stop=(kc == CH - 1), skip_group_check=True)
                return mm
            mmn = St("tensor", cn2mm, inc=Tc)
            cn2_mm_last = mmn
            base = (mc % 2) * 4
            last = 0
            for tq in range(TQ):
                W("scalar", Tc, mmn)
                last = St("scalar", lambda e, mc=mc, tq=tq, base=base: e.activation(
                    out=R3[:, mc * T + tq * NQ: mc * T + (tq + 1) * NQ],
                    in_=PS[:, (base + tq) * NQ:(base + tq + 1) * NQ], func=AF.Gelu,
                    bias=cb2[:, mc:mc + 1]), inc=Sc)
            gelu2[mc] = last
        if i == 2:
            cn2_fence = St("tensor", lambda e: e.nop(), inc=Tc, drain=True)
        if i == 0:
            W("vector", s_gcret, 16)
            v_gcbf = St("vector", lambda e: e.tensor_scalar(
                out=gcbf, in0=gcp, scalar1=1.0 / 8192.0, scalar2=0.0, op0=ALU.mult,
                op1=ALU.add), inc=Vc)
            W("tensor", s_gw1, 16)
            W("tensor", s_gw2, 16)
            W("tensor", s_gw3, 16)
            W("tensor", Vc, v_gcbf)
            W("tensor", Sc, gelu2[3])
            def gf1(e):
                for mc in range(4):
                    for kc in range(CH):
                        mm = e.matmul(PS[:, 2048 + mc: 2048 + mc + 1],
                                      gw1sb[:, kc * 512 + mc * 128: kc * 512 + (mc + 1) * 128],
                                      gcbf[:, kc:kc + 1],
                                      start=(mc == 0 and kc == 0), stop=(mc == 3 and kc == CH - 1),
                                      skip_group_check=True)
                return mm
            t_gf1 = St("tensor", gf1, inc=Tc, drain=True)
            W("scalar", Tc, t_gf1)
            s_gh1 = 0
            for mc in range(4):
                s_gh1 = St("scalar", lambda e, mc=mc: e.activation(
                    out=gh1[:, mc:mc + 1], in_=PS[:, 2048 + mc:2048 + mc + 1],
                    func=AF.Gelu, bias=gb1[:, mc:mc + 1]), inc=Sc)
            W("tensor", Sc, s_gh1)
            def gf2(e):
                for mc in range(2):
                    for kc in range(4):
                        mm = e.matmul(PS[:, 2056 + mc: 2056 + mc + 1],
                                      gw2sb[:, kc * 256 + mc * 128: kc * 256 + (mc + 1) * 128],
                                      gh1[:, kc:kc + 1],
                                      start=(mc == 0 and kc == 0), stop=(mc == 1 and kc == 3),
                                      skip_group_check=True)
                return mm
            t_gf2 = St("tensor", gf2, inc=Tc, drain=True)
            W("scalar", Tc, t_gf2)
            s_gh2 = 0
            for mc in range(2):
                s_gh2 = St("scalar", lambda e, mc=mc: e.activation(
                    out=gh2[:, mc:mc + 1], in_=PS[:, 2056 + mc:2056 + mc + 1],
                    func=AF.Gelu, bias=gb2[:, mc:mc + 1]), inc=Sc)
            W("tensor", Sc, s_gh2)
            def gf3(e):
                for kc in range(2):
                    mm = e.matmul(PS[0:4, 2064:2065], GW3[:, kc * 4:(kc + 1) * 4],
                                  gh2[:, kc:kc + 1], start=(kc == 0), stop=(kc == 1),
                                  skip_group_check=True)
                return mm
            t_gf3 = St("tensor", gf3, inc=Tc, drain=True)
            W("scalar", Tc, t_gf3)
            St("scalar", lambda e: e.activation(
                out=g4sb, in_=PS[0:4, 2064:2065], func=AF.Tanh,
                bias=gb3h4, scale=0.5))
            c0_done = St("scalar", lambda e: e.mul(out=g7col, in_=g4sb, mul=0.7), inc=Sc)
        gelu3 = [0] * 2
        for mc in range(2):
            W("tensor", Sc, gelu2[3])
            if i == 0:
                W("tensor", Sc, c0_done)
            def cn3mm(e, mc=mc):
                base = (mc % 2) * 4
                for kc in range(4):
                    lhs = CW3[:, kc * 256 + mc * 128: kc * 256 + (mc + 1) * 128]
                    for tq in range(TQ):
                        mm = e.matmul(PS[:, (base + tq) * NQ:(base + tq + 1) * NQ], lhs,
                                      R3[:, kc * T + tq * NQ: kc * T + (tq + 1) * NQ],
                                      start=(kc == 0), stop=(kc == 3), skip_group_check=True)
                return mm
            mmn = St("tensor", cn3mm, inc=Tc)
            cn3_mm_last = mmn
            base = (mc % 2) * 4
            last = 0
            for tq in range(TQ):
                W("scalar", Tc, mmn)
                if tq == 0:
                    W("scalar", Tc, cn1_mm_last)
                last = St("scalar", lambda e, mc=mc, tq=tq, base=base: e.activation(
                    out=h3[:, mc * T + tq * NQ: mc * T + (tq + 1) * NQ],
                    in_=PS[:, (base + tq) * NQ:(base + tq + 1) * NQ], func=AF.Gelu,
                    bias=cb3[:, mc:mc + 1]), inc=Sc)
            gelu3[mc] = last
        if i == 2:
            cn3_fence = St("tensor", lambda e: e.nop(), inc=Tc, drain=True)
        W("tensor", Sc, gelu3[1])
        def cn4mm(e):
            for tq in range(TQ):
                for kc in range(2):
                    lhs = (cw4m0 if kc == 0 else cw4m1)[:, tq * 4:(tq + 1) * 4]
                    mm = e.matmul(PS[0:4, 2560:2560 + NQ], lhs,
                                  h3[:, kc * T + tq * NQ: kc * T + (tq + 1) * NQ],
                                  start=(tq == 0 and kc == 0), stop=(tq == TQ - 1 and kc == 1),
                                  skip_group_check=True)
            return mm
        t_cn4 = St("tensor", cn4mm, inc=Tc, drain=True)
        W("scalar", Tc, t_cn4)
        if i > 0 and DEBUG_TAPS:
            W("scalar", dTap, cp_tap)
        th_done[i] = St("scalar", lambda e: e.activation(
            out=TH4[:, :], in_=PS[0:4, 2560:2560 + NQ], func=AF.Tanh,
            bias=b4h4, scale=0.5), inc=Sc)
        if i == 0 and DEBUG_TAPS:
            W("gpsimd", Sc, th_done[0])
            th_tap = St("gpsimd", lambda g: g.dma_start(out=taps["t_th"], in_=TH4[:, :]),
                        inc=dTap, k=16)
            W("gpsimd", Sc, c0_done)
            St("gpsimd", lambda g: g.dma_start(out=taps["t_tiny"], in_=TINY[0:4, 40:48]),
               inc=dTap, k=16)
        W("vector", Sc, th_done[i])
        if i == 0:
            W("vector", Sc, c0_done)
            if DEBUG_TAPS:
                W("vector", dTap, th_tap)
        St("vector", lambda e: e.tensor_scalar(out=TH4[:, :], in0=TH4[:, :],
                                               scalar1=0.3, scalar2=0.0,
                                               op0=ALU.mult, op1=ALU.add))
        St("vector", lambda e: e.tensor_scalar(out=TH4[:, :], in0=TH4[:, :],
                                               scalar1=g7col, scalar2=0.0,
                                               op0=ALU.add, op1=ALU.add))
        St("vector", lambda e: e.tensor_scalar(out=TH4[:, :], in0=TH4[:, :],
                                               scalar1=arc, scalar2=0.1,
                                               op0=ALU.mult, op1=ALU.add))
        St("vector", lambda e: e.tensor_copy(CPHI[:, :], TH4[:, :]))
        cp_done[i] = St("vector", lambda e: e.tensor_tensor(
            out=CPLO[:, :], in0=TH4[:, :], in1=CPHI[:, :], op=ALU.subtract), inc=Vc)
        if i == 0 and DEBUG_TAPS:
            W("gpsimd", Vc, cp_done[0])
            cp_tap = St("gpsimd", lambda g: g.dma_start(out=taps["t_cp"], in_=TH4[:, :]),
                        inc=dTap, k=16)
        W("tensor", Vc, cp_done[i])
        def cbmm(e):
            for tq in range(TQ):
                dst = PS[:, 2048 + tq * NQ: 2048 + (tq + 1) * NQ]
                e.matmul(dst, jsel[:, tq * 128:(tq + 1) * 128], CPHI[:, :], start=True, stop=False,
                         skip_group_check=True)
                mm = e.matmul(dst, jsel[:, tq * 128:(tq + 1) * 128], CPLO[:, :], start=False,
                              stop=True, skip_group_check=True)
            return mm
        cb_mm[i] = St("tensor", cbmm, inc=Tc, drain=True)
        W("vector", Tc, cb_mm[i])
        if i == 0:
            W("vector", s_park, 16)
            if DEBUG_TAPS:
                W("vector", dTap, x1_tap)
        if i < 2:
            twq = [0] * TQ
            for tq in range(TQ):
                cbq = PS[:, 2048 + tq * NQ: 2048 + (tq + 1) * NQ]
                for p, (ca, sA, sB) in enumerate(((0, 1.0, -1.0), (1, 1.0, -1.0),
                                                  (2, -1.0, 1.0), (3, -1.0, 1.0))):
                    St("vector", lambda e, ca=ca, sA=sA, tq=tq, p=p, cbq=cbq: e.scalar_tensor_tensor(
                        out=twslot[2 * p],
                        in0=stream[:, (ca + 4) * T + tq * NQ: (ca + 4) * T + (tq + 1) * NQ],
                        scalar=sA, in1=cbq, op0=ALU.mult, op1=ALU.mult))
                    St("vector", lambda e, ca=ca, sB=sB, tq=tq, p=p, cbq=cbq: e.scalar_tensor_tensor(
                        out=twslot[2 * p + 1],
                        in0=stream[:, ca * T + tq * NQ: ca * T + (tq + 1) * NQ],
                        scalar=sB, in1=cbq, op0=ALU.mult, op1=ALU.mult))
                for p, (ca, _, _) in enumerate(((0, 0, 0), (1, 0, 0), (2, 0, 0), (3, 0, 0))):
                    St("vector", lambda e, ca=ca, tq=tq, p=p: e.tensor_tensor(
                        out=stream[:, ca * T + tq * NQ: ca * T + (tq + 1) * NQ],
                        in0=stream[:, ca * T + tq * NQ: ca * T + (tq + 1) * NQ],
                        in1=twslot[2 * p], op=ALU.add))
                    last = St("vector", lambda e, ca=ca, tq=tq, p=p: e.tensor_tensor(
                        out=stream[:, (ca + 4) * T + tq * NQ: (ca + 4) * T + (tq + 1) * NQ],
                        in0=stream[:, (ca + 4) * T + tq * NQ: (ca + 4) * T + (tq + 1) * NQ],
                        in1=twslot[2 * p + 1], op=ALU.add), inc=(Vc if p == 3 else None))
                twq[tq] = last
            twist_v[i] = [twq[3]] * 4
            twist_tq = twq
        else:
            x1R1 = R1[:, 0:8192].bitcast(F32)
            x1R3 = R3[:, 0:8192].bitcast(F32)
            x1R4 = R4[:, 0:8192].bitcast(F32)
            x1C6 = CW2[:, 0:4096].bitcast(F32)
            s_xr4 = mkdma(); s_xc6 = mkdma(); s_xr3 = mkdma(); s_x1pf = mkdma()
            W("sync", Tc, cn1_fence)
            St("sync", lambda e: e.dma_start(out=x1R4, in_=x1d[:, 4 * T:6 * T]), inc=s_xr4, k=16)
            W("sync", Tc, cn2_fence)
            St("sync", lambda e: e.dma_start(out=x1C6, in_=x1d[:, 6 * T:7 * T]), inc=s_xc6, k=16)
            W("sync", Tc, cn3_fence)
            St("sync", lambda e: e.dma_start(out=x1R3, in_=x1d[:, 2 * T:4 * T]), inc=s_xr3, k=16)
            W("sync", Tc, t_cn4)
            St("sync", lambda e: e.dma_start(out=x1R1, in_=x1d[:, 0:2 * T]), inc=s_x1pf, k=16)
            x1c = {0: x1R1[:, 0:T], 1: x1R1[:, T:2 * T],
                   2: x1R3[:, 0:T], 3: x1R3[:, T:2 * T],
                   4: x1R4[:, 0:T], 5: x1R4[:, T:2 * T], 6: x1C6[:, 0:T]}
            csem = {0: s_x1pf, 1: s_x1pf, 2: s_xr3, 3: s_xr3,
                    4: s_xr4, 5: s_xr4, 6: s_xc6}
            twp = [0] * 4
            r2add_sem = [Vc] * CH
            for p, (sA, sB) in enumerate(((1.0, -1.0), (1.0, -1.0),
                                          (-1.0, 1.0), (-1.0, 1.0))):
                for tq in range(TQ):
                    cbq = PS[:, 2048 + tq * NQ: 2048 + (tq + 1) * NQ]
                    St("vector", lambda e, p=p, tq=tq, sA=sA, cbq=cbq: e.scalar_tensor_tensor(
                        out=twslot[2 * tq],
                        in0=stream[:, (p + 4) * T + tq * NQ: (p + 4) * T + (tq + 1) * NQ],
                        scalar=sA, in1=cbq, op0=ALU.mult, op1=ALU.mult))
                    St("vector", lambda e, p=p, tq=tq, sB=sB, cbq=cbq: e.scalar_tensor_tensor(
                        out=twslot[2 * tq + 1],
                        in0=stream[:, p * T + tq * NQ: p * T + (tq + 1) * NQ],
                        scalar=sB, in1=cbq, op0=ALU.mult, op1=ALU.mult))
                for tq in range(TQ):
                    St("vector", lambda e, p=p, tq=tq: e.tensor_tensor(
                        out=stream[:, p * T + tq * NQ: p * T + (tq + 1) * NQ],
                        in0=stream[:, p * T + tq * NQ: p * T + (tq + 1) * NQ],
                        in1=twslot[2 * tq], op=ALU.add))
                    twp[p] = St("vector", lambda e, p=p, tq=tq: e.tensor_tensor(
                        out=stream[:, (p + 4) * T + tq * NQ: (p + 4) * T + (tq + 1) * NQ],
                        in0=stream[:, (p + 4) * T + tq * NQ: (p + 4) * T + (tq + 1) * NQ],
                        in1=twslot[2 * tq + 1], op=ALU.add), inc=Vc)
                W("vector", csem[p], 16)
                r2add[p] = St("vector", lambda e, c=p: e.tensor_tensor(
                    out=stream[:, c * T:(c + 1) * T], in0=stream[:, c * T:(c + 1) * T],
                    in1=x1c[c], op=ALU.add), inc=Vc)
                r2add_sem[p] = Vc
                if p + 4 < 7:
                    W("gpsimd", Vc, twp[p])
                    W("gpsimd", csem[p + 4], 16)
                    r2add[p + 4] = St("gpsimd", lambda e, c=p + 4: e.tensor_tensor(
                        out=stream[:, c * T:(c + 1) * T], in0=stream[:, c * T:(c + 1) * T],
                        in1=x1c[c], op=ALU.add), inc=Gc, drain=True)
                    r2add_sem[p + 4] = Gc
            rs7 = [mkdma(), mkdma()]
            W("sync", Vc, twp[3])
            St("sync", lambda e: e.dma_start(out=x1t[0], in_=x1d[:, 7 * T: 7 * T + 1024]),
               inc=rs7[0], k=16)
            St("sync", lambda e: e.dma_start(out=x1t[1], in_=x1d[:, 7 * T + 1024: 8 * T]),
               inc=rs7[1], k=16)
            for h in range(2):
                W("vector", rs7[h], 16)
                r2add[7] = St("vector", lambda e, h=h: e.tensor_tensor(
                    out=stream[:, 7 * T + h * 1024: 7 * T + (h + 1) * 1024],
                    in0=stream[:, 7 * T + h * 1024: 7 * T + (h + 1) * 1024],
                    in1=x1t[h], op=ALU.add), inc=Vc)
            ln2_r2sem = r2add_sem
        if i == 0 and DEBUG_TAPS:
            W("gpsimd", Vc, twist_tq[3])
            c1_tap = St("gpsimd", lambda g: g.dma_start(out=taps["t_c1"], in_=stream[:, :]),
                        inc=dTap, k=16)
        if i < 2:
            nbq_ = [[0] * CH for _ in range(TQ)]
            nbq_sem = [Sc, Gc, Gc, Gc]
            for tq in range(TQ):
                eng = "scalar" if tq == 0 else "gpsimd"
                W(eng, Vc, twist_tq[tq])
                if i == 0 and DEBUG_TAPS and tq == 0:
                    W(eng, dTap, c1_tap)
                for c in range(CH):
                    if tq == 0:
                        nbq_[tq][c] = St("scalar", lambda e, c=c, tq=tq: e.activation(
                            out=outbf[:, c * T + tq * NQ: c * T + (tq + 1) * NQ],
                            in_=stream[:, c * T + tq * NQ: c * T + (tq + 1) * NQ],
                            func=AF.Copy), inc=Sc)
                    else:
                        nbq_[tq][c] = St("gpsimd", lambda e, c=c, tq=tq: e.tensor_copy(
                            outbf[:, c * T + tq * NQ: c * T + (tq + 1) * NQ],
                            stream[:, c * T + tq * NQ: c * T + (tq + 1) * NQ]), inc=Gc,
                            drain=(c == CH - 1))
            prev_outbf_tq = nbq_
            prev_outbf_sem = nbq_sem
            prev_outbf = [nbq_[3][c] for c in range(CH)]

    norm2 = [0] * (TQ * CH)
    layer_norm("ln2", norm2)

    ysem = [mkdma() for _ in range(CH)]
    for c in range(CH):
        W("sync", Vc, norm2[3 * CH + c])
        St("sync", lambda e, c=c: e.dma_start(out=y[c], in_=stream[:, c * T:(c + 1) * T]),
           inc=ysem[c], k=16)
    for c in range(CH):
        W("gpsimd", ysem[c], 16)
    St("gpsimd", lambda e: e.nop())

    class _SimSync:
        _SKIP = {"wait_ge", "dma_start", "dma_start_transpose", "then_inc", "drain", "nop"}

        def __init__(self, eng, sem):
            self._e = eng
            self._s = sem
            self._n = 0

        def __getattr__(self, name):
            a = getattr(self._e, name)
            if name.startswith("_") or name in self._SKIP or not callable(a):
                return a
            def call(*args, **kw):
                if self._n:
                    self._e.wait_ge(self._s, self._n)
                r = a(*args, **kw)
                try:
                    r.then_inc(self._s, 1)
                    self._n += 1
                except Exception:
                    pass
                return r
            return call

    if SIMSYNC:
        ss = {e: es.enter_context(nc.semaphore(f"ss_{e}")) for e in ("t", "v", "s")}

    with nc.Block() as block:
        @block.tensor
        def _(t):
            if SIMSYNC:
                t = _SimSync(t, ss["t"])
            for f in kb.prog["tensor"]:
                f(t)

        @block.vector
        def _(v):
            if SIMSYNC:
                v = _SimSync(v, ss["v"])
            for f in kb.prog["vector"]:
                f(v)

        @block.scalar
        def _(s):
            if SIMSYNC:
                s = _SimSync(s, ss["s"])
            for f in kb.prog["scalar"]:
                f(s)

        @block.gpsimd
        def _(g):
            for f in kb.prog["gpsimd"]:
                f(g)

        @block.sync
        def _(s):
            for f in kb.prog["sync"]:
                f(s)

    es.close()
    return nc


_NC_CACHE = {}


def _get_nc():
    key = ("main", DEBUG_TAPS)
    if key not in _NC_CACHE:
        _NC_CACHE[key] = build()
    return _NC_CACHE[key]


def _host_packs(inp):
    bp = np.zeros((128, 64), np.float32)
    fm = lambda b: np.ascontiguousarray(np.asarray(b).reshape(-1, 128).T)
    bp[:, 0:8] = fm(inp["L0_attn_wq_b"])
    bp[:, 8:16] = fm(inp["L0_attn_wg_b"])
    bp[:, 16:24] = fm(inp["L0_attn_out_b"])
    bp[:, 24:32] = fm(inp["L0_ffn_cn_b1"])
    bp[:, 32:36] = fm(inp["L0_ffn_cn_b2"])
    bp[:, 36:38] = fm(inp["L0_ffn_cn_b3"])
    bp[:, 38:42] = fm(inp["L0_ffn_gc_b1"])
    bp[:, 42:44] = fm(inp["L0_ffn_gc_b2"])
    bp[0:4, 44] = np.asarray(inp["L0_ffn_cn_b4"])[0]
    bp[0:4, 45] = np.asarray(inp["L0_ffn_gc_b3"])[0]
    bp[0:4, 46] = np.asarray(inp["L0_ffn_adaptive_range"])[0]
    bp[:, 47] = 1e-5
    rp = np.zeros((64, 1024), np.float32)
    rp[0, :] = np.asarray(inp["L0_attn_wk_b"])
    rp[32, :] = np.asarray(inp["L0_attn_wv_b"])
    cp = np.zeros((128, 1824), np.float32)
    for tq in range(4):
        cp[:, tq * 4 + tq] = 0.0
    for tq in range(4):
        for j in range(4):
            if j == tq:
                cp[:, tq * 4 + j] = 1.0
    for c in range(8):
        cp[2 * c, 16 + c * 128: 16 + c * 128 + 64] = 1.0
        cp[2 * c + 1, 16 + c * 128 + 64: 16 + (c + 1) * 128] = 1.0
    for tq in range(4):
        cp[tq, 1040 + tq * 128: 1040 + (tq + 1) * 128] = 1.0
    cp[0:2, 1552:1680] = 1.0
    cp[32, 1552:1680] = 1.0
    cp[:, 1680] = 1.0
    for c in range(8):
        cp[0:64, 1696 + c * 16 + 2 * c] = 1.0
        cp[64:128, 1696 + c * 16 + 2 * c + 1] = 1.0
    gwp = np.zeros((128, 8), np.float32)
    g3 = np.asarray(inp["L0_ffn_gc_w3"]).reshape(2, 128)
    for kc in range(2):
        for j in range(4):
            gwp[:, kc * 4 + j] = g3[kc]
    return bp, rp, cp.astype(ml_dtypes.bfloat16), gwp


def kernel(**inp):
    x = np.asarray(inp["L0_attn_x"])
    bp, rp, cp, gwp = _host_packs(inp)
    base = dict(
        wq=np.asarray(inp["L0_attn_wq_w"]), wk=np.asarray(inp["L0_attn_wk_w"]),
        wv=np.asarray(inp["L0_attn_wv_w"]), wg=np.asarray(inp["L0_attn_wg_w"]),
        wo=np.asarray(inp["L0_attn_out_w"]), cw1=np.asarray(inp["L0_ffn_cn_w1"]),
        cw2=np.asarray(inp["L0_ffn_cn_w2"]), cw3=np.asarray(inp["L0_ffn_cn_w3"]),
        cw4=np.asarray(inp["L0_ffn_cn_w4"]), gw1=np.asarray(inp["L0_ffn_gc_w1"]),
        gw2=np.asarray(inp["L0_ffn_gc_w2"]), gwpack=gwp,
        biaspack=bp, rowpack=rp, cpack=cp,
    )
    in_maps = []
    for core in range(8):
        b, s = core // 4, core % 4
        m = dict(base)
        m["x"] = np.ascontiguousarray(x[b, s * T:(s + 1) * T, :].T)
        in_maps.append(m)
    nc = _get_nc()
    res = run_bass_kernel_spmd(nc, in_maps, core_ids=list(range(8)))
    out = np.empty((2, 8192, D), np.float32)
    for core in range(8):
        b, s = core // 4, core % 4
        yc = np.asarray(res.results[core]["y"])
        out[b, s * T:(s + 1) * T, :] = yc.reshape(D, T).T
    return out


# version 24
# speedup vs baseline: 1.0104x; 1.0097x over previous
import os
import numpy as np
import ml_dtypes
import concourse.bass as bass
import concourse.mybir as mybir
from concourse.bass_utils import run_bass_kernel_spmd

F32 = mybir.dt.float32
BF16 = mybir.dt.bfloat16
AF = mybir.ActivationFunctionType
ALU = mybir.AluOpType
AXX = mybir.AxisListType.X

D = 1024
T = 2048
CH = 8
NT = 16
NQ = 512
TQ = 4

DEBUG_TAPS = False


class Ctr:
    __slots__ = ("h", "n")

    def __init__(self, h):
        self.h = h
        self.n = 0


class KB:
    ENGS = ("tensor", "vector", "scalar", "gpsimd", "sync")

    def __init__(self, simsync=False):
        self.prog = {e: [] for e in self.ENGS}
        self.simsync = simsync

    def emit(self, eng, fn):
        self.prog[eng].append(fn)

    def wait(self, eng, ctr, val):
        assert 0 <= val <= ctr.n, (eng, val, ctr.n)
        if val > 0:
            self.emit(eng, lambda e, h=ctr.h, v=val: e.wait_ge(h, v))

    def step(self, eng, fn, inc=None, k=1, drain=False):
        if inc is None:
            self.emit(eng, fn)
            return None
        inc.n += k
        if drain or (self.simsync and eng in ("tensor", "vector", "scalar")):
            def run(e, f=fn, h=inc.h, kk=k):
                f(e)
                e.drain().then_inc(h, kk)
            self.emit(eng, run)
        else:
            self.emit(eng, lambda e, f=fn, h=inc.h, kk=k: f(e).then_inc(h, kk))
        return inc.n


def build():
    nc = bass.Bass("TRN2", target_bir_lowering=False, dynamic_dma_scratch_size=12288)

    dr = {}
    def din(name, shape):
        dr[name] = nc.dram_tensor(name, list(shape), F32, kind="ExternalInput").ap()
    din("x", (D, T))
    for w in ("wq", "wk", "wv", "wg", "wo", "cw1"):
        din(w, (D, D))
    din("cw2", (D, 512)); din("cw3", (512, 256)); din("cw4", (256, 1))
    din("gw1", (D, 512)); din("gw2", (512, 256))
    din("gwpack", (128, 8))
    din("biaspack", (128, 64))
    din("rowpack", (64, D))
    dr["cpack"] = nc.dram_tensor("cpack", [128, 1824], BF16, kind="ExternalInput").ap()
    y = nc.dram_tensor("y", [CH, 128, T], F32, kind="ExternalOutput").ap()
    x1d = nc.dram_tensor("x1d", [128, CH * T], F32, kind="Internal").ap()
    cc_in = nc.dram_tensor("cc_in", [128, 1032], F32, kind="Internal").ap()
    cc_out = nc.dram_tensor("cc_out", [128, 1032], F32, kind="Internal").ap()
    gc_in = nc.dram_tensor("gc_in", [128, 8], F32, kind="Internal").ap()
    gc_out = nc.dram_tensor("gc_out", [128, 8], F32, kind="Internal").ap()
    taps = {}
    if DEBUG_TAPS:
        for nm, shape, dt_ in (("t_kvacc", [128, 1032], F32), ("t_phiq", [128, CH * T], BF16),
                               ("t_r", [128, CH * T], F32), ("t_x1", [128, CH * T], F32),
                               ("t_z", [16, T], F32), ("t_gs", [128, CH * T], BF16),
                               ("t_c1", [128, CH * T], F32), ("t_cp", [4, 512], F32),
                               ("t_th", [4, 512], F32), ("t_tiny", [4, 8], F32),
                               ("t_ksf", [128, 8], F32), ("t_kvbd", [128, 1024], BF16),
                               ("t_cb", [128, 2048], F32),
                               ("t_h1", [128, CH * T], BF16)):
            taps[nm] = nc.dram_tensor(nm, shape, dt_, kind="ExternalOutput").ap()

    from contextlib import ExitStack
    es = ExitStack()
    sb = lambda name, shape, dt_: es.enter_context(nc.sbuf_tensor(name, list(shape), dt_))

    stream = sb("stream", (128, CH * T), F32)
    R1 = sb("R1", (128, CH * T), BF16)
    R2 = sb("R2", (128, CH * T), BF16)
    R3 = sb("R3", (128, CH * 1024), BF16)
    R4 = sb("R4", (128, CH * 1024), BF16)
    CW2 = sb("CW2", (128, 4096), BF16)
    CW3 = sb("CW3", (128, 1024), BF16)
    SCR = sb("SCR", (128, 3072), F32)
    ZHI = sb("ZHI", (16, 2048), BF16)
    ZLO = sb("ZLO", (16, 2048), BF16)
    S4 = sb("S4", (4, 512), F32)
    Q4 = sb("Q4", (4, 512), F32)
    TH4 = sb("TH4", (4, 512), F32)
    CPHI = sb("CPHI", (4, 512), BF16)
    CPLO = sb("CPLO", (4, 512), BF16)
    HI4 = sb("HI4", (64, D), BF16)
    LO4 = sb("LO4", (64, D), BF16)
    BP = sb("BP", (128, 64), F32)
    CPK = sb("CPK", (128, 1824), BF16)
    KVBD = sb("KVBD", (128, 1024), BF16)
    TINY = sb("TINY", (128, 48), F32)
    TINYB = sb("TINYB", (128, 64), BF16)
    PS = es.enter_context(nc.psum_tensor("PS", [128, 4096], F32))

    xbf = R1
    phiq = R2
    kvacc = R2[:, 0:2064].bitcast(F32)
    ktile = [SCR[:, 0:512].bitcast(BF16), SCR[:, 512:1024].bitcast(BF16)]
    vtile = [SCR[:, 1024:1536].bitcast(BF16), SCR[:, 1536:2048].bitcast(BF16)]
    etile = [SCR[:, 2048:2560], SCR[:, 2560:3072]]
    etq = [SCR[:, 0:512], SCR[:, 512:1024]]
    tgt = [SCR[:, 1024:1280].bitcast(BF16), SCR[:, 1280:1536].bitcast(BF16)]
    gw1sb = SCR[:, 0:2048].bitcast(BF16)
    gw2sb = SCR[:, 2048:2560].bitcast(BF16)
    x1t = [SCR[:, 0:1024], SCR[:, 1024:2048]]
    twslot = [SCR[:, i * 256:(i + 1) * 256].bitcast(BF16) for i in range(8)]
    rowstg = SCR[0:64, 0:1024]
    outbf = R1
    h3 = R1
    twA = R1[:, 0:4096].bitcast(F32)
    twB = R1[:, 4096:8192].bitcast(F32)
    r2t = [R1[:, 8192 + i * 2048: 8192 + (i + 1) * 2048] for i in range(4)]
    a4hi = CPHI[0:4, :]
    a4lo = CPLO[0:4, :]
    b4hi = TH4[0:4, 0:256].bitcast(BF16)
    b4lo = TH4[0:4, 256:512].bitcast(BF16)
    ntmp = SCR[:, 2560:3072]
    tqmask = CPK[:, 0:16]
    selz = CPK[0:16, 16:1040]
    jsel = CPK[0:4, 1040:1552]
    ones1 = CPK[0:1, 1552:1680]
    ones1b = CPK[32:33, 1552:1680]
    onescol = CPK[:, 1680:1681]
    zmask = CPK[:, 1696:1824]
    bq = BP[:, 0:8]; bg = BP[:, 8:16]; bo = BP[:, 16:24]
    cb1 = BP[:, 24:32]; cb2 = BP[:, 32:36]; cb3 = BP[:, 36:38]
    gb1 = BP[:, 38:42]; gb2 = BP[:, 42:44]
    cb4c = BP[0:4, 44:45]; gb3c = BP[0:4, 45:46]; arc = BP[0:4, 46:47]
    epsc = BP[:, 47:48]
    nbq = TINY[:, 0:8]; bq1 = TINY[:, 8:16]; bgh = TINY[:, 16:24]
    epz = TINY[:, 44:45]
    ksf = TINY[:, 24:32]; gcp = TINY[:, 32:40]; cw4f = TINY[:, 40:42]
    b4h4 = TINY[0:4, 42:43]; gb3h4 = TINY[0:4, 43:44]
    g4sb = TINY[0:4, 46:47]; g7col = TINY[0:4, 47:48]
    ZLF = SCR[:, 1024:1152]
    ZLF2 = SCR[:, 1152:1280]
    zlhi = SCR[:, 1280:1344].bitcast(BF16)
    zllo = SCR[:, 1344:1408].bitcast(BF16)
    ZBr = SCR[0:16, 2560:3072]
    gcbf = TINYB[:, 0:8]; gh1 = TINYB[:, 8:12]; gh2 = TINYB[:, 12:14]
    cw4m0 = TINYB[:, 16:32]; cw4m1 = TINYB[:, 32:48]
    GW3 = TINYB[:, 48:56]

    SIMSYNC = os.environ.get("BASS_SIM_FP64") == "1"
    sems = {}
    for s in ("dT", "Ts", "Vs", "Ss", "Gs", "CC1", "CC2"):
        sems[s] = Ctr(es.enter_context(nc.semaphore(s)))
    dTap, CC1, CC2, Tc, Vc, Sc, Gc = (
        sems[s] for s in ("dT", "CC1", "CC2", "Ts", "Vs", "Ss", "Gs"))
    _semn = [0]
    def mkdma():
        _semn[0] += 1
        return Ctr(es.enter_context(nc.semaphore(f"dma{_semn[0]}")))

    kb = KB(simsync=SIMSYNC)
    W = kb.wait
    St = kb.step

    s_bp = mkdma(); s_rp = mkdma(); s_cpk = mkdma(); s_x = mkdma(); s_cw4f = mkdma()
    St("sync", lambda e: e.dma_start(out=BP[:, :], in_=dr["biaspack"]), inc=s_bp, k=16)
    St("sync", lambda e: e.dma_start(out=rowstg, in_=dr["rowpack"]), inc=s_rp, k=16)
    St("sync", lambda e: e.dma_start(out=CPK[:, :], in_=dr["cpack"]), inc=s_cpk, k=16)
    St("sync", lambda e: e.dma_start(
        out=stream[:, :].rearrange("p (c t) -> p c t", c=CH),
        in_=dr["x"].rearrange("(c p) t -> p c t", p=128)), inc=s_x, k=16)
    St("sync", lambda e: e.dma_start(
        out=cw4f, in_=dr["cw4"].rearrange("(kc p) o -> p (kc o)", p=128),
        allow_slow_non_contiguous=True), inc=s_cw4f, k=16)

    def wload(dst, name, kc):
        return lambda e: e.dma_start(
            out=dst.rearrange("p (kc m) -> p kc m", kc=kc),
            in_=dr[name].rearrange("(kc p) m -> p kc m", p=128))

    s_xbf = mkdma(); s_wk = mkdma(); s_wv = mkdma()
    s_cw2 = mkdma(); s_cw3 = mkdma(); s_gw3 = mkdma()
    St("gpsimd", lambda e: e.dma_start(
        out=xbf[:, :].rearrange("p (c t) -> p c t", c=CH),
        in_=dr["x"].rearrange("(c p) t -> p c t", p=128)), inc=s_xbf, k=16)
    St("gpsimd", wload(R3, "wk", 8), inc=s_wk, k=16)
    St("gpsimd", wload(R4, "wv", 8), inc=s_wv, k=16)
    St("gpsimd", wload(CW2, "cw2", 8), inc=s_cw2, k=16)
    St("gpsimd", wload(CW3, "cw3", 4), inc=s_cw3, k=16)
    St("gpsimd", lambda e: e.dma_start(out=GW3, in_=dr["gwpack"]), inc=s_gw3, k=16)

    for s_ in (s_bp, s_rp, s_cpk, s_cw4f):
        W("vector", s_, 16)
    St("vector", lambda e: e.tensor_scalar(out=nbq, in0=bq, scalar1=-1.0, scalar2=0.0, op0=ALU.mult, op1=ALU.add))
    St("vector", lambda e: e.tensor_scalar(out=bq1, in0=bq, scalar1=1.0, scalar2=0.0, op0=ALU.add, op1=ALU.add))
    St("vector", lambda e: e.tensor_scalar(out=bgh, in0=bg, scalar1=0.5, scalar2=0.0, op0=ALU.mult, op1=ALU.add))
    W("scalar", s_bp, 16)
    St("scalar", lambda e: e.mul(out=b4h4, in_=cb4c, mul=0.5))
    St("scalar", lambda e: e.mul(out=gb3h4, in_=gb3c, mul=0.5), inc=Sc)
    St("vector", lambda e: e.memset(epz, 1e-6))
    St("vector", lambda e: e.tensor_copy(HI4[:, :], rowstg))
    St("vector", lambda e: e.tensor_tensor(out=LO4[:, :], in0=rowstg, in1=HI4[:, :], op=ALU.subtract))
    St("vector", lambda e: e.tensor_scalar(out=cw4m0, in0=tqmask, scalar1=cw4f[:, 0:1], scalar2=0.0, op0=ALU.mult, op1=ALU.add))
    v_consts = St("vector", lambda e: e.tensor_scalar(out=cw4m1, in0=tqmask, scalar1=cw4f[:, 1:2],
                                                      scalar2=0.0, op0=ALU.mult, op1=ALU.add), inc=Vc)

    kdone = [0] * NT; vdone = [0] * NT; vevac = [0] * NT
    kstt = [0] * NT; kvdone = [0] * NT; kvadd = [0] * NT

    def proj_tm(wsb, base, hrow, t):
        onesr = ones1 if hrow == 0 else ones1b
        def f(e):
            for kc in range(CH):
                lhs = xbf[:, kc * T + t * 128: kc * T + (t + 1) * 128]
                for h in range(2):
                    e.matmul(PS[:, base + h * NQ: base + (h + 1) * NQ], lhs,
                             wsb[:, kc * 1024 + h * NQ: kc * 1024 + (h + 1) * NQ],
                             start=(kc == 0), stop=False, skip_group_check=True)
            for h in range(2):
                dst = PS[:, base + h * NQ: base + (h + 1) * NQ]
                e.matmul(dst, onesr, HI4[hrow:hrow + 1, h * NQ:(h + 1) * NQ],
                         start=False, stop=False, skip_group_check=True)
                mm = e.matmul(dst, onesr, LO4[hrow:hrow + 1, h * NQ:(h + 1) * NQ],
                              start=False, stop=True, skip_group_check=True)
            return mm
        return f

    def kv_mm(j):
        sj = j % 2
        kbj = sj * 2048; vbj = kbj + 1024
        def f(e):
            for c in range(CH):
                off = kbj + (c // 4) * NQ + (c % 4) * 128
                e.matmul(PS[:, off:off + 128], ktile[sj][:, c * 128:(c + 1) * 128],
                         vtile[sj][:, c * 128:(c + 1) * 128],
                         start=(c % 4 == 0), stop=(c % 4 == 3), skip_group_check=True)
                mm = e.matmul(PS[:, vbj + c: vbj + c + 1],
                              ktile[sj][:, c * 128:(c + 1) * 128], onescol,
                              start=(c == 0), stop=(c == CH - 1), skip_group_check=True)
            return mm
        return f

    def emit_kvadd(j):
        sj = j % 2; kbj = sj * 2048; vbj = kbj + 1024
        W("vector", Tc, kvdone[j])
        if j == 0:
            St("vector", lambda e, kbj=kbj: e.tensor_copy(kvacc[:, 0:512], PS[:, kbj:kbj + 512]))
            St("vector", lambda e, kbj=kbj: e.tensor_copy(kvacc[:, 512:1024], PS[:, kbj + 512:kbj + 1024]))
            kvadd[j] = St("vector", lambda e, vbj=vbj: e.tensor_copy(
                kvacc[:, 1024:1032], PS[:, vbj:vbj + 8]), inc=Vc)
        else:
            St("vector", lambda e, kbj=kbj: e.tensor_tensor(
                out=kvacc[:, 0:512], in0=kvacc[:, 0:512], in1=PS[:, kbj:kbj + 512], op=ALU.add))
            St("vector", lambda e, kbj=kbj: e.tensor_tensor(
                out=kvacc[:, 512:1024], in0=kvacc[:, 512:1024], in1=PS[:, kbj + 512:kbj + 1024], op=ALU.add))
            kvadd[j] = St("vector", lambda e, vbj=vbj: e.tensor_tensor(
                out=kvacc[:, 1024:1032], in0=kvacc[:, 1024:1032], in1=PS[:, vbj:vbj + 8],
                op=ALU.add), inc=Vc)

    for t in range(NT):
        s = t % 2
        kbase = s * 2048
        vbase = kbase + 1024
        if t >= 2:
            W("tensor", Vc, kvadd[t - 2])
        if t == 0:
            W("tensor", s_xbf, 16)
            W("tensor", s_wk, 16)
            W("tensor", s_cpk, 16)
            W("tensor", Vc, v_consts)
        kdone[t] = St("tensor", proj_tm(R3, kbase, 0, t), inc=Tc)
        if t == 0:
            W("tensor", s_wv, 16)
        vdone[t] = St("tensor", proj_tm(R4, vbase, 32, t), inc=Tc)
        if t >= 1:
            jj = t - 1
            W("tensor", Vc, kstt[jj])
            W("tensor", Sc, vevac[jj])
            kvdone[jj] = St("tensor", kv_mm(jj), inc=Tc, drain=True)
        W("scalar", Tc, kdone[t])
        if t >= 2:
            W("scalar", Vc, kstt[t - 2])
        khalf = [0, 0]
        for h in range(2):
            if h == 1:
                W("scalar", Vc, khalf[0])
            St("scalar", lambda e, s=s, kbase=kbase, h=h: e.activation(
                out=etile[s], in_=PS[:, kbase + h * NQ: kbase + (h + 1) * NQ],
                func=AF.Relu, scale=-1.0))
            kexp_h = St("scalar", lambda e, s=s: e.activation(
                out=etile[s], in_=etile[s], func=AF.Exp, scale=-1.0), inc=Sc)
            W("vector", Sc, kexp_h)
            W("vector", Tc, kdone[t])
            if t >= 2:
                W("vector", Tc, kvdone[t - 2])
            khalf[h] = St("vector", lambda e, s=s, kbase=kbase, h=h: e.scalar_tensor_tensor(
                out=ktile[s][:, h * NQ:(h + 1) * NQ],
                in0=PS[:, kbase + h * NQ: kbase + (h + 1) * NQ],
                scalar=1.0, in1=etile[s], op0=ALU.add, op1=ALU.max), inc=Vc)
        kstt[t] = khalf[1]
        W("scalar", Tc, vdone[t])
        if t >= 2:
            W("scalar", Tc, kvdone[t - 2])
        vevac[t] = St("scalar", lambda e, s=s, vbase=vbase: e.activation(
            out=vtile[s][:, :], in_=PS[:, vbase:vbase + 1024], func=AF.Copy), inc=Sc)
        if t >= 1:
            emit_kvadd(t - 1)
    jj = NT - 1
    W("tensor", Vc, kstt[jj])
    W("tensor", Sc, vevac[jj])
    kvdone[jj] = St("tensor", kv_mm(jj), inc=Tc, drain=True)
    emit_kvadd(jj)

    s_wq = mkdma(); s_wg = mkdma(); s_ccin = mkdma(); s_kvbd = mkdma(); s_ksf = mkdma()
    W("gpsimd", Tc, kdone[NT - 1])
    St("gpsimd", wload(R3, "wq", 8), inc=s_wq, k=16)
    W("gpsimd", Tc, vdone[NT - 1])
    St("gpsimd", wload(R4, "wg", 8), inc=s_wg, k=16)
    W("gpsimd", Vc, kvadd[NT - 1])
    if DEBUG_TAPS:
        St("gpsimd", lambda e: e.dma_start(out=taps["t_kvacc"], in_=kvacc[:, :]), inc=dTap, k=16)
    St("gpsimd", lambda e: e.dma_start(out=cc_in, in_=kvacc[:, :]), inc=s_ccin, k=16)
    W("gpsimd", s_ccin, 16)
    St("gpsimd", lambda e: e.collective_compute(
        "AllReduce", ALU.add, replica_groups=[[0, 1, 2, 3], [4, 5, 6, 7]],
        ins=[cc_in.opt()], outs=[cc_out.opt()]), inc=CC1)
    W("gpsimd", CC1, 1)
    St("gpsimd", lambda e: e.dma_start(out=KVBD[:, :], in_=cc_out[:, 0:1024]), inc=s_kvbd, k=16)
    St("gpsimd", lambda e: e.dma_start(out=ksf, in_=cc_out[:, 1024:1032]), inc=s_ksf, k=16)

    phiq_stt = [[0] * TQ for _ in range(CH)]
    phiq_mm = [0] * CH
    etq_guard = [0, 0]
    for c in range(CH):
        if c == 0:
            W("tensor", s_wq, 16)
            W("tensor", Vc, kvadd[NT - 1])
            W("tensor", Vc, kvadd[NT - 2])
        if c >= 2:
            W("tensor", Vc, phiq_stt[c - 2][TQ - 1])
        qb = (c % 2) * 4
        def qmm(e, c=c, qb=qb):
            for kc in range(CH):
                lhs = R3[:, kc * 1024 + c * 128: kc * 1024 + (c + 1) * 128]
                for tq in range(TQ):
                    mm = e.matmul(PS[:, (qb + tq) * NQ:(qb + tq + 1) * NQ], lhs,
                                  xbf[:, kc * T + tq * NQ: kc * T + (tq + 1) * NQ],
                                  start=(kc == 0), stop=(kc == CH - 1), skip_group_check=True)
            return mm
        phiq_mm[c] = St("tensor", qmm, inc=Tc)
        for tq in range(TQ):
            sl = tq % 2
            W("scalar", Tc, phiq_mm[c])
            if etq_guard[sl]:
                W("scalar", Vc, etq_guard[sl])
            St("scalar", lambda e, c=c, qb=qb, tq=tq, sl=sl: e.activation(
                out=etq[sl], in_=PS[:, (qb + tq) * NQ:(qb + tq + 1) * NQ],
                func=AF.Relu, bias=nbq[:, c:c + 1], scale=-1.0))
            sx = St("scalar", lambda e, sl=sl: e.activation(
                out=etq[sl], in_=etq[sl], func=AF.Exp, scale=-1.0), inc=Sc)
            W("vector", Sc, sx)
            if c == 0 and tq == 0:
                W("vector", s_ccin, 16)
            phiq_stt[c][tq] = St("vector", lambda e, c=c, qb=qb, tq=tq, sl=sl: e.scalar_tensor_tensor(
                out=phiq[:, c * T + tq * NQ: c * T + (tq + 1) * NQ],
                in0=PS[:, (qb + tq) * NQ:(qb + tq + 1) * NQ], scalar=bq1[:, c:c + 1],
                in1=etq[sl], op0=ALU.add, op1=ALU.max), inc=Vc)
            etq_guard[sl] = phiq_stt[c][tq]
    if DEBUG_TAPS:
        W("gpsimd", Vc, phiq_stt[CH - 1][TQ - 1])
        phiq_tap = St("gpsimd", lambda e: e.dma_start(out=taps["t_phiq"], in_=phiq[:, :]),
                      inc=dTap, k=16)
        St("gpsimd", lambda e: e.dma_start(out=taps["t_ksf"], in_=ksf), inc=dTap, k=16)

    W("gpsimd", s_kvbd, 16)
    g_kv = 0
    for c in range(CH):
        St("gpsimd", lambda e, c=c: e.memset(KVBD[0:64, c * 128 + 64:(c + 1) * 128], 0.0))
        g_kv = St("gpsimd", lambda e, c=c: e.memset(KVBD[64:128, c * 128:c * 128 + 64], 0.0),
                  inc=Gc, drain=(c == CH - 1))
    W("vector", s_cpk, 16)
    W("vector", s_ksf, 16)
    St("vector", lambda e: e.tensor_copy(ZLF2[:, :], zmask))
    for c in range(CH):
        St("vector", lambda e, c=c: e.tensor_scalar(
            out=ZLF[:, c * 16:(c + 1) * 16], in0=ZLF2[:, c * 16:(c + 1) * 16],
            scalar1=ksf[:, c:c + 1], scalar2=0.0, op0=ALU.mult, op1=ALU.add))
    St("vector", lambda e: e.tensor_copy(zlhi[:, :], ZLF[:, :]))
    St("vector", lambda e: e.tensor_copy(ZLF2[:, :], zlhi[:, :]))
    St("vector", lambda e: e.tensor_tensor(out=ZLF[:, :], in0=ZLF[:, :], in1=ZLF2[:, :],
                                           op=ALU.subtract))
    v_zl = St("vector", lambda e: e.tensor_copy(zllo[:, :], ZLF[:, :]), inc=Vc)

    if DEBUG_TAPS:
        W("gpsimd", Gc, g_kv)
        W("gpsimd", Vc, v_zl)
        St("gpsimd", lambda e: e.dma_start(out=taps["t_kvbd"], in_=KVBD[:, :]), inc=dTap, k=16)

    W("tensor", Gc, g_kv)
    W("tensor", Vc, v_zl)
    W("tensor", Vc, phiq_stt[CH - 1][TQ - 1])
    W("tensor", Vc, kvadd[NT - 1])
    def zden_mm(e):
        for c in range(CH):
            for tq in range(TQ):
                dst = PS[0:16, 1024 + tq * NQ: 1024 + (tq + 1) * NQ]
                e.matmul(dst, zlhi[:, c * 16:(c + 1) * 16],
                         phiq[:, c * T + tq * NQ: c * T + (tq + 1) * NQ],
                         start=(c == 0), stop=False, skip_group_check=True)
                mm = e.matmul(dst, zllo[:, c * 16:(c + 1) * 16],
                              phiq[:, c * T + tq * NQ: c * T + (tq + 1) * NQ],
                              start=False, stop=(c == CH - 1), skip_group_check=True)
        return mm
    zden_done = St("tensor", zden_mm, inc=Tc, drain=True)

    zrow = [0] * TQ
    for tq in range(TQ):
        W("scalar", Tc, zden_done)
        if tq > 0:
            W("scalar", Vc, zrow[tq - 1])
        St("scalar", lambda e, tq=tq: e.activation(
            out=ZBr, in_=PS[0:16, 1024 + tq * NQ: 1024 + (tq + 1) * NQ],
            func=AF.Ln, bias=epz[0:16, :]))
        zs = St("scalar", lambda e: e.activation(out=ZBr, in_=ZBr, func=AF.Exp,
                                                 scale=-1.0), inc=Sc)
        W("vector", Sc, zs)
        slot = tq
        St("vector", lambda e, slot=slot: e.tensor_scalar(
            out=ZHI[:, slot * NQ:(slot + 1) * NQ], in0=ZBr, scalar1=0.5,
            scalar2=0.0, op0=ALU.mult, op1=ALU.add))
        zrow[tq] = St("vector", lambda e, slot=slot: e.scalar_tensor_tensor(
            out=ZLO[:, slot * NQ:(slot + 1) * NQ], in0=ZBr, scalar=0.5,
            in1=ZHI[:, slot * NQ:(slot + 1) * NQ], op0=ALU.mult, op1=ALU.subtract), inc=Vc)
        if DEBUG_TAPS:
            W("gpsimd", Vc, zrow[tq])
            zt = St("gpsimd", lambda g, tq=tq: g.dma_start(
                out=taps["t_z"][:, tq * NQ:(tq + 1) * NQ], in_=ZBr), inc=dTap, k=16)
            W("vector", dTap, zt)

    s_wo = mkdma()
    W("gpsimd", Tc, phiq_mm[CH - 1])
    St("gpsimd", wload(R3, "wo", 8), inc=s_wo, k=16)

    gs_v = [[0] * CH for _ in range(TQ)]
    oevac = [[0] * TQ for _ in range(CH)]
    gate_last_mm = 0
    op_mm_last = 0
    for tq in range(TQ):
        slot = tq
        for c in range(CH):
            idx = tq * CH + c
            rot = idx % 2
            if idx == 0:
                W("tensor", s_kvbd, 16)
                W("tensor", s_wg, 16)
                W("tensor", Vc, zrow[1])
            if idx >= 2:
                W("tensor", Vc, gs_v[(idx - 2) // CH][(idx - 2) % CH])
            W("tensor", Vc, zrow[tq])
            def c4mm(e, c=c, tq=tq, rot=rot, slot=slot):
                zb_dst = PS[:, 3072 + rot * NQ: 3072 + (rot + 1) * NQ]
                e.matmul(zb_dst, selz[:, c * 128:(c + 1) * 128],
                         ZHI[:, slot * NQ:(slot + 1) * NQ], start=True, stop=False,
                         skip_group_check=True)
                e.matmul(zb_dst, selz[:, c * 128:(c + 1) * 128],
                         ZLO[:, slot * NQ:(slot + 1) * NQ], start=False, stop=True,
                         skip_group_check=True)
                sc_dst = PS[:, rot * NQ:(rot + 1) * NQ]
                e.matmul(sc_dst, KVBD[:, c * 128:(c + 1) * 128],
                         phiq[:, c * T + tq * NQ: c * T + (tq + 1) * NQ],
                         start=True, stop=True, skip_group_check=True)
                g_dst = PS[:, 1024 + rot * NQ: 1024 + (rot + 1) * NQ]
                for kc in range(CH):
                    mm = e.matmul(g_dst, R4[:, kc * 1024 + c * 128: kc * 1024 + (c + 1) * 128],
                                  xbf[:, kc * T + tq * NQ: kc * T + (tq + 1) * NQ],
                                  start=(kc == 0), stop=(kc == CH - 1), skip_group_check=True)
                return mm
            mm_n = St("tensor", c4mm, inc=Tc)
            gate_last_mm = mm_n
            W("scalar", Tc, mm_n)
            if idx >= 2:
                W("scalar", Vc, gs_v[(idx - 2) // CH][(idx - 2) % CH])
            tg_n = St("scalar", lambda e, c=c, rot=rot: e.activation(
                out=tgt[rot], in_=PS[:, 1024 + rot * NQ: 1024 + (rot + 1) * NQ],
                func=AF.Tanh, bias=bgh[:, c:c + 1], scale=0.5), inc=Sc)
            W("vector", Tc, mm_n)
            W("vector", Sc, tg_n)
            if DEBUG_TAPS and idx == 0:
                W("vector", dTap, phiq_tap)
            St("vector", lambda e, rot=rot: e.scalar_tensor_tensor(
                out=etq[rot], in0=tgt[rot], scalar=1.0,
                in1=PS[:, 3072 + rot * NQ: 3072 + (rot + 1) * NQ],
                op0=ALU.add, op1=ALU.mult))
            gs_v[tq][c] = St("vector", lambda e, c=c, tq=tq, rot=rot: e.tensor_tensor(
                out=phiq[:, c * T + tq * NQ: c * T + (tq + 1) * NQ],
                in0=etq[rot], in1=PS[:, rot * NQ:(rot + 1) * NQ], op=ALU.mult), inc=Vc)

    for mc in range(CH):
        if mc == 0:
            W("tensor", Vc, gs_v[TQ - 1][CH - 1])
            W("tensor", s_wo, 16)
        if mc >= 2:
            W("tensor", Vc, oevac[mc - 2][TQ - 1])
        ob_ = (mc % 2) * 4
        def opmm(e, mc=mc, ob_=ob_):
            for kc in range(CH):
                lhs = R3[:, kc * 1024 + mc * 128: kc * 1024 + (mc + 1) * 128]
                for tq in range(TQ):
                    mm = e.matmul(PS[:, (ob_ + tq) * NQ:(ob_ + tq + 1) * NQ], lhs,
                                  phiq[:, kc * T + tq * NQ: kc * T + (tq + 1) * NQ],
                                  start=(kc == 0), stop=(kc == CH - 1), skip_group_check=True)
            return mm
        op_mm_last = St("tensor", opmm, inc=Tc)
        for tq in range(TQ):
            W("vector", Tc, op_mm_last)
            if mc == 0 and tq == 0:
                W("vector", s_x, 16)
            oevac[mc][tq] = St("vector", lambda e, mc=mc, tq=tq, ob_=ob_: e.scalar_tensor_tensor(
                out=stream[:, mc * T + tq * NQ: mc * T + (tq + 1) * NQ],
                in0=PS[:, (ob_ + tq) * NQ:(ob_ + tq + 1) * NQ],
                scalar=bo[:, mc:mc + 1],
                in1=stream[:, mc * T + tq * NQ: mc * T + (tq + 1) * NQ],
                op0=ALU.add, op1=ALU.add), inc=Vc)

    s_cw1 = mkdma(); s_gw1 = mkdma(); s_gw2 = mkdma()
    W("gpsimd", Tc, gate_last_mm)
    St("gpsimd", wload(R4, "cw1", 8), inc=s_cw1, k=16)
    W("gpsimd", Vc, gs_v[TQ - 1][CH - 1])
    St("gpsimd", lambda e: e.dma_start(
        out=gw1sb.rearrange("p (kc m) -> p kc m", kc=8),
        in_=dr["gw1"].rearrange("(kc p) m -> p kc m", p=128)), inc=s_gw1, k=16)
    St("gpsimd", lambda e: e.dma_start(
        out=gw2sb.rearrange("p (kc m) -> p kc m", kc=4),
        in_=dr["gw2"].rearrange("(kc p) m -> p kc m", p=128)), inc=s_gw2, k=16)
    if DEBUG_TAPS:
        W("gpsimd", Vc, oevac[CH - 1][TQ - 1])
        r_tap = St("gpsimd", lambda e: e.dma_start(out=taps["t_r"], in_=stream[:, :]),
                   inc=dTap, k=16)
        gs_tap = St("gpsimd", lambda e: e.dma_start(out=taps["t_gs"], in_=phiq[:, :]),
                    inc=dTap, k=16)

    cb_mm = [0, 0, 0]
    r2add = [0] * CH

    def layer_norm(tag, norm_list):
        ln2 = tag == "ln2"
        ord_ = [0, 4, 1, 5, 2, 6, 3, 7] if ln2 else list(range(CH))
        sq = [0] * CH
        rbfv = [0] * CH
        sums_c = [0] * CH
        for pos, c in enumerate(ord_):
            if not ln2:
                W("scalar", Tc, op_mm_last)
                W("vector", Tc, op_mm_last)
                W("scalar", Vc, oevac[c][TQ - 1])
                if DEBUG_TAPS:
                    W("scalar", dTap, gs_tap)
                    W("vector", dTap, gs_tap)
                rbfv[c] = St("vector", lambda e, c=c: e.tensor_copy(
                    R2[:, c * T:(c + 1) * T], stream[:, c * T:(c + 1) * T]), inc=Vc)
            else:
                W("scalar", Vc, r2add[c])
                W("gpsimd", Vc, r2add[c])
                rbfv[c] = St("gpsimd", lambda e, c=c: e.tensor_copy(
                    R2[:, c * T:(c + 1) * T], stream[:, c * T:(c + 1) * T]), inc=Gc,
                    drain=True)
            slot = pos % 4
            if pos >= 4:
                W("scalar", Tc, sums_c[pos - 4])
            sq[c] = St("scalar", lambda e, c=c, slot=slot: e.activation(
                out=r2t[slot], in_=stream[:, c * T:(c + 1) * T], func=AF.Square), inc=Sc)
            W("tensor", Sc, sq[c])
            W("tensor", Gc if ln2 else Vc, rbfv[c])
            def summ(e, c=c, slot=slot, pos=pos):
                for tq in range(TQ):
                    e.matmul(PS[0:4, 0:NQ], tqmask[:, tq * 4:(tq + 1) * 4],
                             R2[:, c * T + tq * NQ: c * T + (tq + 1) * NQ],
                             start=(pos == 0 and tq == 0), stop=(pos == CH - 1 and tq == TQ - 1),
                             skip_group_check=True)
                    mm = e.matmul(PS[0:4, NQ:2 * NQ], tqmask[:, tq * 4:(tq + 1) * 4],
                                  r2t[slot][:, tq * NQ:(tq + 1) * NQ],
                                  start=(pos == 0 and tq == 0), stop=(pos == CH - 1 and tq == TQ - 1),
                                  skip_group_check=True)
                return mm
            sums_c[pos] = St("tensor", summ, inc=Tc, drain=True)
        W("vector", Tc, sums_c[CH - 1])
        St("vector", lambda e: e.tensor_scalar(out=S4[:, :], in0=PS[0:4, 0:NQ],
                                               scalar1=1.0 / D, scalar2=0.0, op0=ALU.mult, op1=ALU.add))
        St("vector", lambda e: e.tensor_scalar(out=Q4[:, :], in0=PS[0:4, NQ:2 * NQ],
                                               scalar1=1.0 / D, scalar2=0.0, op0=ALU.mult, op1=ALU.add))
        St("vector", lambda e: e.tensor_tensor(out=TH4[:, :], in0=S4[:, :], in1=S4[:, :], op=ALU.mult))
        v_var = St("vector", lambda e: e.tensor_tensor(out=Q4[:, :], in0=Q4[:, :], in1=TH4[:, :],
                                                       op=ALU.subtract), inc=Vc)
        W("scalar", Vc, v_var)
        St("scalar", lambda e: e.activation(out=Q4[:, :], in_=Q4[:, :], func=AF.Ln, bias=epsc[0:4, :]))
        s_a = St("scalar", lambda e: e.activation(out=Q4[:, :], in_=Q4[:, :], func=AF.Exp,
                                                  scale=-0.5), inc=Sc)
        W("vector", Sc, s_a)
        St("vector", lambda e: e.scalar_tensor_tensor(out=S4[:, :], in0=S4[:, :], scalar=-1.0,
                                                      in1=Q4[:, :], op0=ALU.mult, op1=ALU.mult))
        St("vector", lambda e: e.tensor_copy(a4hi, Q4[:, :]))
        St("vector", lambda e: e.tensor_tensor(out=a4lo, in0=Q4[:, :], in1=a4hi, op=ALU.subtract))
        St("vector", lambda e: e.tensor_copy(b4hi, S4[:, :]))
        v_ab = St("vector", lambda e: e.tensor_tensor(out=b4lo, in0=S4[:, :], in1=b4hi,
                                                      op=ALU.subtract), inc=Vc)
        last_norm = 0
        bc_mm = [0] * TQ
        aps2 = [PS[:, tq * NQ:(tq + 1) * NQ] for tq in range(TQ)]
        bps2 = [PS[:, 2048 + tq * NQ: 2048 + (tq + 1) * NQ] for tq in range(TQ)]
        for tq in range(TQ):
            if ln2:
                a_ps = aps2[tq]
                b_ps = bps2[tq]
            else:
                rot = tq % 2
                a_ps = PS[:, 2048 + rot * NQ: 2048 + (rot + 1) * NQ]
                b_ps = PS[:, 3072 + rot * NQ: 3072 + (rot + 1) * NQ]
            W("tensor", Vc, v_ab)
            if not ln2 and tq >= 2:
                W("tensor", Vc, norm_list[(tq - 2) * CH + CH - 1])
            def bcmm(e, a_ps=a_ps, b_ps=b_ps, tq=tq):
                e.matmul(a_ps, jsel[:, tq * 128:(tq + 1) * 128], a4hi, start=True, stop=False,
                         skip_group_check=True)
                e.matmul(a_ps, jsel[:, tq * 128:(tq + 1) * 128], a4lo, start=False, stop=True,
                         skip_group_check=True)
                e.matmul(b_ps, jsel[:, tq * 128:(tq + 1) * 128], b4hi, start=True, stop=False,
                         skip_group_check=True)
                mm = e.matmul(b_ps, jsel[:, tq * 128:(tq + 1) * 128], b4lo, start=False, stop=True,
                              skip_group_check=True)
                return mm
            bc_mm[tq] = St("tensor", bcmm, inc=Tc, drain=True)
            if ln2:
                continue
            for c in range(CH):
                W("vector", Tc, bc_mm[tq])
                if not ln2 and DEBUG_TAPS and tq == 0 and c == 0:
                    W("vector", dTap, r_tap)
                St("vector", lambda e, c=c, tq=tq, a_ps=a_ps: e.tensor_tensor(
                    out=ntmp, in0=stream[:, c * T + tq * NQ: c * T + (tq + 1) * NQ],
                    in1=a_ps, op=ALU.mult))
                last_norm = St("vector", lambda e, c=c, tq=tq, b_ps=b_ps: e.tensor_tensor(
                    out=stream[:, c * T + tq * NQ: c * T + (tq + 1) * NQ],
                    in0=ntmp, in1=b_ps,
                    op=ALU.add), inc=Vc)
                norm_list[tq * CH + c] = last_norm
        if ln2:
            W("vector", Tc, bc_mm[TQ - 1])
            ntmp2 = SCR[:, 0:2048]
            for c in range(CH):
                St("vector", lambda e, c=c: e.tensor_tensor(
                    out=ntmp2, in0=stream[:, c * T:(c + 1) * T],
                    in1=PS[:, 0:2048], op=ALU.mult))
                last_norm = St("vector", lambda e, c=c: e.tensor_tensor(
                    out=stream[:, c * T:(c + 1) * T], in0=ntmp2,
                    in1=PS[:, 2048:4096], op=ALU.add), inc=Vc)
                for tq in range(TQ):
                    norm_list[tq * CH + c] = last_norm
        return last_norm

    norm1 = [0] * (TQ * CH)
    ln1_last = layer_norm("ln1", norm1)
    if DEBUG_TAPS:
        W("gpsimd", Vc, ln1_last)
        x1_tap = St("gpsimd", lambda e: e.dma_start(out=taps["t_x1"], in_=stream[:, :]),
                    inc=dTap, k=16)

    prev_outbf_tq0 = [[0] * CH for _ in range(TQ)]
    prev_outbf_sem0 = [Sc, Gc, Gc, Gc]
    for tq in range(TQ):
        eng = "scalar" if tq == 0 else "gpsimd"
        W(eng, Vc, norm1[tq * CH + CH - 1])
        for c in range(CH):
            if tq == 0:
                prev_outbf_tq0[tq][c] = St("scalar", lambda e, c=c, tq=tq: e.activation(
                    out=outbf[:, c * T + tq * NQ: c * T + (tq + 1) * NQ],
                    in_=stream[:, c * T + tq * NQ: c * T + (tq + 1) * NQ],
                    func=AF.Copy), inc=Sc)
            else:
                prev_outbf_tq0[tq][c] = St("gpsimd", lambda e, c=c, tq=tq: e.tensor_copy(
                    outbf[:, c * T + tq * NQ: c * T + (tq + 1) * NQ],
                    stream[:, c * T + tq * NQ: c * T + (tq + 1) * NQ]), inc=Gc,
                    drain=(c == CH - 1))
    outbf_c = [prev_outbf_tq0[3][c] for c in range(CH)]

    for c in range(CH):
        W("vector", Vc, norm1[3 * CH + c])
        St("vector", lambda e, c=c: e.reduce_sum(out=gcp[:, c:c + 1],
                                                 in_=stream[:, c * T:(c + 1) * T], axis=AXX))
    v_gcp = St("vector", lambda e: e.engine_nop(), inc=Vc)

    s_park = mkdma()
    W("sync", Vc, ln1_last)
    St("sync", lambda e: e.dma_start(out=x1d, in_=stream[:, :]), inc=s_park, k=16)

    s_gcin = mkdma(); s_gcret = mkdma()
    W("gpsimd", Vc, v_gcp)
    St("gpsimd", lambda e: e.dma_start(out=gc_in, in_=gcp), inc=s_gcin, k=16)
    W("gpsimd", s_gcin, 16)
    St("gpsimd", lambda e: e.collective_compute(
        "AllReduce", ALU.add, replica_groups=[[0, 1, 2, 3], [4, 5, 6, 7]],
        ins=[gc_in.opt()], outs=[gc_out.opt()]), inc=CC2)
    W("sync", CC2, 1)
    St("sync", lambda e: e.dma_start(out=gcp, in_=gc_out), inc=s_gcret, k=16)

    twist_v = [[0] * 4 for _ in range(3)]
    th_done = [0] * 3
    cp_done = [0] * 3
    prev_outbf = outbf_c[:]
    c0_done = 0

    for i in range(3):
        cn1_mm_last = 0
        gelu1w = [[0] * CH for _ in range(TQ)]
        for tq in range(TQ):
            for mc in range(CH):
                idx = tq * CH + mc
                rot4 = idx % 4
                if idx == 0:
                    W("tensor", s_cw1, 16)
                    if i == 0:
                        W("tensor", s_cw2, 16)
                        W("tensor", s_cw3, 16)
                if mc == 0 and tq > 0:
                    if i > 0:
                        W("tensor", prev_outbf_sem[tq], prev_outbf_tq[tq][CH - 1])
                    else:
                        W("tensor", prev_outbf_sem0[tq], prev_outbf_tq0[tq][CH - 1])
                if idx >= 4:
                    W("tensor", Sc, gelu1w[(idx - 4) // CH][(idx - 4) % CH])
                if idx == 0:
                    pv = prev_outbf_tq if i > 0 else prev_outbf_tq0
                    for kc in range(CH):
                        W("tensor", Sc, pv[0][kc])
                        def cn1mm0(e, kc=kc):
                            return e.matmul(PS[:, 0:NQ],
                                            R4[:, kc * 1024: kc * 1024 + 128],
                                            outbf[:, kc * T: kc * T + NQ],
                                            start=(kc == 0), stop=(kc == CH - 1),
                                            skip_group_check=True)
                        cn1_mm_last = St("tensor", cn1mm0,
                                         inc=(Tc if kc == CH - 1 else None))
                    W("scalar", Tc, cn1_mm_last)
                    gelu1w[tq][mc] = St("scalar", lambda e: e.activation(
                        out=R2[:, 0:NQ], in_=PS[:, 0:NQ], func=AF.Gelu,
                        bias=cb1[:, 0:1]), inc=Sc)
                    continue
                def cn1mm(e, mc=mc, tq=tq, rot4=rot4):
                    dst = PS[:, rot4 * NQ:(rot4 + 1) * NQ]
                    for kc in range(CH):
                        mm = e.matmul(dst, R4[:, kc * 1024 + mc * 128: kc * 1024 + (mc + 1) * 128],
                                      outbf[:, kc * T + tq * NQ: kc * T + (tq + 1) * NQ],
                                      start=(kc == 0), stop=(kc == CH - 1), skip_group_check=True)
                    return mm
                cn1_mm_last = St("tensor", cn1mm, inc=Tc)
                W("scalar", Tc, cn1_mm_last)
                if i > 0 and idx == 0 and DEBUG_TAPS:
                    W("scalar", dTap, h1_tap)
                gelu1w[tq][mc] = St("scalar", lambda e, mc=mc, tq=tq, rot4=rot4: e.activation(
                    out=R2[:, mc * T + tq * NQ: mc * T + (tq + 1) * NQ],
                    in_=PS[:, rot4 * NQ:(rot4 + 1) * NQ], func=AF.Gelu,
                    bias=cb1[:, mc:mc + 1]), inc=Sc)
        gelu1 = [None, gelu1w[TQ - 1]]
        if i == 2:
            cn1_fence = St("tensor", lambda e: e.nop(), inc=Tc, drain=True)
        if i == 0 and DEBUG_TAPS:
            W("gpsimd", Sc, gelu1[TQ - 1][CH - 1])
            h1_tap = St("gpsimd", lambda e: e.dma_start(out=taps["t_h1"], in_=R2[:, :]),
                        inc=dTap, k=16)
        gelu2 = [0] * 4
        for mc in range(4):
            W("tensor", Sc, gelu1[1][CH - 1])
            if mc >= 2:
                W("tensor", Sc, gelu2[mc - 2])
            def cn2mm(e, mc=mc):
                base = (mc % 2) * 4
                for kc in range(CH):
                    lhs = CW2[:, kc * 512 + mc * 128: kc * 512 + (mc + 1) * 128]
                    for tq in range(TQ):
                        mm = e.matmul(PS[:, (base + tq) * NQ:(base + tq + 1) * NQ], lhs,
                                      R2[:, kc * T + tq * NQ: kc * T + (tq + 1) * NQ],
                                      start=(kc == 0), stop=(kc == CH - 1), skip_group_check=True)
                return mm
            mmn = St("tensor", cn2mm, inc=Tc)
            cn2_mm_last = mmn
            base = (mc % 2) * 4
            last = 0
            for tq in range(TQ):
                W("scalar", Tc, mmn)
                last = St("scalar", lambda e, mc=mc, tq=tq, base=base: e.activation(
                    out=R3[:, mc * T + tq * NQ: mc * T + (tq + 1) * NQ],
                    in_=PS[:, (base + tq) * NQ:(base + tq + 1) * NQ], func=AF.Gelu,
                    bias=cb2[:, mc:mc + 1]), inc=Sc)
            gelu2[mc] = last
        if i == 2:
            cn2_fence = St("tensor", lambda e: e.nop(), inc=Tc, drain=True)
        if i == 0:
            W("vector", s_gcret, 16)
            v_gcbf = St("vector", lambda e: e.tensor_scalar(
                out=gcbf, in0=gcp, scalar1=1.0 / 8192.0, scalar2=0.0, op0=ALU.mult,
                op1=ALU.add), inc=Vc)
            W("tensor", s_gw1, 16)
            W("tensor", s_gw2, 16)
            W("tensor", s_gw3, 16)
            W("tensor", Vc, v_gcbf)
            W("tensor", Sc, gelu2[3])
            def gf1(e):
                for mc in range(4):
                    for kc in range(CH):
                        mm = e.matmul(PS[:, 2048 + mc: 2048 + mc + 1],
                                      gw1sb[:, kc * 512 + mc * 128: kc * 512 + (mc + 1) * 128],
                                      gcbf[:, kc:kc + 1],
                                      start=(mc == 0 and kc == 0), stop=(mc == 3 and kc == CH - 1),
                                      skip_group_check=True)
                return mm
            t_gf1 = St("tensor", gf1, inc=Tc, drain=True)
            W("scalar", Tc, t_gf1)
            s_gh1 = 0
            for mc in range(4):
                s_gh1 = St("scalar", lambda e, mc=mc: e.activation(
                    out=gh1[:, mc:mc + 1], in_=PS[:, 2048 + mc:2048 + mc + 1],
                    func=AF.Gelu, bias=gb1[:, mc:mc + 1]), inc=Sc)
            W("tensor", Sc, s_gh1)
            def gf2(e):
                for mc in range(2):
                    for kc in range(4):
                        mm = e.matmul(PS[:, 2056 + mc: 2056 + mc + 1],
                                      gw2sb[:, kc * 256 + mc * 128: kc * 256 + (mc + 1) * 128],
                                      gh1[:, kc:kc + 1],
                                      start=(mc == 0 and kc == 0), stop=(mc == 1 and kc == 3),
                                      skip_group_check=True)
                return mm
            t_gf2 = St("tensor", gf2, inc=Tc, drain=True)
            W("scalar", Tc, t_gf2)
            s_gh2 = 0
            for mc in range(2):
                s_gh2 = St("scalar", lambda e, mc=mc: e.activation(
                    out=gh2[:, mc:mc + 1], in_=PS[:, 2056 + mc:2056 + mc + 1],
                    func=AF.Gelu, bias=gb2[:, mc:mc + 1]), inc=Sc)
            W("tensor", Sc, s_gh2)
            def gf3(e):
                for kc in range(2):
                    mm = e.matmul(PS[0:4, 2064:2065], GW3[:, kc * 4:(kc + 1) * 4],
                                  gh2[:, kc:kc + 1], start=(kc == 0), stop=(kc == 1),
                                  skip_group_check=True)
                return mm
            t_gf3 = St("tensor", gf3, inc=Tc, drain=True)
            W("scalar", Tc, t_gf3)
            St("scalar", lambda e: e.activation(
                out=g4sb, in_=PS[0:4, 2064:2065], func=AF.Tanh,
                bias=gb3h4, scale=0.5))
            c0_done = St("scalar", lambda e: e.mul(out=g7col, in_=g4sb, mul=0.7), inc=Sc)
        gelu3 = [0] * 2
        for mc in range(2):
            W("tensor", Sc, gelu2[3])
            if i == 0:
                W("tensor", Sc, c0_done)
            def cn3mm(e, mc=mc):
                base = (mc % 2) * 4
                for kc in range(4):
                    lhs = CW3[:, kc * 256 + mc * 128: kc * 256 + (mc + 1) * 128]
                    for tq in range(TQ):
                        mm = e.matmul(PS[:, (base + tq) * NQ:(base + tq + 1) * NQ], lhs,
                                      R3[:, kc * T + tq * NQ: kc * T + (tq + 1) * NQ],
                                      start=(kc == 0), stop=(kc == 3), skip_group_check=True)
                return mm
            mmn = St("tensor", cn3mm, inc=Tc)
            cn3_mm_last = mmn
            base = (mc % 2) * 4
            last = 0
            for tq in range(TQ):
                W("scalar", Tc, mmn)
                if tq == 0:
                    W("scalar", Tc, cn1_mm_last)
                last = St("scalar", lambda e, mc=mc, tq=tq, base=base: e.activation(
                    out=h3[:, mc * T + tq * NQ: mc * T + (tq + 1) * NQ],
                    in_=PS[:, (base + tq) * NQ:(base + tq + 1) * NQ], func=AF.Gelu,
                    bias=cb3[:, mc:mc + 1]), inc=Sc)
            gelu3[mc] = last
        if i == 2:
            cn3_fence = St("tensor", lambda e: e.nop(), inc=Tc, drain=True)
        W("tensor", Sc, gelu3[1])
        def cn4mm(e):
            for tq in range(TQ):
                for kc in range(2):
                    lhs = (cw4m0 if kc == 0 else cw4m1)[:, tq * 4:(tq + 1) * 4]
                    mm = e.matmul(PS[0:4, 2560:2560 + NQ], lhs,
                                  h3[:, kc * T + tq * NQ: kc * T + (tq + 1) * NQ],
                                  start=(tq == 0 and kc == 0), stop=(tq == TQ - 1 and kc == 1),
                                  skip_group_check=True)
            return mm
        t_cn4 = St("tensor", cn4mm, inc=Tc, drain=True)
        W("scalar", Tc, t_cn4)
        if i > 0 and DEBUG_TAPS:
            W("scalar", dTap, cp_tap)
        th_done[i] = St("scalar", lambda e: e.activation(
            out=TH4[:, :], in_=PS[0:4, 2560:2560 + NQ], func=AF.Tanh,
            bias=b4h4, scale=0.5), inc=Sc)
        if i == 0 and DEBUG_TAPS:
            W("gpsimd", Sc, th_done[0])
            th_tap = St("gpsimd", lambda g: g.dma_start(out=taps["t_th"], in_=TH4[:, :]),
                        inc=dTap, k=16)
            W("gpsimd", Sc, c0_done)
            St("gpsimd", lambda g: g.dma_start(out=taps["t_tiny"], in_=TINY[0:4, 40:48]),
               inc=dTap, k=16)
        W("vector", Sc, th_done[i])
        if i == 0:
            W("vector", Sc, c0_done)
            if DEBUG_TAPS:
                W("vector", dTap, th_tap)
        St("vector", lambda e: e.tensor_scalar(out=TH4[:, :], in0=TH4[:, :],
                                               scalar1=0.3, scalar2=0.0,
                                               op0=ALU.mult, op1=ALU.add))
        St("vector", lambda e: e.tensor_scalar(out=TH4[:, :], in0=TH4[:, :],
                                               scalar1=g7col, scalar2=0.0,
                                               op0=ALU.add, op1=ALU.add))
        St("vector", lambda e: e.tensor_scalar(out=TH4[:, :], in0=TH4[:, :],
                                               scalar1=arc, scalar2=0.1,
                                               op0=ALU.mult, op1=ALU.add))
        St("vector", lambda e: e.tensor_copy(CPHI[:, :], TH4[:, :]))
        cp_done[i] = St("vector", lambda e: e.tensor_tensor(
            out=CPLO[:, :], in0=TH4[:, :], in1=CPHI[:, :], op=ALU.subtract), inc=Vc)
        if i == 0 and DEBUG_TAPS:
            W("gpsimd", Vc, cp_done[0])
            cp_tap = St("gpsimd", lambda g: g.dma_start(out=taps["t_cp"], in_=TH4[:, :]),
                        inc=dTap, k=16)
        W("tensor", Vc, cp_done[i])
        def cbmm(e):
            for tq in range(TQ):
                dst = PS[:, 2048 + tq * NQ: 2048 + (tq + 1) * NQ]
                e.matmul(dst, jsel[:, tq * 128:(tq + 1) * 128], CPHI[:, :], start=True, stop=False,
                         skip_group_check=True)
                mm = e.matmul(dst, jsel[:, tq * 128:(tq + 1) * 128], CPLO[:, :], start=False,
                              stop=True, skip_group_check=True)
            return mm
        cb_mm[i] = St("tensor", cbmm, inc=Tc, drain=True)
        W("vector", Tc, cb_mm[i])
        if i == 0:
            W("vector", s_park, 16)
            if DEBUG_TAPS:
                W("vector", dTap, x1_tap)
        if i < 2:
            twq = [0] * TQ
            for tq in range(TQ):
                cbq = PS[:, 2048 + tq * NQ: 2048 + (tq + 1) * NQ]
                for p, (ca, sA, sB) in enumerate(((0, 1.0, -1.0), (1, 1.0, -1.0),
                                                  (2, -1.0, 1.0), (3, -1.0, 1.0))):
                    St("vector", lambda e, ca=ca, sA=sA, tq=tq, p=p, cbq=cbq: e.scalar_tensor_tensor(
                        out=twslot[2 * p],
                        in0=stream[:, (ca + 4) * T + tq * NQ: (ca + 4) * T + (tq + 1) * NQ],
                        scalar=sA, in1=cbq, op0=ALU.mult, op1=ALU.mult))
                    St("vector", lambda e, ca=ca, sB=sB, tq=tq, p=p, cbq=cbq: e.scalar_tensor_tensor(
                        out=twslot[2 * p + 1],
                        in0=stream[:, ca * T + tq * NQ: ca * T + (tq + 1) * NQ],
                        scalar=sB, in1=cbq, op0=ALU.mult, op1=ALU.mult))
                for p, (ca, _, _) in enumerate(((0, 0, 0), (1, 0, 0), (2, 0, 0), (3, 0, 0))):
                    St("vector", lambda e, ca=ca, tq=tq, p=p: e.tensor_tensor(
                        out=stream[:, ca * T + tq * NQ: ca * T + (tq + 1) * NQ],
                        in0=stream[:, ca * T + tq * NQ: ca * T + (tq + 1) * NQ],
                        in1=twslot[2 * p], op=ALU.add))
                    last = St("vector", lambda e, ca=ca, tq=tq, p=p: e.tensor_tensor(
                        out=stream[:, (ca + 4) * T + tq * NQ: (ca + 4) * T + (tq + 1) * NQ],
                        in0=stream[:, (ca + 4) * T + tq * NQ: (ca + 4) * T + (tq + 1) * NQ],
                        in1=twslot[2 * p + 1], op=ALU.add), inc=(Vc if p == 3 else None))
                twq[tq] = last
            twist_v[i] = [twq[3]] * 4
            twist_tq = twq
        else:
            x1R1 = R1[:, 0:8192].bitcast(F32)
            x1R3 = R3[:, 0:8192].bitcast(F32)
            x1R4 = R4[:, 0:8192].bitcast(F32)
            x1C6 = CW2[:, 0:4096].bitcast(F32)
            s_xr4 = mkdma(); s_xc6 = mkdma(); s_xr3 = mkdma(); s_x1pf = mkdma()
            W("sync", Tc, cn1_fence)
            St("sync", lambda e: e.dma_start(out=x1R4, in_=x1d[:, 4 * T:6 * T]), inc=s_xr4, k=16)
            W("sync", Tc, cn2_fence)
            St("sync", lambda e: e.dma_start(out=x1C6, in_=x1d[:, 6 * T:7 * T]), inc=s_xc6, k=16)
            W("sync", Tc, cn3_fence)
            St("sync", lambda e: e.dma_start(out=x1R3, in_=x1d[:, 2 * T:4 * T]), inc=s_xr3, k=16)
            W("sync", Tc, t_cn4)
            St("sync", lambda e: e.dma_start(out=x1R1, in_=x1d[:, 0:2 * T]), inc=s_x1pf, k=16)
            x1c = {0: x1R1[:, 0:T], 1: x1R1[:, T:2 * T],
                   2: x1R3[:, 0:T], 3: x1R3[:, T:2 * T],
                   4: x1R4[:, 0:T], 5: x1R4[:, T:2 * T], 6: x1C6[:, 0:T]}
            csem = {0: s_x1pf, 1: s_x1pf, 2: s_xr3, 3: s_xr3,
                    4: s_xr4, 5: s_xr4, 6: s_xc6}
            twp = [0] * 4
            for p, (sA, sB) in enumerate(((1.0, -1.0), (1.0, -1.0),
                                          (-1.0, 1.0), (-1.0, 1.0))):
                for tq in range(TQ):
                    cbq = PS[:, 2048 + tq * NQ: 2048 + (tq + 1) * NQ]
                    St("vector", lambda e, p=p, tq=tq, sA=sA, cbq=cbq: e.scalar_tensor_tensor(
                        out=twslot[2 * tq],
                        in0=stream[:, (p + 4) * T + tq * NQ: (p + 4) * T + (tq + 1) * NQ],
                        scalar=sA, in1=cbq, op0=ALU.mult, op1=ALU.mult))
                    St("vector", lambda e, p=p, tq=tq, sB=sB, cbq=cbq: e.scalar_tensor_tensor(
                        out=twslot[2 * tq + 1],
                        in0=stream[:, p * T + tq * NQ: p * T + (tq + 1) * NQ],
                        scalar=sB, in1=cbq, op0=ALU.mult, op1=ALU.mult))
                for tq in range(TQ):
                    St("vector", lambda e, p=p, tq=tq: e.tensor_tensor(
                        out=stream[:, p * T + tq * NQ: p * T + (tq + 1) * NQ],
                        in0=stream[:, p * T + tq * NQ: p * T + (tq + 1) * NQ],
                        in1=twslot[2 * tq], op=ALU.add))
                    twp[p] = St("vector", lambda e, p=p, tq=tq: e.tensor_tensor(
                        out=stream[:, (p + 4) * T + tq * NQ: (p + 4) * T + (tq + 1) * NQ],
                        in0=stream[:, (p + 4) * T + tq * NQ: (p + 4) * T + (tq + 1) * NQ],
                        in1=twslot[2 * tq + 1], op=ALU.add), inc=Vc)
                for c in (p, p + 4):
                    if c == 7:
                        continue
                    W("vector", csem[c], 16)
                    r2add[c] = St("vector", lambda e, c=c: e.tensor_tensor(
                        out=stream[:, c * T:(c + 1) * T], in0=stream[:, c * T:(c + 1) * T],
                        in1=x1c[c], op=ALU.add), inc=Vc)
            rs7 = [mkdma(), mkdma()]
            W("sync", Vc, twp[3])
            St("sync", lambda e: e.dma_start(out=x1t[0], in_=x1d[:, 7 * T: 7 * T + 1024]),
               inc=rs7[0], k=16)
            St("sync", lambda e: e.dma_start(out=x1t[1], in_=x1d[:, 7 * T + 1024: 8 * T]),
               inc=rs7[1], k=16)
            for h in range(2):
                W("vector", rs7[h], 16)
                r2add[7] = St("vector", lambda e, h=h: e.tensor_tensor(
                    out=stream[:, 7 * T + h * 1024: 7 * T + (h + 1) * 1024],
                    in0=stream[:, 7 * T + h * 1024: 7 * T + (h + 1) * 1024],
                    in1=x1t[h], op=ALU.add), inc=Vc)
        if i == 0 and DEBUG_TAPS:
            W("gpsimd", Vc, twist_tq[3])
            c1_tap = St("gpsimd", lambda g: g.dma_start(out=taps["t_c1"], in_=stream[:, :]),
                        inc=dTap, k=16)
        if i < 2:
            nbq_ = [[0] * CH for _ in range(TQ)]
            nbq_sem = [Sc, Gc, Gc, Gc]
            for tq in range(TQ):
                eng = "scalar" if tq == 0 else "gpsimd"
                W(eng, Vc, twist_tq[tq])
                if i == 0 and DEBUG_TAPS and tq == 0:
                    W(eng, dTap, c1_tap)
                for c in range(CH):
                    if tq == 0:
                        nbq_[tq][c] = St("scalar", lambda e, c=c, tq=tq: e.activation(
                            out=outbf[:, c * T + tq * NQ: c * T + (tq + 1) * NQ],
                            in_=stream[:, c * T + tq * NQ: c * T + (tq + 1) * NQ],
                            func=AF.Copy), inc=Sc)
                    else:
                        nbq_[tq][c] = St("gpsimd", lambda e, c=c, tq=tq: e.tensor_copy(
                            outbf[:, c * T + tq * NQ: c * T + (tq + 1) * NQ],
                            stream[:, c * T + tq * NQ: c * T + (tq + 1) * NQ]), inc=Gc,
                            drain=(c == CH - 1))
            prev_outbf_tq = nbq_
            prev_outbf_sem = nbq_sem
            prev_outbf = [nbq_[3][c] for c in range(CH)]

    norm2 = [0] * (TQ * CH)
    layer_norm("ln2", norm2)

    ysem = [mkdma() for _ in range(CH)]
    for c in range(CH):
        W("sync", Vc, norm2[3 * CH + c])
        St("sync", lambda e, c=c: e.dma_start(out=y[c], in_=stream[:, c * T:(c + 1) * T]),
           inc=ysem[c], k=16)
    for c in range(CH):
        W("gpsimd", ysem[c], 16)
    St("gpsimd", lambda e: e.nop())

    class _SimSync:
        _SKIP = {"wait_ge", "dma_start", "dma_start_transpose", "then_inc", "drain", "nop"}

        def __init__(self, eng, sem):
            self._e = eng
            self._s = sem
            self._n = 0

        def __getattr__(self, name):
            a = getattr(self._e, name)
            if name.startswith("_") or name in self._SKIP or not callable(a):
                return a
            def call(*args, **kw):
                if self._n:
                    self._e.wait_ge(self._s, self._n)
                r = a(*args, **kw)
                try:
                    r.then_inc(self._s, 1)
                    self._n += 1
                except Exception:
                    pass
                return r
            return call

    if SIMSYNC:
        ss = {e: es.enter_context(nc.semaphore(f"ss_{e}")) for e in ("t", "v", "s")}

    with nc.Block() as block:
        @block.tensor
        def _(t):
            if SIMSYNC:
                t = _SimSync(t, ss["t"])
            for f in kb.prog["tensor"]:
                f(t)

        @block.vector
        def _(v):
            if SIMSYNC:
                v = _SimSync(v, ss["v"])
            for f in kb.prog["vector"]:
                f(v)

        @block.scalar
        def _(s):
            if SIMSYNC:
                s = _SimSync(s, ss["s"])
            for f in kb.prog["scalar"]:
                f(s)

        @block.gpsimd
        def _(g):
            for f in kb.prog["gpsimd"]:
                f(g)

        @block.sync
        def _(s):
            for f in kb.prog["sync"]:
                f(s)

    es.close()
    return nc


_NC_CACHE = {}


def _get_nc():
    key = ("main", DEBUG_TAPS)
    if key not in _NC_CACHE:
        _NC_CACHE[key] = build()
    return _NC_CACHE[key]


def _host_packs(inp):
    bp = np.zeros((128, 64), np.float32)
    fm = lambda b: np.ascontiguousarray(np.asarray(b).reshape(-1, 128).T)
    bp[:, 0:8] = fm(inp["L0_attn_wq_b"])
    bp[:, 8:16] = fm(inp["L0_attn_wg_b"])
    bp[:, 16:24] = fm(inp["L0_attn_out_b"])
    bp[:, 24:32] = fm(inp["L0_ffn_cn_b1"])
    bp[:, 32:36] = fm(inp["L0_ffn_cn_b2"])
    bp[:, 36:38] = fm(inp["L0_ffn_cn_b3"])
    bp[:, 38:42] = fm(inp["L0_ffn_gc_b1"])
    bp[:, 42:44] = fm(inp["L0_ffn_gc_b2"])
    bp[0:4, 44] = np.asarray(inp["L0_ffn_cn_b4"])[0]
    bp[0:4, 45] = np.asarray(inp["L0_ffn_gc_b3"])[0]
    bp[0:4, 46] = np.asarray(inp["L0_ffn_adaptive_range"])[0]
    bp[:, 47] = 1e-5
    rp = np.zeros((64, 1024), np.float32)
    rp[0, :] = np.asarray(inp["L0_attn_wk_b"])
    rp[32, :] = np.asarray(inp["L0_attn_wv_b"])
    cp = np.zeros((128, 1824), np.float32)
    for tq in range(4):
        cp[:, tq * 4 + tq] = 0.0
    for tq in range(4):
        for j in range(4):
            if j == tq:
                cp[:, tq * 4 + j] = 1.0
    for c in range(8):
        cp[2 * c, 16 + c * 128: 16 + c * 128 + 64] = 1.0
        cp[2 * c + 1, 16 + c * 128 + 64: 16 + (c + 1) * 128] = 1.0
    for tq in range(4):
        cp[tq, 1040 + tq * 128: 1040 + (tq + 1) * 128] = 1.0
    cp[0:2, 1552:1680] = 1.0
    cp[32, 1552:1680] = 1.0
    cp[:, 1680] = 1.0
    for c in range(8):
        cp[0:64, 1696 + c * 16 + 2 * c] = 1.0
        cp[64:128, 1696 + c * 16 + 2 * c + 1] = 1.0
    gwp = np.zeros((128, 8), np.float32)
    g3 = np.asarray(inp["L0_ffn_gc_w3"]).reshape(2, 128)
    for kc in range(2):
        for j in range(4):
            gwp[:, kc * 4 + j] = g3[kc]
    return bp, rp, cp.astype(ml_dtypes.bfloat16), gwp


def kernel(**inp):
    x = np.asarray(inp["L0_attn_x"])
    bp, rp, cp, gwp = _host_packs(inp)
    base = dict(
        wq=np.asarray(inp["L0_attn_wq_w"]), wk=np.asarray(inp["L0_attn_wk_w"]),
        wv=np.asarray(inp["L0_attn_wv_w"]), wg=np.asarray(inp["L0_attn_wg_w"]),
        wo=np.asarray(inp["L0_attn_out_w"]), cw1=np.asarray(inp["L0_ffn_cn_w1"]),
        cw2=np.asarray(inp["L0_ffn_cn_w2"]), cw3=np.asarray(inp["L0_ffn_cn_w3"]),
        cw4=np.asarray(inp["L0_ffn_cn_w4"]), gw1=np.asarray(inp["L0_ffn_gc_w1"]),
        gw2=np.asarray(inp["L0_ffn_gc_w2"]), gwpack=gwp,
        biaspack=bp, rowpack=rp, cpack=cp,
    )
    in_maps = []
    for core in range(8):
        b, s = core // 4, core % 4
        m = dict(base)
        m["x"] = np.ascontiguousarray(x[b, s * T:(s + 1) * T, :].T)
        in_maps.append(m)
    nc = _get_nc()
    res = run_bass_kernel_spmd(nc, in_maps, core_ids=list(range(8)))
    out = np.empty((2, 8192, D), np.float32)
    for core in range(8):
        b, s = core // 4, core % 4
        yc = np.asarray(res.results[core]["y"])
        out[b, s * T:(s + 1) * T, :] = yc.reshape(D, T).T
    return out
